# version 09 (final)
# speedup vs baseline: 1.2494x; 1.0514x over previous
import numpy as np
import concourse.bass as bass
import concourse.mybir as mybir

F32 = mybir.dt.float32
BF16 = mybir.dt.bfloat16
AF = mybir.ActivationFunctionType
ALU = mybir.AluOpType
AX = mybir.AxisListType

D = 1024
S = 16384
TPC = 4096
NEG_BIG = -1.0e30
MAGIC = float(1.5 * 2.0 ** 23)
_TP = 2.0 * np.pi
TWO_PI_HI = float(np.float32(_TP))
TWO_PI_LO = float(_TP - np.float64(np.float32(_TP)))
INV_2PI = float(np.float32(1.0 / _TP))
HALF_PI = float(np.float32(np.pi / 2.0))
LN_EPS = 1e-5

ENGS = ("sync", "vector", "scalar", "tensor", "gpsimd")


def fm(dram_ap):
    return dram_ap.rearrange("(c p) t -> p c t", p=128)


class Ctx:

    SERIALIZE = False
    SER_REGION = ""

    def __init__(self, nc):
        self.region = ""
        self.nc = nc
        self.q = {k: [] for k in ENGS}
        self.counts = {k: 0 for k in ENGS}
        self.waited = {a: {b: 0 for b in ENGS} for a in ENGS}
        self._cms = []
        self._sem_waited = set()
        self._ccn = 0
        self._dmn = 0
        self.chans = {}
        self.chan_waited = {}
        self.sems = {}
        for e in ENGS:
            cm = nc.semaphore("done_" + e)
            self.sems[e] = cm.__enter__()
            self._cms.append(cm)

    def _chan(self, name):
        if name not in self.chans:
            cm = self.nc.semaphore("ch_" + name)
            h = cm.__enter__()
            self._cms.append(cm)
            self.chans[name] = [h, 0]
        return self.chans[name]

    def _dep(self, eng, deps):
        if self.SERIALIZE or (self.SER_REGION and self.region == self.SER_REGION):
            deps = list(deps) + [(e2, self.counts[e2]) for e2 in ENGS if e2 != eng]
            deps += [("#chan", nm, hc[1]) for nm, hc in self.chans.items()]
        for tok in deps:
            if tok is None:
                continue
            if tok[0] == "#sem":
                key = (eng, id(tok[1]))
                if key not in self._sem_waited:
                    self.q[eng].append(lambda e, h=tok[1]: e.wait_ge(h, 1))
                    self._sem_waited.add(key)
                continue
            if tok[0] == "#chan":
                _, nm, c = tok
                key = (eng, nm)
                if self.chan_waited.get(key, 0) < c:
                    h = self.chans[nm][0]
                    self.q[eng].append(lambda e, h=h, c=c: e.wait_ge(h, c))
                    self.chan_waited[key] = c
                continue
            e2, c = tok
            if self.waited[eng][e2] < c:
                h = self.sems[e2]
                self.q[eng].append(lambda e, h=h, c=c: e.wait_ge(h, c))
                self.waited[eng][e2] = c

    def step(self, eng, fns, deps=()):
        if callable(fns):
            fns = [fns]
        h = self.sems[eng]
        tok = None
        for i, f in enumerate(fns):
            self._dep(eng, (deps if i == 0 else [tok]))
            self.counts[eng] += 1
            c = self.counts[eng]
            self.q[eng].append(lambda e, f=f, h=h: f(e).then_inc(h, 1))
            tok = (eng, c)
        return tok

    def dma(self, eng, fn, deps=(), chan=None):
        self._dep(eng, deps)
        if chan is None:
            chan = f"_dma{self._dmn}"
            self._dmn += 1
        hc = self._chan(chan)
        hc[1] += 16
        c = hc[1]
        self.q[eng].append(lambda e, f=fn, h=hc[0]: f(e).then_inc(h, 16))
        return ("#chan", chan, c)

    def cc(self, fn, deps=()):
        self._dep("gpsimd", deps)
        cm = self.nc.semaphore(f"ccsem{self._ccn}")
        self._ccn += 1
        h = cm.__enter__()
        self._cms.append(cm)
        self.q["gpsimd"].append(lambda e, f=fn, h=h: f(e).then_inc(h))
        return ("#sem", h)

    def wait_all(self, eng, deps):
        self._dep(eng, deps)

    def barrier(self, dummy):
        deps = [(e2, self.counts[e2]) for e2 in ENGS if e2 != "gpsimd"]
        deps += [("#chan", nm, hc[1]) for nm, hc in self.chans.items()]
        self._dep("gpsimd", deps)
        mark = self.step("gpsimd", lambda e, t=dummy: e.memset(t[0:1, :], 0.0))
        for eng in ENGS:
            if eng != "gpsimd":
                self._dep(eng, [mark])
        return mark

    def tok(self, eng):
        return (eng, self.counts[eng])

    def finalize(self):
        nc = self.nc
        with nc.Block() as block:
            @block.sync
            def _(e):
                for f in self.q["sync"]:
                    f(e)

            @block.vector
            def _(e):
                for f in self.q["vector"]:
                    f(e)

            @block.scalar
            def _(e):
                for f in self.q["scalar"]:
                    f(e)

            @block.tensor
            def _(e):
                for f in self.q["tensor"]:
                    f(e)

            @block.gpsimd
            def _(e):
                for f in self.q["gpsimd"]:
                    f(e)
        for cm in reversed(self._cms):
            cm.__exit__(None, None, None)


def host_prep(inputs):
    import ml_dtypes
    P = {}
    Wqkv = np.asarray(inputs["L0_attn_W_qkv"])
    P["WqT"] = np.ascontiguousarray(Wqkv[0::3, :].T)
    P["WkT"] = np.ascontiguousarray(Wqkv[1::3, :].T)
    P["WvT"] = np.ascontiguousarray(Wqkv[2::3, :].T)
    P["WoT"] = np.ascontiguousarray(np.asarray(inputs["L0_attn_W_out"]).T)
    P["Wg0T"] = np.ascontiguousarray(np.asarray(inputs["L0_ffn_Wg"]).T)
    P["W1T0"] = np.ascontiguousarray(np.asarray(inputs["L0_ffn_W1"]).transpose(0, 2, 1))
    P["W3T0"] = np.ascontiguousarray(np.asarray(inputs["L0_ffn_W3"]).transpose(0, 2, 1))
    P["W2T0"] = np.ascontiguousarray(np.asarray(inputs["L0_ffn_W2"]).transpose(0, 2, 1))
    P["qk_w"] = np.ascontiguousarray(np.asarray(inputs["L1_attn_qk_w"]))
    P["qk_b"] = np.ascontiguousarray(np.asarray(inputs["L1_attn_qk_b"]).reshape(1, 2048))
    P["lepe_w"] = np.ascontiguousarray(np.asarray(inputs["L1_attn_lepe_w"]))
    P["lepe_b"] = np.ascontiguousarray(np.asarray(inputs["L1_attn_lepe_b"]).reshape(8, 128).T)
    P["Wg1"] = np.ascontiguousarray(np.asarray(inputs["L1_ffn_Wg"]))
    P["bg1"] = np.ascontiguousarray(np.asarray(inputs["L1_ffn_bg"]).reshape(1, 8))
    P["W1_1"] = np.ascontiguousarray(np.asarray(inputs["L1_ffn_W1"]))
    P["b1_1"] = np.ascontiguousarray(np.asarray(inputs["L1_ffn_b1"]).reshape(8, 32, 128).transpose(0, 2, 1))
    P["W2_1"] = np.ascontiguousarray(np.asarray(inputs["L1_ffn_W2"]))
    P["b2_1"] = np.ascontiguousarray(np.asarray(inputs["L1_ffn_b2"]))
    P["identb"] = np.eye(128, dtype=ml_dtypes.bfloat16)
    P["identf"] = np.eye(128, dtype=np.float32)
    P["ones_row"] = np.ones((1, 128), np.float32)
    P["ones_colb"] = np.ones((128, 1), ml_dtypes.bfloat16)
    ln = np.zeros((128, 2), np.float32)
    ln[:, 0] = -1.0 / 1024.0
    ln[:, 1] = 1.0 / 1024.0
    P["ln_cols"] = ln
    P["actconsts"] = np.full((128, 1), LN_EPS, np.float32)
    P["theta"] = (1.0 / 10000.0 ** (np.arange(512, dtype=np.float64) / 512.0)).astype(np.float32).reshape(1, 512)
    x = np.asarray(inputs["L0_attn_x"])
    P["x0_fm"] = [np.ascontiguousarray(x[b].T) for b in range(2)]
    P["pos_rows"] = [np.arange(qq * TPC, (qq + 1) * TPC, dtype=np.float32).reshape(32, 128) for qq in range(4)]
    return P


def sbuf(nc, stack, name, shape, dt):
    return stack.enter_context(nc.sbuf_tensor("sb_" + name, shape, dt))


def psum(nc, stack, name, shape, dt=F32):
    return stack.enter_context(nc.psum_tensor("ps_" + name, shape, dt))


def alloc_A_shared(nc, stack):
    s = {}
    s["identb"] = sbuf(nc, stack, "identb", [128, 128], BF16)
    s["sc"] = psum(nc, stack, "sc", [128, 1024])
    s["ptp"] = psum(nc, stack, "ptp", [128, 1024], BF16)
    s["pAV"] = psum(nc, stack, "pAV", [128, 1024])
    s["pR1"] = psum(nc, stack, "pR1", [1, 512])
    s["pR2"] = psum(nc, stack, "pR2", [1, 512])
    return s


def alloc_A1(nc, stack):
    s = {}
    for n in ("wq", "wk", "wv"):
        s[n] = sbuf(nc, stack, n, [128, 8, 1024], BF16)
    s["xb"] = [sbuf(nc, stack, f"xb{i}", [128, 8, 512], BF16) for i in range(2)]
    s["ktev"] = [sbuf(nc, stack, f"ktev{i}", [128, 8, 512], BF16) for i in range(2)]
    s["vev"] = [sbuf(nc, stack, f"vev{i}", [128, 4, 1024], BF16) for i in range(2)]
    s["qev"] = [sbuf(nc, stack, f"qev{i}", [128, 8, 512], BF16) for i in range(2)]
    return s


def alloc_A2(nc, stack):
    s = {}
    s["wo"] = sbuf(nc, stack, "wo", [128, 8, 1024], BF16)
    s["qt"] = sbuf(nc, stack, "qt", [128, 8, 1024], BF16)
    s["O"] = sbuf(nc, stack, "O", [128, 8, 1024], F32)
    s["OT"] = sbuf(nc, stack, "OT", [128, 8, 1024], BF16)
    s["ktile"] = [sbuf(nc, stack, f"ktile{i}", [128, 8, 512], BF16) for i in range(2)]
    s["vtile"] = [sbuf(nc, stack, f"vtile{i}", [128, 4, 1024], BF16) for i in range(2)]
    s["P"] = [sbuf(nc, stack, f"P{i}", [128, 512], BF16) for i in range(2)]
    s["PT"] = sbuf(nc, stack, "PT", [128, 512], BF16)
    s["stats"] = sbuf(nc, stack, "stats", [128, 32], F32)
    s["x0c"] = sbuf(nc, stack, "x0c", [128, 8, 512], F32)
    s["r1"] = sbuf(nc, stack, "r1", [128, 8, 1024], F32)
    s["sq"] = sbuf(nc, stack, "sq", [128, 512], F32)
    s["rowA"] = sbuf(nc, stack, "rowA", [1, 512], F32)
    s["rowB"] = sbuf(nc, stack, "rowB", [1, 512], F32)
    s["rowC"] = sbuf(nc, stack, "rowC", [1, 512], F32)
    s["bcm"] = sbuf(nc, stack, "bcm", [128, 512], F32)
    s["bcr"] = sbuf(nc, stack, "bcr", [128, 512], F32)
    return s


def emit_A1(p, d, sh, s):
    wtok = {}
    for n, src in (("wq", d["WqT"]), ("wk", d["WkT"]), ("wv", d["WvT"])):
        wtok[n] = p.dma("gpsimd", lambda e, t=s[n], v=fm(src): e.dma_start(out=t[:, :, :], in_=v), chan=n)
    NT = S // 512
    xload = [None, None]
    xlast = [None, None]
    stored = {"ktev": [None, None], "vev": [None, None], "qev": [None, None]}
    pslot = [(sh["pAV"], slice(0, 512)), (sh["pAV"], slice(512, 1024))]
    ps_evtok = [None, None]
    gidx = 0
    end_toks = []

    for it in range(NT + TPC // 512):
        buf = it % 2
        kv_part = it < NT
        t0 = (it * 512) if kv_part else ((it - NT) * 512)
        srcname = "x0_fm" if kv_part else "x0q_fm"
        xload[buf] = p.dma("gpsimd",
                           lambda e, t=s["xb"][buf], v=fm(d[srcname])[:, :, t0:t0 + 512]:
                           e.dma_start(out=t[:, :, :], in_=v),
                           deps=[xlast[buf]], chan=f"xb{buf}")

        def proj(wname, ev_name):
            nonlocal gidx
            evtoks = []
            mmtok = None
            for oc in range(8):
                sl = gidx % 2
                ps, psl = pslot[sl]
                for c in range(8):
                    mmtok = p.step("tensor",
                                   lambda e, ps=ps, psl=psl, w=s[wname], c=c, oc=oc, x=s["xb"][buf]:
                                   e.matmul(ps[:, psl], w[:, c, oc * 128:(oc + 1) * 128], x[:, c, :],
                                            start=(c == 0), stop=(c == 7)),
                                   deps=([xload[buf], wtok[wname], ps_evtok[sl]] if c == 0 else []))
                evt = p.step("scalar",
                             lambda e, t=s[ev_name][buf], oc=oc, ps=ps, psl=psl: e.copy(t[:, oc, :], ps[:, psl]),
                             deps=[mmtok, stored[ev_name][buf]])
                ps_evtok[sl] = evt
                evtoks.append(evt)
                gidx += 1
            return mmtok, evtoks

        if kv_part:
            mmk, evk = proj("wk", "ktev")
            stored["ktev"][buf] = p.dma("sync",
                                        lambda e, t=s["ktev"][buf], v=fm(d["KT_d"])[:, :, t0:t0 + 512]:
                                        e.dma_start(out=v, in_=t[:, :, :]),
                                        deps=evk, chan=f"ktev{buf}")
            evv = []
            mmv = None
            for tc in range(4):
                for dh in range(2):
                    sl = gidx % 2
                    ps, psl = pslot[sl]
                    for c in range(8):
                        mmv = p.step("tensor",
                                     lambda e, ps=ps, psl=psl, x=s["xb"][buf], c=c, tc=tc, w=s["wv"], dh=dh:
                                     e.matmul(ps[:, psl], x[:, c, tc * 128:(tc + 1) * 128],
                                              w[:, c, dh * 512:(dh + 1) * 512],
                                              start=(c == 0), stop=(c == 7)),
                                     deps=([xload[buf], wtok["wv"], ps_evtok[sl]] if c == 0 else []))
                    evt = p.step("vector",
                                 lambda e, t=s["vev"][buf], tc=tc, dh=dh, ps=ps, psl=psl:
                                 e.tensor_copy(t[:, tc, dh * 512:(dh + 1) * 512], ps[:, psl]),
                                 deps=[mmv, stored["vev"][buf]])
                    ps_evtok[sl] = evt
                    evv.append(evt)
                    gidx += 1
            stored["vev"][buf] = p.dma("sync",
                                       lambda e, t=s["vev"][buf],
                                       v=d["V_d"][t0:t0 + 512, :].rearrange("(a p) dd -> p a dd", p=128):
                                       e.dma_start(out=v, in_=t[:, :, :]),
                                       deps=evv, chan=f"vev{buf}")
            xlast[buf] = mmv
        else:
            mmq, evq = proj("wq", "qev")
            stored["qev"][buf] = p.dma("sync",
                                       lambda e, t=s["qev"][buf],
                                       v=fm(d["QT_d"])[:, :, t0:t0 + 512]:
                                       e.dma_start(out=v, in_=t[:, :, :]),
                                       deps=evq, chan=f"qev{buf}")
            xlast[buf] = mmq
    end_toks = [xlast[0], xlast[1], ps_evtok[0], ps_evtok[1],
                stored["ktev"][0], stored["ktev"][1], stored["vev"][0], stored["vev"][1],
                stored["qev"][0], stored["qev"][1], xload[0], xload[1]]
    return [t for t in end_toks if t is not None]


def emit_A2(p, d, sh, s, a1_end, consts):
    for eng in ENGS:
        p.wait_all(eng, a1_end)
    wo_tok = p.dma("gpsimd", lambda e, t=s["wo"], v=fm(d["WoT"]): e.dma_start(out=t[:, :, :], in_=v))
    consts_tok = None

    QB = 1024
    NQB = TPC // QB
    KTS = 512
    NKT = S // KTS
    qt, O, OT = s["qt"], s["O"], s["OT"]
    Psb, PTsb, stats = s["P"], s["PT"], s["stats"]
    sc, ptp, pAV = sh["sc"], sh["ptp"], sh["pAV"]
    pR1, pR2 = sh["pR1"], sh["pR2"]
    r1, x0c, sq = s["r1"], s["x0c"], s["sq"]

    ktlast = [None, None]
    ktlast_v = [None, None]
    qt_lastuse = None
    x0c_lastuse = None
    r1_stored = None
    x1b_stored = None
    pAB_free = []
    sc_free = [None, None]
    albl = [None, None]
    av_last = None
    ptp_free = None
    pR1_free = None
    pR2_free = None
    Psb_free = [None, None]
    o_upd = None
    mm = None

    for qb in range(NQB):
        qb0 = qb * QB
        ldq = p.dma("sync", lambda e, t=qt, v=fm(d["QT_d"])[:, :, qb0:qb0 + QB]:
                    e.dma_start(out=t[:, :, :], in_=v),
                    deps=[qt_lastuse], chan="qt")
        ini = p.step("vector", [
            lambda e, t=stats: e.memset(t[:, 0:8], -NEG_BIG),
            lambda e, t=stats: e.memset(t[:, 8:16], 0.0),
            lambda e, t=O: e.memset(t[:, :, :], 0.0),
        ], deps=sc_free + albl)

        vtlast = ktlast_v
        p1 = None
        p2 = None
        ldk = ldv = None
        for it in range(NKT * 8):
            kt, qtc = divmod(it, 8)
            kb = kt % 2
            if qtc == 0:
                k0 = kt * KTS
                ldk = p.dma("sync", lambda e, t=s["ktile"][kb], v=fm(d["KT_d"])[:, :, k0:k0 + KTS]:
                            e.dma_start(out=t[:, :, :], in_=v),
                            deps=[ktlast[kb]], chan=f"ktile{kb}")
                ldv = p.dma("sync", lambda e, t=s["vtile"][kb],
                            v=d["V_d"][k0:k0 + KTS, :].rearrange("(a p) dd -> p a dd", p=128):
                            e.dma_start(out=t[:, :, :], in_=v),
                            deps=[vtlast[kb]], chan=f"vtile{kb}")
            half = qtc % 2
            scsl = slice(half * 512, (half + 1) * 512)
            mm = None
            for c in range(8):
                dep0 = ([ldk, ldq, ini, sc_free[half]] + consts) if c == 0 else []
                mm = p.step("tensor", lambda e, w=qt, c=c, q=qtc, k=s["ktile"][kb], ps=sc, scsl=scsl:
                            e.matmul(ps[:, scsl], w[:, c, q * 128:(q + 1) * 128], k[:, c, :],
                                     start=(c == 0), stop=(c == 7)), deps=dep0)
            if qtc == 7:
                ktlast[kb] = mm
            h4 = half * 4
            st1 = p.step("vector", [
                lambda e, t=stats, ps=sc, scsl=scsl, h4=h4: e.reduce_max(t[:, 16 + h4:17 + h4], ps[:, scsl], axis=AX.X),
                lambda e, t=stats, h4=h4: e.tensor_scalar(t[:, 17 + h4:18 + h4], t[:, 16 + h4:17 + h4], -1.0 / 32.0, None, ALU.mult),
                lambda e, t=stats, q=qtc, h4=h4: e.tensor_tensor(t[:, 18 + h4:19 + h4], t[:, 17 + h4:18 + h4], t[:, q:q + 1], ALU.min),
                lambda e, t=stats, q=qtc, h4=h4: e.tensor_sub(t[:, 19 + h4:20 + h4], t[:, 18 + h4:19 + h4], t[:, q:q + 1]),
                lambda e, t=stats, q=qtc, h4=h4: e.tensor_copy(t[:, q:q + 1], t[:, 18 + h4:19 + h4]),
            ], deps=[mm, albl[half]])
            q4 = qtc % 4
            al = p.step("scalar", lambda e, t=stats, q4=q4, h4=h4:
                        e.activation(t[:, 24 + q4:25 + q4], t[:, 19 + h4:20 + h4], AF.Exp),
                        deps=[st1])
            p_tok = p.step("scalar", lambda e, t=Psb[half], ps=sc, st=stats, q=qtc, scsl=scsl, q4=q4:
                           e.activation(t[:, :], ps[:, scsl], AF.Exp, bias=st[:, q:q + 1], scale=1.0 / 32.0,
                                        accum_out=st[:, 28 + q4:29 + q4]),
                           deps=[Psb_free[half], st1])
            sc_free[half] = p_tok
            lup = p.step("vector", lambda e, t=stats, q=qtc, q4=q4:
                         e.scalar_tensor_tensor(t[:, 8 + q:9 + q], t[:, 8 + q:9 + q],
                                                t[:, 24 + q4:25 + q4],
                                                t[:, 28 + q4:29 + q4], ALU.mult, ALU.add),
                         deps=[al, p_tok])
            albl[half] = lup
            cx = {"q": qtc, "q4": q4, "half": half, "mm": mm, "p_tok": p_tok, "lup": lup,
                  "kb": kb, "ldv": ldv, "last_of_kt": (qtc == 7)}
            if p2 is not None:
                av = None
                for kc in range(4):
                    dep0 = ([p2["pte"], p2["ldv"]] + pAB_free) if kc == 0 else []
                    p.step("tensor", lambda e, kc=kc, t=PTsb, v=s["vtile"][p2["kb"]], ps=pAV:
                           e.matmul(ps[:, 0:512], t[:, kc * 128:(kc + 1) * 128], v[:, kc, 0:512],
                                    start=(kc == 0), stop=(kc == 3)), deps=dep0)
                    av = p.step("tensor", lambda e, kc=kc, t=PTsb, v=s["vtile"][p2["kb"]], ps=pAV:
                                e.matmul(ps[:, 512:1024], t[:, kc * 128:(kc + 1) * 128], v[:, kc, 512:1024],
                                         start=(kc == 0), stop=(kc == 3)), deps=[])
                p2["av"] = av
                av_last = av
                if p2["last_of_kt"]:
                    vtlast[p2["kb"]] = av
            if p1 is not None:
                tt_tok = None
                for i in range(4):
                    tt_tok = p.step("tensor", lambda e, i=i, t=Psb[p1["half"]], ps=ptp, idn=sh["identb"]:
                                    e.transpose(ps[:, i * 128:(i + 1) * 128], t[:, i * 128:(i + 1) * 128],
                                                idn[:, :]),
                                    deps=([p1["p_tok"], ptp_free] if i == 0 else []))
                Psb_free[p1["half"]] = tt_tok
                p1["tt"] = tt_tok
                pte = p.step("vector", lambda e, t=PTsb, ps=ptp: e.tensor_copy(t[:, :], ps[:, 0:512]),
                             deps=[tt_tok, (p2["av"] if p2 is not None else None)])
                ptp_free = pte
                p1["pte"] = pte
            if p2 is not None:
                o_upd = p.step("vector", lambda e, t=O, q=p2["q"], st=stats, ps=pAV, q4=p2["q4"]:
                               e.scalar_tensor_tensor(t[:, q, :], t[:, q, :], st[:, 24 + q4:25 + q4],
                                                      ps[:, :], ALU.mult, ALU.add),
                               deps=[p2["av"], p2["lup"]])
                pAB_free = [o_upd]
            p2 = p1
            p1 = cx
        for drain in range(2):
            if p2 is not None:
                av = None
                for kc in range(4):
                    dep0 = ([p2["pte"], p2["ldv"]] + pAB_free) if kc == 0 else []
                    p.step("tensor", lambda e, kc=kc, t=PTsb, v=s["vtile"][p2["kb"]], ps=pAV:
                           e.matmul(ps[:, 0:512], t[:, kc * 128:(kc + 1) * 128], v[:, kc, 0:512],
                                    start=(kc == 0), stop=(kc == 3)), deps=dep0)
                    av = p.step("tensor", lambda e, kc=kc, t=PTsb, v=s["vtile"][p2["kb"]], ps=pAV:
                                e.matmul(ps[:, 512:1024], t[:, kc * 128:(kc + 1) * 128], v[:, kc, 512:1024],
                                         start=(kc == 0), stop=(kc == 3)), deps=[])
                p2["av"] = av
                av_last = av
                if p2["last_of_kt"]:
                    vtlast[p2["kb"]] = av
            if p1 is not None and "tt" not in p1:
                tt_tok = None
                for i in range(4):
                    tt_tok = p.step("tensor", lambda e, i=i, t=Psb[p1["half"]], ps=ptp, idn=sh["identb"]:
                                    e.transpose(ps[:, i * 128:(i + 1) * 128], t[:, i * 128:(i + 1) * 128],
                                                idn[:, :]),
                                    deps=([p1["p_tok"], ptp_free] if i == 0 else []))
                Psb_free[p1["half"]] = tt_tok
                p1["tt"] = tt_tok
                pte = p.step("vector", lambda e, t=PTsb, ps=ptp: e.tensor_copy(t[:, :], ps[:, 0:512]),
                             deps=[tt_tok, (p2["av"] if p2 is not None else None)])
                ptp_free = pte
                p1["pte"] = pte
            if p2 is not None:
                o_upd = p.step("vector", lambda e, t=O, q=p2["q"], st=stats, ps=pAV, q4=p2["q4"]:
                               e.scalar_tensor_tensor(t[:, q, :], t[:, q, :], st[:, 24 + q4:25 + q4],
                                                      ps[:, :], ALU.mult, ALU.add),
                               deps=[p2["av"], p2["lup"]])
                pAB_free = [o_upd]
            p2 = p1
            p1 = None
        qt_lastuse = mm

        ot_evac = None
        for qtc in range(8):
            for half in range(2):
                ob = p.step("vector", ([
                    lambda e, t=stats, q=qtc: e.reciprocal(t[:, 16:17], t[:, 8 + q:9 + q]),
                ] if half == 0 else []) + [
                    lambda e, t=O, st=stats, q=qtc, h=half:
                    e.tensor_scalar(t[:, q, h * 512:(h + 1) * 512], t[:, q, h * 512:(h + 1) * 512],
                                    st[:, 16:17], None, ALU.mult),
                    lambda e, t=Psb[0], o=O, q=qtc, h=half: e.tensor_copy(t[:, :], o[:, q, h * 512:(h + 1) * 512]),
                ], deps=[o_upd] + Psb_free)
                tt_tok = None
                for c in range(4):
                    tt_tok = p.step("tensor", lambda e, c=c, t=Psb[0], ps=ptp, idn=sh["identb"]:
                                    e.transpose(ps[:, c * 128:(c + 1) * 128], t[:, c * 128:(c + 1) * 128],
                                                idn[:, :]),
                                    deps=([ob, ptp_free] if c == 0 else []))
                Psb_free = [tt_tok, tt_tok]
                ot_evac = p.step("vector", [
                    (lambda e, c=c, q=qtc, h=half, t=OT, ps=ptp:
                     e.tensor_copy(t[:, h * 4 + c, q * 128:(q + 1) * 128], ps[:, c * 128:(c + 1) * 128]))
                    for c in range(4)
                ], deps=[tt_tok])
                ptp_free = ot_evac

        if qb == 0 and "dbgO" in d:
            p.dma("sync", lambda e, t=O: e.dma_start(out=d["dbgO"], in_=t[:, :, :].rearrange("p c t -> p (c t)")),
                  deps=[ot_evac, o_upd])
            p.dma("sync", lambda e, t=OT: e.dma_start(out=d["dbgOT"], in_=t[:, :, :].rearrange("p c t -> p (c t)")),
                  deps=[ot_evac])
            p.dma("sync", lambda e, t=stats: e.dma_start(out=d["dbgS"], in_=t[:, :]), deps=[ot_evac])
        for tt in range(2):
            ldx = p.dma("sync", lambda e, t=x0c, v=fm(d["x0q_fm"])[:, :, qb0 + tt * 512:qb0 + tt * 512 + 512]:
                        e.dma_start(out=t[:, :, :], in_=v), deps=[x0c_lastuse], chan="x0c")
            radd_toks = []
            for oc in range(8):
                psl2 = slice(0, 512) if oc % 2 == 0 else slice(512, 1024)
                for c in range(8):
                    dep0 = ([ot_evac, wo_tok] + pAB_free) if c == 0 else []
                    mm = p.step("tensor", lambda e, ps=pAV, psl2=psl2, w=s["wo"], c=c, oc=oc, t=OT, tt=tt:
                                e.matmul(ps[:, psl2], w[:, c, oc * 128:(oc + 1) * 128],
                                         t[:, c, tt * 512:(tt + 1) * 512],
                                         start=(c == 0), stop=(c == 7)), deps=dep0)
                ra = p.step("vector", lambda e, t=r1, oc=oc, tt=tt, x=x0c, ps=pAV, psl2=psl2:
                            e.tensor_add(t[:, oc, tt * 512:(tt + 1) * 512], ps[:, psl2], x[:, oc, :]),
                            deps=[mm, ldx, r1_stored, x1b_stored])
                pAB_free = ([ra] if oc == 0 else pAB_free + [ra])
                radd_toks.append(ra)
            ot_lastread = mm
            mm1 = None
            for oc in range(8):
                mm1 = p.step("tensor", lambda e, oc=oc, t=r1, tt=tt, lc=sh["ln_cols"], ps=pR1:
                             e.matmul(ps[:, :], lc[:, 0:1], t[:, oc, tt * 512:(tt + 1) * 512],
                                      start=(oc == 0), stop=(oc == 7)),
                             deps=((radd_toks + [pR1_free]) if oc == 0 else []))
            mm2 = None
            for oc in range(8):
                sq_tok = p.step("vector", lambda e, t=sq, r=r1, oc=oc, tt=tt:
                                e.tensor_mul(t[:, :], r[:, oc, tt * 512:(tt + 1) * 512],
                                             r[:, oc, tt * 512:(tt + 1) * 512]),
                                deps=[radd_toks[oc], mm2])
                mm2 = p.step("tensor", lambda e, oc=oc, t=sq, ps=pR2, lc=sh["ln_cols"]:
                             e.matmul(ps[:, :], lc[:, 1:2], t[:, :],
                                      start=(oc == 0), stop=(oc == 7)),
                             deps=([sq_tok, pR2_free] if oc == 0 else [sq_tok]))
            rows = p.step("vector", [
                lambda e, t=s["rowA"], ps=pR1: e.tensor_copy(t[0:1, :], ps[0:1, :]),
                lambda e, a=s["rowA"], b=s["rowB"]:
                e.scalar_tensor_tensor(b[0:1, :], a[0:1, :], -1.0, a[0:1, :], ALU.mult, ALU.mult),
                lambda e, b=s["rowB"], ps=pR2: e.tensor_add(b[0:1, :], b[0:1, :], ps[0:1, :]),
            ], deps=[mm1, mm2, pR1_free, pR2_free])
            rstd = p.step("scalar", [
                lambda e, b=s["rowB"], c_=s["rowC"]: e.activation(c_[0:1, :], b[0:1, :], AF.Ln, bias=LN_EPS),
                lambda e, c_=s["rowC"]: e.activation(c_[0:1, :], c_[0:1, :], AF.Exp, scale=-0.5),
            ], deps=[rows])
            bc = p.step("tensor", [
                lambda e, ps=sc, o=sh["ones_row"], a=s["rowA"]:
                e.matmul(ps[:, 0:512], o[0:1, :], a[0:1, :], start=True, stop=True),
                lambda e, ps=sc, o=sh["ones_row"], c_=s["rowC"]:
                e.matmul(ps[:, 512:1024], o[0:1, :], c_[0:1, :], start=True, stop=True),
            ], deps=[rstd, rows] + sc_free)
            pR1_free = bc
            pR2_free = bc
            bce = p.step("vector", [
                lambda e, t=s["bcm"], ps=sc: e.tensor_copy(t[:, :], ps[:, 0:512]),
                lambda e, t=s["bcr"], ps=sc: e.tensor_copy(t[:, :], ps[:, 512:1024]),
            ], deps=[bc])
            sc_free = [bce, bce]
            norm_toks = []
            for oc in range(8):
                nt = p.step("vector", [
                    lambda e, t=r1, oc=oc, tt=tt, m=s["bcm"]:
                    e.tensor_add(t[:, oc, tt * 512:(tt + 1) * 512], t[:, oc, tt * 512:(tt + 1) * 512], m[:, :]),
                    lambda e, t=r1, oc=oc, tt=tt, m=s["bcr"]:
                    e.tensor_mul(t[:, oc, tt * 512:(tt + 1) * 512], t[:, oc, tt * 512:(tt + 1) * 512], m[:, :]),
                ], deps=[bce, mm2])
                norm_toks.append(nt)
            x0c_lastuse = radd_toks[-1]
            if tt == 0:
                norm0 = norm_toks
            else:
                norm1 = norm_toks
        r1_stored = p.dma("sync", lambda e, t=r1, v=fm(d["x1_fm"])[:, :, qb0:qb0 + QB]:
                          e.dma_start(out=v, in_=t[:, :, :]), deps=norm0 + norm1, chan="stf")
        x1b_stored = p.dma("gpsimd", lambda e, t=r1, v=fm(d["x1b_fm"])[:, :, qb0:qb0 + QB]:
                           e.dma_start(out=v, in_=t[:, :, :]), deps=norm0 + norm1, chan="stb")
    p.wait_all("gpsimd", [r1_stored])
    p.wait_all("sync", [x1b_stored])
    return [r1_stored, x1b_stored]


def load_consts(p, d, sh):
    t1 = p.dma("sync", lambda e, t=sh["identb"], v=d["identb"]: e.dma_start(out=t[:, :], in_=v))
    t2 = p.dma("sync", lambda e, t=sh["ones_row"], v=d["ones_row"]: e.dma_start(out=t[:, :], in_=v))
    t3 = p.dma("sync", lambda e, t=sh["ln_cols"], v=d["ln_cols"]: e.dma_start(out=t[:, :], in_=v))
    t4 = p.dma("sync", lambda e, t=sh["actc"], v=d["actconsts"]: e.dma_start(out=t[:, :], in_=v))
    return [t1, t2, t3, t4]


def alloc_B(nc, stack):
    s = {}
    s["x1b"] = sbuf(nc, stack, "b_x1b", [128, 8, 1024], BF16)
    s["x1c"] = sbuf(nc, stack, "b_x1c", [128, 8, 1024], F32)
    s["y"] = sbuf(nc, stack, "b_y", [128, 8, 1024], F32)
    s["G"] = sbuf(nc, stack, "b_G", [128, 22, 1024], BF16)
    s["wg0"] = sbuf(nc, stack, "b_wg0", [128, 8, 8], F32)
    s["identf"] = sbuf(nc, stack, "b_identf", [128, 128], F32)
    s["w1g"] = [sbuf(nc, stack, f"b_w1g{i}", [128, 8, 256], BF16) for i in range(2)]
    s["w3g"] = [sbuf(nc, stack, f"b_w3g{i}", [128, 8, 256], BF16) for i in range(2)]
    s["w2g"] = [sbuf(nc, stack, f"b_w2g{i}", [128, 22, 128], BF16) for i in range(2)]
    s["lg"] = sbuf(nc, stack, "b_lg", [128, 48], F32)
    s["cwT"] = sbuf(nc, stack, "b_cwT", [8, 1024], F32)
    s["cwrow"] = sbuf(nc, stack, "b_cwrow", [1, 1024], F32)
    s["cwb"] = sbuf(nc, stack, "b_cwb", [128, 2, 512], F32)
    s["sil"] = sbuf(nc, stack, "b_sil", [128, 512], F32)
    s["tmp"] = sbuf(nc, stack, "b_tmp", [128, 512], F32)
    s["sq"] = sbuf(nc, stack, "b_sq", [128, 512], F32)
    s["rowA"] = sbuf(nc, stack, "b_rowA", [1, 512], F32)
    s["rowB"] = sbuf(nc, stack, "b_rowB", [1, 512], F32)
    s["rowC"] = sbuf(nc, stack, "b_rowC", [1, 512], F32)
    s["bcm"] = sbuf(nc, stack, "b_bcm", [128, 512], F32)
    s["bcr"] = sbuf(nc, stack, "b_bcr", [128, 512], F32)
    s["pg"] = psum(nc, stack, "b_pg", [128, 512])
    s["pH1"] = psum(nc, stack, "b_pH1", [128, 1024])
    s["pH3"] = psum(nc, stack, "b_pH3", [128, 1024])
    s["pcw"] = psum(nc, stack, "b_pcw", [8, 128])
    s["pR1"] = psum(nc, stack, "b_pR1", [1, 512])
    s["pR2"] = psum(nc, stack, "b_pR2", [1, 512])
    s["pBC"] = s["pH1"]
    return s


B_HGROUPS = [(i, min(i + 2, 22)) for i in range(0, 22, 2)]


def emit_LN(p, s, sh, consts, r, dst_f32, dst_b16, tb0, deps_ready, pR1_free, pR2_free, pBC_free, y_stored):
    pR1, pR2, pBC = s["pR1"], s["pR2"], s["pBC"]
    norm_all = []
    for tt in range(2):
        sl = (slice(None), slice(None), slice(tt * 512, (tt + 1) * 512))
        mm1 = None
        for oc in range(8):
            mm1 = p.step("tensor", lambda e, oc=oc, t=r, tt=tt, lc=sh["ln_cols"], ps=pR1:
                         e.matmul(ps[:, :], lc[:, 0:1], t[:, oc, tt * 512:(tt + 1) * 512],
                                  start=(oc == 0), stop=(oc == 7)),
                         deps=(deps_ready + [pR1_free] + consts if oc == 0 else []))
        mm2 = None
        for oc in range(8):
            sq_tok = p.step("vector", lambda e, t=s["sq"], r=r, oc=oc, tt=tt:
                            e.tensor_mul(t[:, :], r[:, oc, tt * 512:(tt + 1) * 512],
                                         r[:, oc, tt * 512:(tt + 1) * 512]),
                            deps=deps_ready + [mm2])
            mm2 = p.step("tensor", lambda e, oc=oc, t=s["sq"], ps=pR2, lc=sh["ln_cols"]:
                         e.matmul(ps[:, :], lc[:, 1:2], t[:, :],
                                  start=(oc == 0), stop=(oc == 7)),
                         deps=([sq_tok, pR2_free] if oc == 0 else [sq_tok]))
        rows = p.step("vector", [
            lambda e, t=s["rowA"], ps=pR1: e.tensor_copy(t[0:1, :], ps[0:1, :]),
            lambda e, a=s["rowA"], b=s["rowB"]:
            e.scalar_tensor_tensor(b[0:1, :], a[0:1, :], -1.0, a[0:1, :], ALU.mult, ALU.mult),
            lambda e, b=s["rowB"], ps=pR2: e.tensor_add(b[0:1, :], b[0:1, :], ps[0:1, :]),
        ], deps=[mm1, mm2, pR1_free, pR2_free])
        rstd = p.step("scalar", [
            lambda e, b=s["rowB"], c_=s["rowC"]: e.activation(c_[0:1, :], b[0:1, :], AF.Ln, bias=LN_EPS),
            lambda e, c_=s["rowC"]: e.activation(c_[0:1, :], c_[0:1, :], AF.Exp, scale=-0.5),
        ], deps=[rows])
        bc = p.step("tensor", [
            lambda e, ps=pBC, o=sh["ones_row"], a=s["rowA"]:
            e.matmul(ps[:, 0:512], o[0:1, :], a[0:1, :], start=True, stop=True),
            lambda e, ps=pBC, o=sh["ones_row"], c_=s["rowC"]:
            e.matmul(ps[:, 512:1024], o[0:1, :], c_[0:1, :], start=True, stop=True),
        ], deps=[rstd, rows, pBC_free])
        pR1_free = bc
        pR2_free = bc
        bce = p.step("vector", [
            lambda e, t=s["bcm"], ps=pBC: e.tensor_copy(t[:, :], ps[:, 0:512]),
            lambda e, t=s["bcr"], ps=pBC: e.tensor_copy(t[:, :], ps[:, 512:1024]),
        ], deps=[bc])
        pBC_free = bce
        for oc in range(8):
            nt1 = p.step("vector", lambda e, t=r, oc=oc, tt=tt, m=s["bcm"]:
                         e.tensor_add(t[:, oc, tt * 512:(tt + 1) * 512],
                                      t[:, oc, tt * 512:(tt + 1) * 512], m[:, :]),
                         deps=[bce, mm2, y_stored])
            nt2 = p.step("vector", lambda e, t=r, oc=oc, tt=tt, m=s["bcr"]:
                         e.tensor_mul(t[:, oc, tt * 512:(tt + 1) * 512],
                                      t[:, oc, tt * 512:(tt + 1) * 512], m[:, :]),
                         deps=[nt1])
            norm_all.append(nt2)
    st1 = p.dma("sync", lambda e, t=r, v=fm(dst_f32)[:, :, tb0:tb0 + 1024]:
                e.dma_start(out=v, in_=t[:, :, :]), deps=norm_all, chan="stf")
    st2 = None
    if dst_b16 is not None:
        st2 = p.dma("gpsimd", lambda e, t=r, v=fm(dst_b16)[:, :, tb0:tb0 + 1024]:
                    e.dma_start(out=v, in_=t[:, :, :]), deps=norm_all, chan="stb")
    return st1, st2, pR1_free, pR2_free, pBC_free, norm_all


def emit_B(p, d, s, sh, consts):
    wg_tok = p.dma("sync", lambda e, t=s["wg0"], v=d["Wg0T"].rearrange("(c p) ee -> p c ee", p=128):
                   e.dma_start(out=t[:, :, :], in_=v))
    idf_tok = p.dma("sync", lambda e, t=s["identf"], v=d["identf"]: e.dma_start(out=t[:, :], in_=v))
    consts = consts + [idf_tok]

    NTB = 4
    TB = 1024
    x1b, x1c, y, G = s["x1b"], s["x1c"], s["y"], s["G"]
    pg, pH1, pH3, pcw = s["pg"], s["pH1"], s["pH3"], s["pcw"]
    lg, cwT = s["lg"], s["cwT"]

    x1b_lastuse = None
    x1c_lastuse = None
    stf = None
    stb = None
    w1_lastuse = [None, None]
    w3_lastuse = [None, None]
    w2_lastuse = [None, None]
    pg_free = None
    pH1_free = [None, None]
    pH3_free = [None, None]
    pcw_free = None
    pR1_free = None
    pR2_free = None
    pBC_free = None
    cwT_lastuse = None
    cwrow_lastuse = None
    G_lastread = None
    sil_free = None
    tmp_free = None
    y_lastread = None
    norm_all = []

    for tb in range(NTB):
        tb0 = tb * TB
        ldx = p.dma("sync", lambda e, t=x1b, v=fm(d["x1b_fm"])[:, :, tb0:tb0 + TB]:
                    e.dma_start(out=t[:, :, :], in_=v), deps=[x1b_lastuse], chan="xmb")
        ldr = p.dma("sync", lambda e, t=x1c, v=fm(d["x1_fm"])[:, :, tb0:tb0 + TB]:
                    e.dma_start(out=t[:, :, :], in_=v), deps=[x1c_lastuse], chan="xmc")
        cwe_toks = []
        for tc in range(8):
            gmm = None
            for c in range(8):
                gmm = p.step("tensor", lambda e, c=c, tc=tc, x=x1c, w=s["wg0"], ps=pg:
                             e.matmul(ps[:, 0:8], x[:, c, tc * 128:(tc + 1) * 128], w[:, c, :],
                                      start=(c == 0), stop=(c == 7)),
                             deps=(([ldr, ldx, wg_tok, pg_free] + consts) if c == 0 else []))
            t1 = p.step("vector", lambda e, t=lg, ps=pg: e.tensor_copy(t[:, 0:8], ps[:, 0:8]), deps=[gmm])
            pg_free = t1
            t2 = p.step("vector", lambda e, t=lg: e.reduce_max(t[:, 8:9], t[:, 0:8], axis=AX.X), deps=[t1])
            t3 = p.step("vector", lambda e, t=lg: e.tensor_scalar(t[:, 9:10], t[:, 8:9], -1.0, None, ALU.mult), deps=[t2])
            t4 = p.step("vector", lambda e, t=lg: e.tensor_scalar(t[:, 16:24], t[:, 0:8], t[:, 8:9], None, ALU.is_ge), deps=[t2])
            t5 = p.step("vector", lambda e, t=lg: e.scalar_tensor_tensor(t[:, 24:32], t[:, 16:24], NEG_BIG, t[:, 0:8], ALU.mult, ALU.add), deps=[t4])
            t6 = p.step("vector", lambda e, t=lg: e.reduce_max(t[:, 10:11], t[:, 24:32], axis=AX.X), deps=[t5])
            t7 = p.step("scalar", lambda e, t=lg: e.activation(t[:, 32:40], t[:, 0:8], AF.Exp, bias=t[:, 9:10]), deps=[t3])
            t8 = p.step("vector", lambda e, t=lg: e.tensor_sub(t[:, 11:12], t[:, 10:11], t[:, 8:9]), deps=[t6])
            t9 = p.step("scalar", lambda e, t=lg: e.activation(t[:, 12:13], t[:, 11:12], AF.Exp), deps=[t8])
            t10 = p.step("vector", lambda e, t=lg: e.tensor_scalar(t[:, 13:14], t[:, 12:13], 1.0, None, ALU.add), deps=[t9])
            t11 = p.step("vector", lambda e, t=lg: e.reciprocal(t[:, 14:15], t[:, 13:14]), deps=[t10])
            t12 = p.step("vector", lambda e, t=lg: e.tensor_scalar(t[:, 16:24], t[:, 0:8], t[:, 10:11], None, ALU.is_ge), deps=[t6, t7])
            t13 = p.step("vector", lambda e, t=lg: e.tensor_mul(t[:, 40:48], t[:, 32:40], t[:, 16:24]), deps=[t12, t7])
            t14 = p.step("vector", lambda e, t=lg: e.tensor_scalar(t[:, 40:48], t[:, 40:48], t[:, 14:15], None, ALU.mult), deps=[t13, t11])
            t15 = p.step("tensor", lambda e, t=lg, ps=pcw, idn=s["identf"]:
                         e.transpose(ps[:, :], t[:, 40:48], idn[:, :]), deps=[t14, pcw_free])
            t16 = p.step("vector", lambda e, t=cwT, tc=tc, ps=pcw:
                         e.tensor_copy(t[0:8, tc * 128:(tc + 1) * 128], ps[0:8, :]),
                         deps=[t15, cwT_lastuse])
            pcw_free = t16
            cwe_toks.append(t16)
        y_first = [True] * 16
        for ex in range(8):
            cwr = p.dma("sync", lambda e, t=s["cwrow"], src=cwT, ex=ex:
                        e.dma_start(out=t[0:1, :], in_=src[ex:ex + 1, :]),
                        deps=cwe_toks + [cwrow_lastuse], chan="cwrow")
            cbm = p.step("tensor", [
                lambda e, ps=pg, o=sh["ones_row"], t=s["cwrow"]:
                e.matmul(ps[:, :], o[0:1, :], t[0:1, 0:512], start=True, stop=True),
            ], deps=[cwr, pg_free])
            cbe1 = p.step("vector", lambda e, t=s["cwb"], ps=pg: e.tensor_copy(t[:, 0, :], ps[:, :]), deps=[cbm])
            cbm2 = p.step("tensor", lambda e, ps=pg, o=sh["ones_row"], t=s["cwrow"]:
                          e.matmul(ps[:, :], o[0:1, :], t[0:1, 512:1024], start=True, stop=True),
                          deps=[cbe1])
            cbe2 = p.step("vector", lambda e, t=s["cwb"], ps=pg: e.tensor_copy(t[:, 1, :], ps[:, :]), deps=[cbm2])
            pg_free = cbe2
            cwrow_lastuse = cbm2
            g_toks = []
            for gi, (h0, h1) in enumerate(B_HGROUPS):
                gb = gi % 2
                nh = h1 - h0
                ld1 = p.dma("gpsimd", lambda e, t=s["w1g"][gb], ex=ex, h0=h0, nh=nh,
                            v=d["W1T0"][ex].rearrange("(c p) hh -> p c hh", p=128)[:, :, h0 * 128:h1 * 128]:
                            e.dma_start(out=t[:, :, 0:nh * 128], in_=v),
                            deps=[w1_lastuse[gb]], chan=f"w1g{gb}")
                ld3 = p.dma("gpsimd", lambda e, t=s["w3g"][gb], ex=ex, h0=h0, nh=nh,
                            v=d["W3T0"][ex].rearrange("(c p) hh -> p c hh", p=128)[:, :, h0 * 128:h1 * 128]:
                            e.dma_start(out=t[:, :, 0:nh * 128], in_=v),
                            deps=[w3_lastuse[gb]], chan=f"w3g{gb}")
                for hi in range(nh):
                    hc = h0 + hi
                    for tt in range(2):
                        par = (hc * 2 + tt) % 2
                        hsl = slice(par * 512, (par + 1) * 512)
                        m1t = None
                        m3t = None
                        for c in range(8):
                            m1t = p.step("tensor", lambda e, c=c, hi=hi, tt=tt, w=s["w1g"][gb], x=x1b, ps=pH1, hsl=hsl:
                                         e.matmul(ps[:, hsl], w[:, c, hi * 128:(hi + 1) * 128],
                                                  x[:, c, tt * 512:(tt + 1) * 512],
                                                  start=(c == 0), stop=(c == 7)),
                                         deps=([ld1, pH1_free[par]] if c == 0 else []))
                            m3t = p.step("tensor", lambda e, c=c, hi=hi, tt=tt, w=s["w3g"][gb], x=x1b, ps=pH3, hsl=hsl:
                                         e.matmul(ps[:, hsl], w[:, c, hi * 128:(hi + 1) * 128],
                                                  x[:, c, tt * 512:(tt + 1) * 512],
                                                  start=(c == 0), stop=(c == 7)),
                                         deps=([ld3, pH3_free[par]] if c == 0 else []))
                        silt = p.step("scalar", lambda e, t=s["sil"], ps=pH1, hsl=hsl:
                                      e.activation(t[:, :], ps[:, hsl], AF.Silu),
                                      deps=[m1t, sil_free])
                        pH1_free[par] = silt
                        gt = p.step("vector", lambda e, t=G, hc=hc, tt=tt, si=s["sil"], ps=pH3, hsl=hsl:
                                    e.tensor_mul(t[:, hc, tt * 512:(tt + 1) * 512], si[:, :], ps[:, hsl]),
                                    deps=[silt, m3t, G_lastread])
                        pH3_free[par] = gt
                        sil_free = gt
                        g_toks.append(gt)
                w1_lastuse[gb] = m1t
                w3_lastuse[gb] = m3t
            out_toks = []
            for dc in range(8):
                wb = dc % 2
                ld2 = p.dma("gpsimd", lambda e, t=s["w2g"][wb], ex=ex, dc=dc,
                            v=d["W2T0"][ex][:, dc * 128:(dc + 1) * 128].rearrange("(a p) q -> p a q", p=128):
                            e.dma_start(out=t[:, :, :], in_=v),
                            deps=[w2_lastuse[wb]], chan=f"w2g{wb}")
                for tt in range(2):
                    par = tt
                    hsl = slice(par * 512, (par + 1) * 512)
                    pHo = pH3
                    mo = None
                    for hc in range(22):
                        mo = p.step("tensor", lambda e, hc=hc, tt=tt, w=s["w2g"][wb], ps=pHo, hsl=hsl:
                                    e.matmul(ps[:, hsl], w[:, hc, :], G[:, hc, tt * 512:(tt + 1) * 512],
                                             start=(hc == 0), stop=(hc == 21)),
                                    deps=(([ld2, pH3_free[par]] + g_toks) if hc == 0 else []))
                    if ex == 0:
                        yt = p.step("vector", lambda e, t=y, dc=dc, tt=tt, cb=s["cwb"], ps=pHo, hsl=hsl:
                                    e.tensor_tensor(t[:, dc, tt * 512:(tt + 1) * 512], ps[:, hsl], cb[:, tt, :], ALU.mult),
                                    deps=[mo, cbe1 if tt == 0 else cbe2, y_lastread])
                    else:
                        tmpt = p.step("vector", lambda e, t=s["tmp"], cb=s["cwb"], tt=tt, ps=pHo, hsl=hsl:
                                      e.tensor_tensor(t[:, :], ps[:, hsl], cb[:, tt, :], ALU.mult),
                                      deps=[mo, cbe1 if tt == 0 else cbe2, tmp_free])
                        yt = p.step("vector", lambda e, t=y, dc=dc, tt=tt, tp=s["tmp"]:
                                    e.tensor_add(t[:, dc, tt * 512:(tt + 1) * 512],
                                                 t[:, dc, tt * 512:(tt + 1) * 512], tp[:, :]),
                                    deps=[tmpt])
                        tmp_free = yt
                    pH3_free[par] = yt
                    out_toks.append(yt)
                w2_lastuse[wb] = mo
            G_lastread = mo
        radd = []
        for oc in range(8):
            rt = p.step("vector", lambda e, t=y, oc=oc, x=x1c:
                        e.tensor_add(t[:, oc, :], t[:, oc, :], x[:, oc, :]),
                        deps=[ldr] + out_toks)
            radd.append(rt)
        x1c_lastuse = radd[-1]
        x1b_lastuse = m3t
        stf, stb, pR1_free, pR2_free, pBC_free, norm_all = emit_LN(
            p, s, sh, consts, y, d["x2_fm"], d["x2b_fm"], tb0, radd + pH1_free,
            pR1_free, pR2_free, pBC_free, stb)
        pH1_free = [pBC_free, pBC_free]
        y_lastread = stb if stb is not None else stf
        cwT_lastuse = cwr
    p.wait_all("gpsimd", [stf])
    p.wait_all("sync", [stb])
    return [stf, stb] + norm_all


def alloc_C(nc, stack):
    s = {}
    s["xw"] = [sbuf(nc, stack, f"c_xw{i}", [128, 8, 514], BF16) for i in range(2)]
    s["qkw"] = sbuf(nc, stack, "c_qkw", [128, 8, 2048], BF16)
    s["lw"] = [sbuf(nc, stack, f"c_lw{i}", [128, 3, 8, 128], BF16) for i in range(2)]
    s["x2c"] = sbuf(nc, stack, "c_x2c", [128, 8, 512], F32)
    s["r"] = sbuf(nc, stack, "c_r", [128, 8, 1024], F32)
    s["identb"] = sbuf(nc, stack, "c_identb", [128, 128], BF16)
    s["onescb"] = sbuf(nc, stack, "c_onescb", [128, 1], BF16)
    s["theta"] = sbuf(nc, stack, "c_theta", [1, 512], F32)
    s["posc"] = sbuf(nc, stack, "c_posc", [1, 128], F32)
    s["qkb"] = sbuf(nc, stack, "c_qkb", [1, 2048], F32)
    s["lepeb"] = sbuf(nc, stack, "c_lepeb", [128, 8], F32)
    s["wl"] = sbuf(nc, stack, "c_wl", [8, 2], F32)
    s["ag"] = sbuf(nc, stack, "c_ag", [8, 1024], F32)
    s["u"] = sbuf(nc, stack, "c_u", [128, 512], F32)
    s["kk"] = sbuf(nc, stack, "c_kk", [128, 512], F32)
    s["red"] = sbuf(nc, stack, "c_red", [128, 512], F32)
    s["sin"] = sbuf(nc, stack, "c_sin", [128, 512], F32)
    s["cos"] = sbuf(nc, stack, "c_cos", [128, 512], F32)
    s["t1"] = sbuf(nc, stack, "c_t1", [128, 1024], F32)
    s["t2"] = sbuf(nc, stack, "c_t2", [128, 512], F32)
    s["kh"] = sbuf(nc, stack, "c_kh", [128, 1024], F32)
    s["kb"] = sbuf(nc, stack, "c_kb", [128, 1024], BF16)
    s["kr"] = sbuf(nc, stack, "c_kr", [128, 1024], BF16)
    s["vt"] = sbuf(nc, stack, "c_vt", [128, 1024], BF16)
    s["prpi"] = sbuf(nc, stack, "c_prpi", [128, 1024], F32)
    s["zq"] = sbuf(nc, stack, "c_zq", [128, 1024], BF16)
    s["zqT"] = sbuf(nc, stack, "c_zqT", [128, 8, 128], BF16)
    s["z8"] = sbuf(nc, stack, "c_z8", [128, 24], F32)
    s["kvb"] = sbuf(nc, stack, "c_kvb", [128, 8, 128], BF16)
    s["kmb"] = sbuf(nc, stack, "c_kmb", [128, 1024], F32)
    s["ksr"] = sbuf(nc, stack, "c_ksr", [1, 1024], F32)
    s["kmr"] = sbuf(nc, stack, "c_kmr", [1, 1024], F32)
    s["hlr"] = sbuf(nc, stack, "c_hlr", [1, 1024], F32)
    s["lep"] = sbuf(nc, stack, "c_lep", [128, 512], F32)
    s["sq"] = sbuf(nc, stack, "c_sq", [128, 512], F32)
    s["rowA"] = sbuf(nc, stack, "c_rowA", [1, 512], F32)
    s["rowB"] = sbuf(nc, stack, "c_rowB", [1, 512], F32)
    s["rowC"] = sbuf(nc, stack, "c_rowC", [1, 512], F32)
    s["bcm"] = sbuf(nc, stack, "c_bcm", [128, 512], F32)
    s["bcr"] = sbuf(nc, stack, "c_bcr", [128, 512], F32)
    s["pANG"] = psum(nc, stack, "c_pANG", [128, 512])
    s["psK0"] = psum(nc, stack, "c_psK0", [128, 512])
    s["psK1"] = psum(nc, stack, "c_psK1", [128, 512])
    s["pR1"] = psum(nc, stack, "c_pR1", [1, 512])
    s["pR2"] = psum(nc, stack, "c_pR2", [1, 512])
    s["pKV"] = psum(nc, stack, "c_pKV", [128, 1024])
    s["ptv"] = psum(nc, stack, "c_ptv", [128, 1024], BF16)
    s["pBC"] = s["pKV"]
    return s


def _rope_tables(p, d, s, tc, deps):
    pANG = s["pANG"]
    pc = p.dma("sync", lambda e, t=s["posc"], v=d["pos_rows"][tc:tc + 1, :]:
               e.dma_start(out=t[0:1, :], in_=v), deps=deps, chan="posc")
    ang = p.step("tensor", lambda e, ps=pANG, pr=s["posc"], th=s["theta"]:
                 e.matmul(ps[:, :], pr[0:1, :], th[0:1, :],
                          start=True, stop=True), deps=[pc] + deps)
    a1 = p.step("vector", lambda e, t=s["u"], ps=pANG:
                e.tensor_scalar(t[:, :], ps[:, :], INV_2PI, None, ALU.mult), deps=[ang])
    a2 = p.step("vector", lambda e, t=s["kk"], u=s["u"]:
                e.tensor_scalar(t[:, :], u[:, :], MAGIC, MAGIC, ALU.add, ALU.subtract), deps=[a1])
    a3 = p.step("vector", lambda e, t=s["red"], k=s["kk"], ps=pANG:
                e.scalar_tensor_tensor(t[:, :], k[:, :], -TWO_PI_HI, ps[:, :], ALU.mult, ALU.add), deps=[a2])
    a4 = p.step("vector", lambda e, t=s["u"], k=s["kk"], r=s["red"]:
                e.scalar_tensor_tensor(t[:, :], k[:, :], -TWO_PI_LO, r[:, :], ALU.mult, ALU.add), deps=[a3])
    sin_t = p.step("scalar", lambda e, t=s["sin"], r=s["u"]:
                   e.activation(t[:, :], r[:, :], AF.Sin), deps=[a4])
    b1 = p.step("vector", lambda e, t=s["u"], ps=pANG:
                e.tensor_scalar(t[:, :], ps[:, :], INV_2PI, 0.25, ALU.mult, ALU.add), deps=[a4, sin_t])
    b2 = p.step("vector", lambda e, t=s["kk"], u=s["u"]:
                e.tensor_scalar(t[:, :], u[:, :], MAGIC, MAGIC, ALU.add, ALU.subtract), deps=[b1])
    b3 = p.step("vector", lambda e, t=s["red"], k=s["kk"], ps=pANG:
                e.scalar_tensor_tensor(t[:, :], k[:, :], -TWO_PI_HI, ps[:, :], ALU.mult, ALU.add), deps=[b2])
    b4 = p.step("vector", lambda e, t=s["u"], k=s["kk"], r=s["red"]:
                e.scalar_tensor_tensor(t[:, :], k[:, :], -TWO_PI_LO, r[:, :], ALU.mult, ALU.add), deps=[b3])
    b5 = p.step("vector", lambda e, t=s["u"]:
                e.tensor_scalar(t[:, :], t[:, :], HALF_PI, None, ALU.add), deps=[b4])
    cos_t = p.step("scalar", lambda e, t=s["cos"], r=s["u"]:
                   e.activation(t[:, :], r[:, :], AF.Sin), deps=[b5])
    return sin_t, cos_t, b5


def _elu1(p, s, psum0, psum1, out, deps0, deps1):
    toks = []
    for oh, ps in ((0, psum0), (1, psum1)):
        sl = slice(oh * 512, (oh + 1) * 512)
        c1 = p.step("vector", lambda e, t=s["t1"], ps=ps, sl=sl:
                    e.tensor_scalar(t[:, sl], ps[:, :], 0.0, None, ALU.min),
                    deps=(deps0 if oh == 0 else deps1))
        c2 = p.step("vector", lambda e, t=s["t2"], ps=ps, sl=sl:
                    e.tensor_scalar(t[:, :], ps[:, :], 0.0, None, ALU.max),
                    deps=[c1])
        c3 = p.step("scalar", lambda e, t=s["t1"], sl=sl:
                    e.activation(t[:, sl], t[:, sl], AF.Exp), deps=[c1])
        c4 = p.step("vector", lambda e, o=out, t1=s["t1"], t2=s["t2"], sl=sl:
                    e.tensor_add(o[:, sl], t1[:, sl], t2[:, :]), deps=[c2, c3])
        toks.append(c4)
    return toks


def emit_C(p, d, s, sh, consts, rg):
    qkw_tok = p.dma("gpsimd", lambda e, t=s["qkw"], v=d["qk_w"].rearrange("(c p) o -> p c o", p=128):
                    e.dma_start(out=t[:, :, :], in_=v))
    small = [
        p.dma("sync", lambda e, t=s["identb"], v=d["identb"]: e.dma_start(out=t[:, :], in_=v)),
        p.dma("sync", lambda e, t=s["onescb"], v=d["ones_colb"]: e.dma_start(out=t[:, :], in_=v)),
        p.dma("sync", lambda e, t=s["theta"], v=d["theta"]: e.dma_start(out=t[:, :], in_=v)),
        p.dma("sync", lambda e, t=s["qkb"], v=d["qk_b"]: e.dma_start(out=t[:, :], in_=v)),
        p.dma("sync", lambda e, t=s["lepeb"], v=d["lepe_b"]: e.dma_start(out=t[:, :], in_=v)),
        p.dma("sync", lambda e, t=s["wl"], v=d["wlr"]: e.dma_start(out=t[:, :], in_=v)),
    ] + consts
    pANG, psK0, psK1, pKV, ptv = s["pANG"], s["psK0"], s["psK1"], s["pKV"], s["ptv"]

    def load_window(w, buf, with_halo, deps):
        toks = []
        lo = w * 512 - 1
        hi = w * 512 + 513
        clo = max(lo, 0)
        chi = min(hi, TPC)
        off = clo - lo
        toks.append(p.dma("sync", lambda e, t=s["xw"][buf], off=off, clo=clo, chi=chi,
                          v=fm(d["x2b_fm"])[:, :, clo:chi]:
                          e.dma_start(out=t[:, :, off:off + (chi - clo)], in_=v), deps=deps, chan=f"xw{buf}"))
        if with_halo:
            if w == 0:
                toks.append(p.dma("gpsimd", lambda e, t=s["xw"][buf]:
                                  e.dma_start(out=t[:, :, 0:1],
                                              in_=d["halo_l"].rearrange("1 (c p) -> p c 1", p=128),
                                              allow_slow_non_contiguous=True),
                                  deps=deps + [halo_ready[0]]))
            if w == 7:
                toks.append(p.dma("gpsimd", lambda e, t=s["xw"][buf]:
                                  e.dma_start(out=t[:, :, 513:514],
                                              in_=d["halo_r"].rearrange("1 (c p) -> p c 1", p=128),
                                              allow_slow_non_contiguous=True),
                                  deps=deps + [halo_ready[1]]))
        return toks

    p.region = "C1"
    kv_mm = None
    ks_mm = None
    tab_free = None
    elu_free = None
    kb_free = None
    kr_free = None
    vt_free = None
    xw_lastuse = [None, None]
    wload = [None, None]
    halo_ready = [None, None]
    for w in range(8):
        buf = w % 2
        wl_toks = load_window(w, buf, False, [xw_lastuse[buf]])
        wload[buf] = wl_toks[0]
        for tci in range(4):
            tc = w * 4 + tci
            xsl = lambda c, tci=tci: (slice(None), c, slice(1 + tci * 128, 1 + (tci + 1) * 128))
            sin_t, cos_t, tabend = _rope_tables(p, d, s, tc, [tab_free] + (small if tc == 0 else []))
            kms = []
            for oh, ps in ((0, psK0), (1, psK1)):
                mmt = None
                for c in range(8):
                    mmt = p.step("tensor", lambda e, c=c, tci=tci, oh=oh, x=s["xw"][buf], w_=s["qkw"], ps=ps:
                                 e.matmul(ps[:, :], x[:, c, 1 + tci * 128:1 + (tci + 1) * 128],
                                          w_[:, c, 1024 + oh * 512:1024 + (oh + 1) * 512],
                                          start=(c == 0), stop=False),
                                 deps=([wload[buf], qkw_tok, elu_free] if c == 0 else []))
                mmt = p.step("tensor", lambda e, oh=oh, o=sh["ones_row"], b=s["qkb"], ps=ps:
                             e.matmul(ps[:, :], o[0:1, :], b[0:1, 1024 + oh * 512:1024 + (oh + 1) * 512],
                                      start=False, stop=True), deps=[])
                kms.append(mmt)
            el = _elu1(p, s, psK0, psK1, s["kh"], [kms[0]], [kms[1]])
            elu_free = el[-1]
            kbt = p.step("vector", lambda e, t=s["kb"], k=s["kh"]: e.tensor_copy(t[:, :], k[:, :]),
                         deps=el + [kb_free])
            ks1 = p.step("tensor", lambda e, t=s["kb"], o=s["onescb"], ps=s["pR1"], tc=tc:
                         e.matmul(ps[0:1, :], o[:, 0:1], t[:, 0:512],
                                  start=(tc == 0), stop=(tc == 31)), deps=[kbt])
            ks_mm = p.step("tensor", lambda e, t=s["kb"], o=s["onescb"], ps=s["pR2"], tc=tc:
                           e.matmul(ps[0:1, :], o[:, 0:1], t[:, 512:1024],
                                    start=(tc == 0), stop=(tc == 31)), deps=[])
            r1 = p.step("vector", lambda e, t=s["t1"], c_=s["cos"], k=s["kb"]:
                        e.tensor_mul(t[:, 0:512], c_[:, :], k[:, 0:1024:2]), deps=[kbt, cos_t, kr_free])
            r2 = p.step("vector", lambda e, t=s["t2"], sn=s["sin"], k=s["kb"]:
                        e.tensor_mul(t[:, :], sn[:, :], k[:, 1:1024:2]), deps=[sin_t])
            r3 = p.step("vector", lambda e, o=s["kr"], a=s["t1"], b=s["t2"]:
                        e.tensor_sub(o[:, 0:512], a[:, 0:512], b[:, :]), deps=[r1, r2])
            r4 = p.step("vector", lambda e, t=s["t1"], c_=s["cos"], k=s["kb"]:
                        e.tensor_mul(t[:, 0:512], c_[:, :], k[:, 1:1024:2]), deps=[r3])
            r5 = p.step("vector", lambda e, t=s["t2"], sn=s["sin"], k=s["kb"]:
                        e.tensor_mul(t[:, :], sn[:, :], k[:, 0:1024:2]), deps=[r3])
            r6 = p.step("vector", lambda e, o=s["kr"], a=s["t1"], b=s["t2"]:
                        e.tensor_add(o[:, 512:1024], a[:, 0:512], b[:, :]), deps=[r4, r5])
            tab_free = r6
            if tc == 0 and "dbgSIN" in d:
                p.dma("sync", lambda e: e.dma_start(out=d["dbgSIN"], in_=s["sin"][:, :]), deps=[r6])
                p.dma("sync", lambda e: e.dma_start(out=d["dbgCOS"], in_=s["cos"][:, :]), deps=[r6])
                v1 = p.step("vector", lambda e: e.tensor_copy(s["t1"][:, :], s["kb"][:, :]), deps=[r6])
                s1tok = p.dma("sync", lambda e: e.dma_start(out=d["dbgKB"], in_=s["t1"][:, :]), deps=[v1])
                v2 = p.step("vector", lambda e: e.tensor_copy(s["t1"][:, :], s["kr"][:, :]), deps=[s1tok])
                p.dma("sync", lambda e: e.dma_start(out=d["dbgKR"], in_=s["t1"][:, :]), deps=[v2])
            vts = []
            for c in range(8):
                vts.append(p.step("tensor", lambda e, c=c, x=s["xw"][buf], tci=tci, ps=ptv, idn=s["identb"]:
                                  e.transpose(ps[:, c * 128:(c + 1) * 128],
                                              x[:, c, 1 + tci * 128:1 + (tci + 1) * 128], idn[:, :]),
                                  deps=([ks_mm, vt_free, small[0]] if c == 0 else [])))
            vte = p.step("vector", lambda e, t=s["vt"], ps=ptv: e.tensor_copy(t[:, :], ps[:, :]),
                         deps=[vts[-1]])
            kvdeps_extra = [p.tok("sync")] if (tc == 0 and "dbgSIN" in d) else []
            for h in range(8):
                kv_mm = p.step("tensor", lambda e, h=h, kr=s["kr"], v=s["vt"], ps=pKV, tc=tc:
                               e.matmul(ps[:, h * 128:(h + 1) * 128], kr[:, h * 128:(h + 1) * 128],
                                        v[:, h * 128:(h + 1) * 128],
                                        start=(tc == 0 and h % 4 == 0), stop=(tc == 31)),
                               deps=([r6, vte] + kvdeps_extra if h == 0 else []))
            vt_free = kv_mm
            kr_free = kv_mm
            kb_free = ks_mm
        xw_lastuse[buf] = kv_mm
    kvev = p.step("vector", lambda e, t=s["t1"], ps=pKV:
                  e.tensor_scalar(t[:, :], ps[:, :], 1.0 / 128.0, None, ALU.mult), deps=[kv_mm])
    ksev = p.step("vector", [
        lambda e, t=s["ksr"], ps=s["pR1"]: e.tensor_copy(t[0:1, 0:512], ps[0:1, :]),
        lambda e, t=s["ksr"], ps=s["pR2"]: e.tensor_copy(t[0:1, 512:1024], ps[0:1, :]),
    ], deps=[ks_mm])
    d1 = p.dma("gpsimd", lambda e, t=s["t1"]: e.dma_start(out=d["cc_kv_in"], in_=t[:, :]), deps=[kvev])
    d2 = p.dma("gpsimd", lambda e, t=s["ksr"]: e.dma_start(out=d["cc_ks_in"], in_=t[0:1, :]), deps=[ksev])
    d3 = p.dma("gpsimd", lambda e: e.dma_start(out=d["ag_in"][0:1, :], in_=d["x2_fm"][:, 0:1].rearrange("f o -> o f"),
                                               allow_slow_non_contiguous=True), deps=[])
    d4 = p.dma("gpsimd", lambda e: e.dma_start(out=d["ag_in"][1:2, :], in_=d["x2_fm"][:, 4095:4096].rearrange("f o -> o f"),
                                               allow_slow_non_contiguous=True), deps=[])
    cc1 = p.cc(lambda e: e.collective_compute(
        "AllReduce", ALU.add, replica_groups=rg, ins=[d["cc_kv_in"]], outs=[d["cc_kv_out"]]),
        deps=[d1])
    cc2 = p.cc(lambda e: e.collective_compute(
        "AllReduce", ALU.add, replica_groups=rg, ins=[d["cc_ks_in"]], outs=[d["cc_ks_out"]]),
        deps=[d2])
    cc3 = p.cc(lambda e: e.collective_compute(
        "AllGather", ALU.bypass, replica_groups=rg, ins=[d["ag_in"]], outs=[d["ag_out"]]),
        deps=[d3, d4])
    agl = p.dma("sync", lambda e, t=s["ag"]: e.dma_start(out=t[:, :], in_=d["ag_out"]), deps=[cc3])
    hl1 = p.step("tensor", lambda e, ps=pANG, w=s["wl"], a=s["ag"]:
                 e.matmul(ps[0:1, :], w[0:8, 0:1], a[0:8, 0:512], start=True, stop=True),
                 deps=[agl, small[-1], tab_free])
    hle1 = p.step("vector", lambda e, t=s["hlr"], ps=pANG: e.tensor_copy(t[0:1, 0:512], ps[0:1, :]), deps=[hl1])
    hl2 = p.step("tensor", lambda e, ps=pANG, w=s["wl"], a=s["ag"]:
                 e.matmul(ps[0:1, :], w[0:8, 0:1], a[0:8, 512:1024], start=True, stop=True), deps=[hle1])
    hle2 = p.step("vector", lambda e, t=s["hlr"], ps=pANG: e.tensor_copy(t[0:1, 512:1024], ps[0:1, :]), deps=[hl2])
    dhl = p.dma("gpsimd", lambda e, t=s["hlr"]: e.dma_start(out=d["halo_l"], in_=t[0:1, :]), deps=[hle2])
    hr1 = p.step("tensor", lambda e, ps=pANG, w=s["wl"], a=s["ag"]:
                 e.matmul(ps[0:1, :], w[0:8, 1:2], a[0:8, 0:512], start=True, stop=True), deps=[hle2])
    hre1 = p.step("vector", lambda e, t=s["hlr"], ps=pANG: e.tensor_copy(t[0:1, 0:512], ps[0:1, :]), deps=[hr1, dhl])
    hr2 = p.step("tensor", lambda e, ps=pANG, w=s["wl"], a=s["ag"]:
                 e.matmul(ps[0:1, :], w[0:8, 1:2], a[0:8, 512:1024], start=True, stop=True), deps=[hre1])
    hre2 = p.step("vector", lambda e, t=s["hlr"], ps=pANG: e.tensor_copy(t[0:1, 512:1024], ps[0:1, :]), deps=[hr2])
    dhr = p.dma("gpsimd", lambda e, t=s["hlr"]: e.dma_start(out=d["halo_r"], in_=t[0:1, :]), deps=[hre2])
    halo_ready = [dhl, dhr]
    kvg = p.dma("gpsimd", lambda e, t=s["kvb"], v=d["cc_kv_out"].rearrange("p (h ee) -> p h ee", h=8):
                e.dma_start(out=t[:, :, :], in_=v), deps=[cc1])
    ksg = p.dma("sync", lambda e, t=s["ksr"]: e.dma_start(out=t[0:1, :], in_=d["cc_ks_out"]), deps=[cc2, ksev, d2])
    km = p.step("vector", lambda e, t=s["kmr"], k=s["ksr"]:
                e.tensor_scalar(t[0:1, :], k[0:1, :], 1.0 / 16384.0, None, ALU.mult), deps=[ksg])
    kb1 = p.step("tensor", lambda e, ps=pANG, o=sh["ones_row"], k=s["kmr"]:
                 e.matmul(ps[:, :], o[0:1, :], k[0:1, 0:512], start=True, stop=True), deps=[km, hre2, dhr])
    kbe1 = p.step("vector", lambda e, t=s["kmb"], ps=pANG: e.tensor_copy(t[:, 0:512], ps[:, :]), deps=[kb1])
    kb2 = p.step("tensor", lambda e, ps=pANG, o=sh["ones_row"], k=s["kmr"]:
                 e.matmul(ps[:, :], o[0:1, :], k[0:1, 512:1024], start=True, stop=True), deps=[kbe1])
    kbe2 = p.step("vector", lambda e, t=s["kmb"], ps=pANG: e.tensor_copy(t[:, 512:1024], ps[:, :]), deps=[kb2])

    p.region = "C2"
    x2c_lastuse = None
    stf = None
    stb = None
    pR1_free = ksg
    pR2_free = ksg
    pBC_free = kvg
    lw_lastuse = [None, None]
    zq_free = None
    zqT_free = None
    prpi_free = None
    qh_free = None
    lep_free = None
    r_sb = s["r"]
    for th in range(4):
        th0 = th * 1024
        half = [[], []]
        for tth in range(2):
            tt = th * 2 + tth
            buf = tt % 2
            wl_toks = load_window(tt, buf, True, [xw_lastuse[buf]])
            ldx2 = p.dma("sync", lambda e, t=s["x2c"], v=fm(d["x2_fm"])[:, :, tt * 512:(tt + 1) * 512]:
                         e.dma_start(out=t[:, :, :], in_=v), deps=[x2c_lastuse], chan="xmc")
            out_toks = []
            for tci in range(4):
                tc = tt * 4 + tci
                sin_t, cos_t, tabend = _rope_tables(p, d, s, tc, [tab_free, kbe2])
                kms = []
                for oh, ps in ((0, psK0), (1, psK1)):
                    mmt = None
                    for c in range(8):
                        mmt = p.step("tensor", lambda e, c=c, tci=tci, oh=oh, x=s["xw"][buf], w_=s["qkw"], ps=ps:
                                     e.matmul(ps[:, :], x[:, c, 1 + tci * 128:1 + (tci + 1) * 128],
                                              w_[:, c, oh * 512:(oh + 1) * 512],
                                              start=(c == 0), stop=False),
                                     deps=(wl_toks + [elu_free] if c == 0 else []))
                    mmt = p.step("tensor", lambda e, oh=oh, o=sh["ones_row"], b=s["qkb"], ps=ps:
                                 e.matmul(ps[:, :], o[0:1, :], b[0:1, oh * 512:(oh + 1) * 512],
                                          start=False, stop=True), deps=[])
                    kms.append(mmt)
                el = _elu1(p, s, psK0, psK1, s["kh"], [kms[0], qh_free], [kms[1]])
                elu_free = el[-1]
                z1 = p.step("vector", lambda e, t=s["t1"], q=s["kh"], k=s["kmb"]:
                            e.tensor_mul(t[:, :], q[:, :], k[:, :]), deps=el + [kbe2, kbe1])
                z2 = p.step("vector", lambda e, t=s["z8"], a=s["t1"]:
                            e.reduce_sum(t[:, 0:8], a[:, :].rearrange("p (h ee) -> p h ee", h=8), axis=AX.X),
                            deps=[z1])
                z3 = p.step("vector", lambda e, t=s["z8"]:
                            e.tensor_scalar(t[:, 8:16], t[:, 0:8], 1e-6, None, ALU.add), deps=[z2])
                z4 = p.step("vector", lambda e, t=s["z8"]:
                            e.reciprocal(t[:, 16:24], t[:, 8:16]), deps=[z3])
                q1 = p.step("vector", lambda e, t=s["t1"], c_=s["cos"], k=s["kh"]:
                            e.tensor_mul(t[:, 0:512], c_[:, :], k[:, 0:1024:2]), deps=[z1, cos_t, prpi_free])
                q2 = p.step("vector", lambda e, t=s["t2"], sn=s["sin"], k=s["kh"]:
                            e.tensor_mul(t[:, :], sn[:, :], k[:, 1:1024:2]), deps=[sin_t])
                q3 = p.step("vector", lambda e, o=s["prpi"], a=s["t1"], b=s["t2"]:
                            e.tensor_sub(o[:, 0:512], a[:, 0:512], b[:, :]), deps=[q1, q2])
                q4 = p.step("vector", lambda e, t=s["t1"], c_=s["cos"], k=s["kh"]:
                            e.tensor_mul(t[:, 0:512], c_[:, :], k[:, 1:1024:2]), deps=[q3])
                q5 = p.step("vector", lambda e, t=s["t2"], sn=s["sin"], k=s["kh"]:
                            e.tensor_mul(t[:, :], sn[:, :], k[:, 0:1024:2]), deps=[q3])
                q6 = p.step("vector", lambda e, o=s["prpi"], a=s["t1"], b=s["t2"]:
                            e.tensor_add(o[:, 512:1024], a[:, 0:512], b[:, :]), deps=[q4, q5])
                tab_free = q6
                qh_free = q6
                zqs = []
                for h in range(8):
                    zqs.append(p.step("vector", lambda e, t=s["zq"], pp=s["prpi"], z=s["z8"], h=h:
                                      e.tensor_scalar(t[:, h * 128:(h + 1) * 128], pp[:, h * 128:(h + 1) * 128],
                                                      z[:, 16 + h:17 + h], None, ALU.mult),
                                      deps=[q6 if h >= 4 else q3, z4, zq_free]))
                prpi_free = zqs[-1]
                tts = []
                for h in range(8):
                    tts.append(p.step("tensor", lambda e, h=h, t=s["zq"], ps=ptv, idn=s["identb"]:
                                      e.transpose(ps[:, h * 128:(h + 1) * 128], t[:, h * 128:(h + 1) * 128],
                                                  idn[:, :]),
                                      deps=([zqs[-1], zqT_free, vt_free] if h == 0 else [])))
                zte = p.step("vector", lambda e, t=s["zqT"], ps=ptv:
                             e.tensor_copy(t[:, :, :], ps[:, :].rearrange("p (h q) -> p h q", h=8)),
                             deps=[tts[-1]])
                zq_free = zte
                vt_free = tts[-1]
                om = None
                for h in range(8):
                    om = p.step("tensor", lambda e, h=h, kv=s["kvb"], zt=s["zqT"], ps=pKV:
                                e.matmul(ps[:, h * 128:(h + 1) * 128], kv[:, h, :], zt[:, h, :],
                                         start=True, stop=True),
                                deps=([zte, kvg, pBC_free] if h == 0 else []))
                zqT_free = om
                rt = p.step("vector", lambda e, t=r_sb, ttt=tth, tci=tci, x=s["x2c"], ps=pKV:
                            e.tensor_add(t[:, :, ttt * 512 + tci * 128:ttt * 512 + (tci + 1) * 128],
                                         ps[:, :].rearrange("p (h q) -> p h q", h=8),
                                         x[:, :, tci * 128:(tci + 1) * 128]),
                            deps=[om, ldx2] + ([stf, stb] if tci == 0 and tth == 0 else []))
                pBC_free = rt
                out_toks.append(rt)
            p.region = "C2L"
            for oc in range(8):
                lb = oc % 2
                ldl = p.dma("gpsimd", lambda e, t=s["lw"][lb], oc=oc,
                            v=d["lepe_w"][:, :, oc * 128:(oc + 1) * 128].rearrange("w (c p) o -> p w c o", p=128):
                            e.dma_start(out=t[:, :, :, :], in_=v),
                            deps=[lw_lastuse[lb]], chan=f"lw{lb}")
                lm = None
                for wk in range(3):
                    for c in range(8):
                        lm = p.step("tensor", lambda e, wk=wk, c=c, lw=s["lw"][lb], x=s["xw"][buf], ps=psK0:
                                    e.matmul(ps[:, :], lw[:, wk, c, :], x[:, c, wk:wk + 512],
                                             start=(wk == 0 and c == 0), stop=(wk == 2 and c == 7)),
                                    deps=(([ldl, elu_free] + wl_toks) if (wk == 0 and c == 0) else []))
                lw_lastuse[lb] = lm
                lev = p.step("vector", lambda e, t=s["lep"], ps=psK0, oc=oc, lb2=s["lepeb"]:
                             e.tensor_scalar(t[:, :], ps[:, :], lb2[:, oc:oc + 1], None, ALU.add),
                             deps=[lm, lep_free])
                elu_free = lev
                lra = p.step("vector", lambda e, t=r_sb, oc=oc, ttt=tth, le=s["lep"]:
                             e.tensor_add(t[:, oc, ttt * 512:(ttt + 1) * 512],
                                          t[:, oc, ttt * 512:(ttt + 1) * 512], le[:, :]),
                             deps=[lev] + out_toks)
                lep_free = lra
                out_toks.append(lra)
            p.region = "C2"
            x2c_lastuse = out_toks[-1]
            xw_lastuse[buf] = lm
            half[tth] = list(out_toks)
        stf, stb, pR1_free, pR2_free, pBC_free, norm_all = emit_LN(
            p, s, sh, consts, r_sb, d["x3_fm"], d["x3b_fm"], th0, half[0] + half[1],
            pR1_free, pR2_free, pBC_free, stb)
    p.wait_all("gpsimd", [stf])
    p.wait_all("sync", [stb])
    return [stf, stb]


def alloc_D(nc, stack):
    s = {}
    s["x3b"] = sbuf(nc, stack, "d_x3b", [128, 8, 1024], BF16)
    s["x3c"] = sbuf(nc, stack, "d_x3c", [128, 8, 512], F32)
    s["y"] = sbuf(nc, stack, "d_y", [128, 8, 1024], F32)
    s["H"] = sbuf(nc, stack, "d_H", [128, 32, 1024], BF16)
    s["wg1"] = sbuf(nc, stack, "d_wg1", [128, 8, 8], F32)
    s["bgr"] = sbuf(nc, stack, "d_bgr", [1, 8], F32)
    s["b1s"] = sbuf(nc, stack, "d_b1s", [128, 8, 32], F32)
    s["b2s"] = sbuf(nc, stack, "d_b2s", [8, 1024], F32)
    s["identf"] = sbuf(nc, stack, "d_identf", [128, 128], F32)
    s["w1g"] = [sbuf(nc, stack, f"d_w1g{i}", [128, 8, 512], BF16) for i in range(2)]
    s["w2g"] = [sbuf(nc, stack, f"d_w2g{i}", [128, 32, 128], BF16) for i in range(2)]
    s["lg"] = sbuf(nc, stack, "d_lg", [128, 32], F32)
    s["gcT"] = sbuf(nc, stack, "d_gcT", [8, 1024], F32)
    s["gcrow"] = sbuf(nc, stack, "d_gcrow", [1, 1024], F32)
    s["gcb"] = sbuf(nc, stack, "d_gcb", [128, 2, 512], F32)
    s["tmp"] = sbuf(nc, stack, "d_tmp", [128, 512], F32)
    s["ot"] = sbuf(nc, stack, "d_ot", [128, 1024], F32)
    s["sq"] = sbuf(nc, stack, "d_sq", [128, 512], F32)
    s["rowA"] = sbuf(nc, stack, "d_rowA", [1, 512], F32)
    s["rowB"] = sbuf(nc, stack, "d_rowB", [1, 512], F32)
    s["rowC"] = sbuf(nc, stack, "d_rowC", [1, 512], F32)
    s["bcm"] = sbuf(nc, stack, "d_bcm", [128, 512], F32)
    s["bcr"] = sbuf(nc, stack, "d_bcr", [128, 512], F32)
    s["pg"] = psum(nc, stack, "d_pg", [128, 512])
    s["pH1"] = psum(nc, stack, "d_pH1", [128, 512])
    s["pH3"] = psum(nc, stack, "d_pH3", [128, 512])
    s["pcw"] = psum(nc, stack, "d_pcw", [8, 128])
    s["pR1"] = psum(nc, stack, "d_pR1", [1, 512])
    s["pR2"] = psum(nc, stack, "d_pR2", [1, 512])
    s["pBC"] = psum(nc, stack, "d_pBC", [128, 1024])
    return s


def emit_D(p, d, s, sh, consts):
    wg_tok = p.dma("sync", lambda e, t=s["wg1"], v=d["Wg1"].rearrange("(c p) ee -> p c ee", p=128):
                   e.dma_start(out=t[:, :, :], in_=v))
    small = [
        p.dma("sync", lambda e, t=s["bgr"], v=d["bg1"]: e.dma_start(out=t[:, :], in_=v)),
        p.dma("sync", lambda e, t=s["b1s"], v=d["b1_1"].rearrange("ee pp hh -> pp ee hh"):
              e.dma_start(out=t[:, :, :], in_=v)),
        p.dma("sync", lambda e, t=s["b2s"], v=d["b2_1"]: e.dma_start(out=t[:, :], in_=v)),
        p.dma("sync", lambda e, t=s["identf"], v=d["identf"]: e.dma_start(out=t[:, :], in_=v)),
        wg_tok,
    ] + consts

    NTB = 4
    TB = 1024
    x3b, x3c, y, H = s["x3b"], s["x3c"], s["y"], s["H"]
    pg, pH1, pH3, pcw = s["pg"], s["pH1"], s["pH3"], s["pcw"]
    lg, gcT = s["lg"], s["gcT"]

    x3b_lastuse = None
    x3c_lastuse = None
    ystored = None
    w1_lastuse = [None, None]
    w2_lastuse = [None, None]
    pg_free = None
    pH1_free = None
    pH3_free = None
    pcw_free = None
    pR1_free = None
    pR2_free = None
    pBC_free = None
    gcT_lastuse = None
    gcrow_lastuse = None
    H_lastread = None
    tmp_free = None
    y_lastread = None
    ot_free = None
    m3t = None
    mo = None

    for tb in range(NTB):
        tb0 = tb * TB
        ldx = p.dma("sync", lambda e, t=x3b, v=fm(d["x3b_fm"])[:, :, tb0:tb0 + TB]:
                    e.dma_start(out=t[:, :, :], in_=v), deps=[x3b_lastuse], chan="xmb")
        gce_toks = []
        for half in range(2):
            ldr = p.dma("sync", lambda e, t=x3c, v=fm(d["x3_fm"])[:, :, tb0 + half * 512:tb0 + (half + 1) * 512]:
                        e.dma_start(out=t[:, :, :], in_=v), deps=[x3c_lastuse], chan="xmc")
            for tci in range(4):
                tc = half * 4 + tci
                gmm = None
                for c in range(8):
                    gmm = p.step("tensor", lambda e, c=c, tci=tci, x=x3c, w=s["wg1"], ps=pg:
                                 e.matmul(ps[:, 0:8], x[:, c, tci * 128:(tci + 1) * 128], w[:, c, :],
                                          start=(c == 0), stop=False),
                                 deps=(([ldr, pg_free] + small) if c == 0 else []))
                gmm = p.step("tensor", lambda e, o=sh["ones_row"], b=s["bgr"], ps=pg:
                             e.matmul(ps[:, 0:8], o[0:1, :], b[0:1, :], start=False, stop=True), deps=[])
                t1 = p.step("vector", lambda e, t=lg, ps=pg: e.tensor_copy(t[:, 0:8], ps[:, 0:8]), deps=[gmm])
                pg_free = t1
                t2 = p.step("vector", lambda e, t=lg: e.reduce_max(t[:, 8:9], t[:, 0:8], axis=AX.X), deps=[t1])
                t3 = p.step("vector", lambda e, t=lg: e.tensor_scalar(t[:, 9:10], t[:, 8:9], -1.0, None, ALU.mult), deps=[t2])
                t4 = p.step("scalar", lambda e, t=lg: e.activation(t[:, 16:24], t[:, 0:8], AF.Exp, bias=t[:, 9:10],
                                                                  accum_out=t[:, 10:11]), deps=[t3])
                t5 = p.step("vector", lambda e, t=lg: e.reciprocal(t[:, 11:12], t[:, 10:11]), deps=[t4])
                t6 = p.step("vector", lambda e, t=lg: e.tensor_scalar(t[:, 24:32], t[:, 0:8], t[:, 8:9], None, ALU.is_ge), deps=[t4])
                t7 = p.step("vector", lambda e, t=lg: e.tensor_scalar(t[:, 24:32], t[:, 24:32], t[:, 11:12], None, ALU.mult), deps=[t6, t5])
                t8 = p.step("tensor", lambda e, t=lg, ps=pcw, idn=s["identf"]:
                            e.transpose(ps[:, :], t[:, 24:32], idn[:, :]), deps=[t7, pcw_free])
                t9 = p.step("vector", lambda e, t=gcT, tc=tc, ps=pcw:
                            e.tensor_copy(t[0:8, tc * 128:(tc + 1) * 128], ps[0:8, :]),
                            deps=[t8, gcT_lastuse])
                pcw_free = t9
                gce_toks.append(t9)
            if half == 0:
                x3c_lastuse = gmm
        binit = []
        for dc in range(8):
            for tt in range(2):
                bmm = p.step("tensor", lambda e, dc=dc, tt=tt, b2=s["b2s"], g=gcT, ps=pg:
                             e.matmul(ps[:, :], b2[0:8, dc * 128:(dc + 1) * 128], g[0:8, tt * 512:(tt + 1) * 512],
                                      start=True, stop=True),
                             deps=(gce_toks + [pg_free] if (dc == 0 and tt == 0) else [pg_free]))
                yt = p.step("vector", lambda e, t=y, dc=dc, tt=tt, ps=pg:
                            e.tensor_copy(t[:, dc, tt * 512:(tt + 1) * 512], ps[:, :]),
                            deps=[bmm, y_lastread])
                pg_free = yt
                binit.append(yt)
        y_lastread = None
        for ex in range(8):
            gcr = p.dma("sync", lambda e, t=s["gcrow"], src=gcT, ex=ex:
                        e.dma_start(out=t[0:1, :], in_=src[ex:ex + 1, :]),
                        deps=gce_toks + [gcrow_lastuse], chan="cwrow")
            cbm = p.step("tensor", lambda e, ps=pg, o=sh["ones_row"], t=s["gcrow"]:
                         e.matmul(ps[:, :], o[0:1, :], t[0:1, 0:512], start=True, stop=True),
                         deps=[gcr, pg_free])
            cbe1 = p.step("vector", lambda e, t=s["gcb"], ps=pg: e.tensor_copy(t[:, 0, :], ps[:, :]), deps=[cbm])
            cbm2 = p.step("tensor", lambda e, ps=pg, o=sh["ones_row"], t=s["gcrow"]:
                          e.matmul(ps[:, :], o[0:1, :], t[0:1, 512:1024], start=True, stop=True),
                          deps=[cbe1])
            cbe2 = p.step("vector", lambda e, t=s["gcb"], ps=pg: e.tensor_copy(t[:, 1, :], ps[:, :]), deps=[cbm2])
            pg_free = cbe2
            gcrow_lastuse = cbm2
            h_toks = []
            for hg in range(8):
                gb = hg % 2
                ld1 = p.dma("gpsimd", lambda e, t=s["w1g"][gb], ex=ex, hg=hg,
                            v=d["W1_1"][ex].rearrange("(c p) hh -> p c hh", p=128)[:, :, hg * 512:(hg + 1) * 512]:
                            e.dma_start(out=t[:, :, :], in_=v),
                            deps=[w1_lastuse[gb]], chan=f"w1g{gb}")
                for hi in range(4):
                    hc = hg * 4 + hi
                    for tt in range(2):
                        ps = pH1 if tt == 0 else pH3
                        m3t = None
                        for c in range(8):
                            m3t = p.step("tensor", lambda e, c=c, hi=hi, tt=tt, w=s["w1g"][gb], x=x3b, ps=ps:
                                         e.matmul(ps[:, :], w[:, c, hi * 128:(hi + 1) * 128],
                                                  x[:, c, tt * 512:(tt + 1) * 512],
                                                  start=(c == 0), stop=(c == 7)),
                                         deps=([ld1, ldx, (pH1_free if tt == 0 else pH3_free)] if c == 0 else []))
                        silt = p.step("scalar", lambda e, t=H, hc=hc, tt=tt, ps=ps, b1=s["b1s"], ex=ex:
                                      e.activation(t[:, hc, tt * 512:(tt + 1) * 512], ps[:, :], AF.Silu,
                                                   bias=b1[:, ex, hc:hc + 1]),
                                      deps=[m3t, H_lastread])
                        if tt == 0:
                            pH1_free = silt
                        else:
                            pH3_free = silt
                        h_toks.append(silt)
                w1_lastuse[gb] = m3t
            for dc in range(8):
                wb = dc % 2
                ld2 = p.dma("gpsimd", lambda e, t=s["w2g"][wb], ex=ex, dc=dc,
                            v=d["W2_1"][ex][:, dc * 128:(dc + 1) * 128].rearrange("(a p) q -> p a q", p=128):
                            e.dma_start(out=t[:, :, :], in_=v),
                            deps=[w2_lastuse[wb]], chan=f"w2g{wb}")
                for tt in range(2):
                    ps = pH1 if tt == 0 else pH3
                    mo = None
                    for hc in range(32):
                        mo = p.step("tensor", lambda e, hc=hc, tt=tt, w=s["w2g"][wb], ps=ps:
                                    e.matmul(ps[:, :], w[:, hc, :], H[:, hc, tt * 512:(tt + 1) * 512],
                                             start=(hc == 0), stop=(hc == 31)),
                                    deps=(([ld2, (pH1_free if tt == 0 else pH3_free)] + h_toks) if hc == 0 else []))
                    tmpt = p.step("vector", lambda e, t=s["tmp"], cb=s["gcb"], tt=tt, ps=ps:
                                  e.tensor_tensor(t[:, :], ps[:, :], cb[:, tt, :], ALU.mult),
                                  deps=[mo, cbe1 if tt == 0 else cbe2, tmp_free])
                    yt = p.step("vector", lambda e, t=y, dc=dc, tt=tt, tp=s["tmp"]:
                                e.tensor_add(t[:, dc, tt * 512:(tt + 1) * 512],
                                             t[:, dc, tt * 512:(tt + 1) * 512], tp[:, :]),
                                deps=[tmpt, binit[dc * 2 + tt]])
                    tmp_free = yt
                    if tt == 0:
                        pH1_free = yt
                    else:
                        pH3_free = yt
                w2_lastuse[wb] = mo
            H_lastread = mo
        radd = []
        for half in range(2):
            ldr2 = p.dma("sync", lambda e, t=x3c, v=fm(d["x3_fm"])[:, :, tb0 + half * 512:tb0 + (half + 1) * 512]:
                         e.dma_start(out=t[:, :, :], in_=v), deps=[x3c_lastuse], chan="xmc")
            for oc in range(8):
                rt = p.step("vector", lambda e, t=y, oc=oc, half=half, x=x3c:
                            e.tensor_add(t[:, oc, half * 512:(half + 1) * 512],
                                         t[:, oc, half * 512:(half + 1) * 512], x[:, oc, :]),
                            deps=[ldr2, yt] + ([] if oc else [pH1_free, pH3_free]))
                radd.append(rt)
            x3c_lastuse = radd[-1]
        x3b_lastuse = m3t
        stf, stb, pR1_free, pR2_free, pBC_free, norm_all = emit_LN_nostore(
            p, s, sh, consts, y, radd, pR1_free, pR2_free, pBC_free)
        for tc in range(8):
            tts = []
            for c in range(8):
                tts.append(p.step("tensor", lambda e, c=c, tc=tc, t=y, ps=s["pBC"], idn=s["identf"]:
                                  e.transpose(ps[:, c * 128:(c + 1) * 128], t[:, c, tc * 128:(tc + 1) * 128],
                                              idn[:, :]),
                                  deps=([norm_all[-1], pBC_free, ot_free] if c == 0 else [])))
            ote = p.step("vector", lambda e, t=s["ot"], ps=s["pBC"]: e.tensor_copy(t[:, :], ps[:, :]),
                         deps=[tts[-1]])
            pBC_free = ote
            ystored = p.dma("sync", lambda e, t=s["ot"], tb0=tb0, tc=tc,
                            v=d["y_out"][tb0 + tc * 128:tb0 + (tc + 1) * 128, :]:
                            e.dma_start(out=v, in_=t[:, :]), deps=[ote], chan="stf")
            ot_free = ystored
        y_lastread = tts[-1]
    p.wait_all("gpsimd", [ystored])
    return [ystored]


def emit_LN_nostore(p, s, sh, consts, r, deps_ready, pR1_free, pR2_free, pBC_free):
    pR1, pR2, pBC = s["pR1"], s["pR2"], s["pBC"]
    norm_all = []
    for tt in range(2):
        mm1 = None
        for oc in range(8):
            mm1 = p.step("tensor", lambda e, oc=oc, t=r, tt=tt, lc=sh["ln_cols"], ps=pR1:
                         e.matmul(ps[:, :], lc[:, 0:1], t[:, oc, tt * 512:(tt + 1) * 512],
                                  start=(oc == 0), stop=(oc == 7)),
                         deps=(deps_ready + [pR1_free] + consts if oc == 0 else []))
        mm2 = None
        for oc in range(8):
            sq_tok = p.step("vector", lambda e, t=s["sq"], r=r, oc=oc, tt=tt:
                            e.tensor_mul(t[:, :], r[:, oc, tt * 512:(tt + 1) * 512],
                                         r[:, oc, tt * 512:(tt + 1) * 512]),
                            deps=deps_ready + [mm2])
            mm2 = p.step("tensor", lambda e, oc=oc, t=s["sq"], ps=pR2, lc=sh["ln_cols"]:
                         e.matmul(ps[:, :], lc[:, 1:2], t[:, :],
                                  start=(oc == 0), stop=(oc == 7)),
                         deps=([sq_tok, pR2_free] if oc == 0 else [sq_tok]))
        rows = p.step("vector", [
            lambda e, t=s["rowA"], ps=pR1: e.tensor_copy(t[0:1, :], ps[0:1, :]),
            lambda e, a=s["rowA"], b=s["rowB"]:
            e.scalar_tensor_tensor(b[0:1, :], a[0:1, :], -1.0, a[0:1, :], ALU.mult, ALU.mult),
            lambda e, b=s["rowB"], ps=pR2: e.tensor_add(b[0:1, :], b[0:1, :], ps[0:1, :]),
        ], deps=[mm1, mm2, pR1_free, pR2_free])
        rstd = p.step("scalar", [
            lambda e, b=s["rowB"], c_=s["rowC"]: e.activation(c_[0:1, :], b[0:1, :], AF.Ln, bias=LN_EPS),
            lambda e, c_=s["rowC"]: e.activation(c_[0:1, :], c_[0:1, :], AF.Exp, scale=-0.5),
        ], deps=[rows])
        bc = p.step("tensor", [
            lambda e, ps=pBC, o=sh["ones_row"], a=s["rowA"]:
            e.matmul(ps[:, 0:512], o[0:1, :], a[0:1, :], start=True, stop=True),
            lambda e, ps=pBC, o=sh["ones_row"], c_=s["rowC"]:
            e.matmul(ps[:, 512:1024], o[0:1, :], c_[0:1, :], start=True, stop=True),
        ], deps=[rstd, rows, pBC_free])
        pR1_free = bc
        pR2_free = bc
        bce = p.step("vector", [
            lambda e, t=s["bcm"], ps=pBC: e.tensor_copy(t[:, :], ps[:, 0:512]),
            lambda e, t=s["bcr"], ps=pBC: e.tensor_copy(t[:, :], ps[:, 512:1024]),
        ], deps=[bc])
        pBC_free = bce
        for oc in range(8):
            nt1 = p.step("vector", lambda e, t=r, oc=oc, tt=tt, m=s["bcm"]:
                         e.tensor_add(t[:, oc, tt * 512:(tt + 1) * 512],
                                      t[:, oc, tt * 512:(tt + 1) * 512], m[:, :]),
                         deps=[bce, mm2])
            nt2 = p.step("vector", lambda e, t=r, oc=oc, tt=tt, m=s["bcr"]:
                         e.tensor_mul(t[:, oc, tt * 512:(tt + 1) * 512],
                                      t[:, oc, tt * 512:(tt + 1) * 512], m[:, :]),
                         deps=[nt1])
            norm_all.append(nt2)
    return None, None, pR1_free, pR2_free, pBC_free, norm_all


from contextlib import ExitStack
from concourse.bass_utils import run_bass_kernel_spmd

RG = [[0, 1, 2, 3], [4, 5, 6, 7]]


def build_fused():
    nc = bass.Bass("TRN2", target_bir_lowering=False, debug=False)
    d = {}
    EIN = [
        ("x0_fm", [1024, S], F32), ("x0q_fm", [1024, TPC], F32),
        ("WqT", [1024, 1024], F32), ("WkT", [1024, 1024], F32),
        ("WvT", [1024, 1024], F32), ("WoT", [1024, 1024], F32),
        ("Wg0T", [1024, 8], F32), ("W1T0", [8, 1024, 2816], F32),
        ("W3T0", [8, 1024, 2816], F32), ("W2T0", [8, 2816, 1024], F32),
        ("qk_w", [1024, 2048], F32), ("qk_b", [1, 2048], F32),
        ("lepe_w", [3, 1024, 1024], F32), ("lepe_b", [128, 8], F32),
        ("theta", [1, 512], F32), ("pos_rows", [32, 128], F32), ("wlr", [8, 2], F32),
        ("Wg1", [1024, 8], F32), ("bg1", [1, 8], F32),
        ("W1_1", [8, 1024, 4096], F32), ("b1_1", [8, 128, 32], F32),
        ("W2_1", [8, 4096, 1024], F32), ("b2_1", [8, 1024], F32),
        ("identb", [128, 128], BF16), ("identf", [128, 128], F32),
        ("ones_row", [1, 128], F32), ("ones_colb", [128, 1], BF16),
        ("ln_cols", [128, 2], F32), ("actconsts", [128, 1], F32),
    ]
    INT = [
        ("KT_d", [1024, S], BF16), ("V_d", [S, 1024], BF16), ("QT_d", [1024, TPC], BF16),
        ("x1_fm", [1024, TPC], F32), ("x1b_fm", [1024, TPC], BF16),
        ("x2_fm", [1024, TPC], F32), ("x2b_fm", [1024, TPC], BF16),
        ("x3_fm", [1024, TPC], F32), ("x3b_fm", [1024, TPC], BF16),
        ("cc_kv_in", [128, 1024], F32), ("cc_kv_out", [128, 1024], F32),
        ("cc_ks_in", [1, 1024], F32), ("cc_ks_out", [1, 1024], F32),
        ("ag_in", [2, 1024], F32), ("ag_out", [8, 1024], F32),
        ("halo_l", [1, 1024], F32), ("halo_r", [1, 1024], F32),
    ]
    for name, shape, dt in EIN:
        d[name] = nc.dram_tensor(name, shape, dt, kind="ExternalInput").ap()
    for name, shape, dt in INT:
        d[name] = nc.dram_tensor(name, shape, dt, kind="Internal").ap()
    d["y_out"] = nc.dram_tensor("y_out", [TPC, 1024], F32, kind="ExternalOutput").ap()

    with ExitStack() as stack:
        p = Ctx(nc)
        sh = {}
        sh["ones_row"] = sbuf(nc, stack, "sh_ones_row", [1, 128], F32)
        sh["ln_cols"] = sbuf(nc, stack, "sh_ln_cols", [128, 2], F32)
        sh["actc"] = sbuf(nc, stack, "sh_actc", [128, 1], F32)
        sh["dummy"] = sbuf(nc, stack, "sh_dummy", [1, 4], F32)
        nc.const_aps.register(F32, LN_EPS, sh["actc"][:, 0:1])
        consts = [
            p.dma("sync", lambda e: e.dma_start(out=sh["ones_row"][:, :], in_=d["ones_row"])),
            p.dma("sync", lambda e: e.dma_start(out=sh["ln_cols"][:, :], in_=d["ln_cols"])),
            p.dma("sync", lambda e: e.dma_start(out=sh["actc"][:, :], in_=d["actconsts"])),
        ]
        with ExitStack() as stA:
            shA = alloc_A_shared(nc, stA)
            shA.update(sh)
            consA = consts + [p.dma("sync", lambda e: e.dma_start(out=shA["identb"][:, :], in_=d["identb"]))]
            with ExitStack() as st1:
                s1 = alloc_A1(nc, st1)
                a1_end = emit_A1(p, d, shA, s1)
            s2 = alloc_A2(nc, stA)
            emit_A2(p, d, shA, s2, a1_end, consA)
        p.barrier(sh["dummy"])
        with ExitStack() as stB:
            sB = alloc_B(nc, stB)
            emit_B(p, d, sB, sh, consts)
        p.barrier(sh["dummy"])
        with ExitStack() as stC:
            sC = alloc_C(nc, stC)
            emit_C(p, d, sC, sh, consts, RG)
        p.barrier(sh["dummy"])
        with ExitStack() as stD:
            sD = alloc_D(nc, stD)
            emit_D(p, d, sD, sh, consts)
        p.finalize()
    return nc


def kernel(**inputs):
    prep = host_prep(inputs)
    nc = build_fused()
    maps = []
    for core in range(8):
        b, qq = core // 4, core % 4
        wlr = np.zeros((8, 2), np.float32)
        if qq > 0:
            wlr[2 * (qq - 1) + 1, 0] = 1.0
        if qq < 3:
            wlr[2 * (qq + 1), 1] = 1.0
        maps.append({
            "x0_fm": prep["x0_fm"][b],
            "x0q_fm": np.ascontiguousarray(prep["x0_fm"][b][:, qq * TPC:(qq + 1) * TPC]),
            "WqT": prep["WqT"], "WkT": prep["WkT"], "WvT": prep["WvT"], "WoT": prep["WoT"],
            "Wg0T": prep["Wg0T"], "W1T0": prep["W1T0"], "W3T0": prep["W3T0"], "W2T0": prep["W2T0"],
            "qk_w": prep["qk_w"], "qk_b": prep["qk_b"],
            "lepe_w": prep["lepe_w"], "lepe_b": prep["lepe_b"],
            "theta": prep["theta"], "pos_rows": prep["pos_rows"][qq], "wlr": wlr,
            "Wg1": prep["Wg1"], "bg1": prep["bg1"],
            "W1_1": prep["W1_1"], "b1_1": prep["b1_1"], "W2_1": prep["W2_1"], "b2_1": prep["b2_1"],
            "identb": prep["identb"], "identf": prep["identf"],
            "ones_row": prep["ones_row"], "ones_colb": prep["ones_colb"],
            "ln_cols": prep["ln_cols"], "actconsts": prep["actconsts"],
        })
    res = run_bass_kernel_spmd(nc, maps, core_ids=list(range(8)))
    out = np.empty((2, S, 1024), np.float32)
    for core in range(8):
        b, qq = core // 4, core % 4
        out[b, qq * TPC:(qq + 1) * TPC, :] = res.results[core]["y_out"]
    return out


# version 10
# speedup vs baseline: 1.2539x; 1.0036x over previous
import numpy as np
import concourse.bass as bass
import concourse.mybir as mybir

F32 = mybir.dt.float32
BF16 = mybir.dt.bfloat16
AF = mybir.ActivationFunctionType
ALU = mybir.AluOpType
AX = mybir.AxisListType

D = 1024
S = 16384
TPC = 4096
NEG_BIG = -1.0e30
MAGIC = float(1.5 * 2.0 ** 23)
_TP = 2.0 * np.pi
TWO_PI_HI = float(np.float32(_TP))
TWO_PI_LO = float(_TP - np.float64(np.float32(_TP)))
INV_2PI = float(np.float32(1.0 / _TP))
HALF_PI = float(np.float32(np.pi / 2.0))
LN_EPS = 1e-5

ENGS = ("sync", "vector", "scalar", "tensor", "gpsimd")


def fm(dram_ap):
    return dram_ap.rearrange("(c p) t -> p c t", p=128)


class Ctx:

    SERIALIZE = False
    SER_REGION = ""

    def __init__(self, nc):
        self.region = ""
        self.nc = nc
        self.q = {k: [] for k in ENGS}
        self.counts = {k: 0 for k in ENGS}
        self.waited = {a: {b: 0 for b in ENGS} for a in ENGS}
        self._cms = []
        self._sem_waited = set()
        self._ccn = 0
        self._dmn = 0
        self.chans = {}
        self.chan_waited = {}
        self.sems = {}
        for e in ENGS:
            cm = nc.semaphore("done_" + e)
            self.sems[e] = cm.__enter__()
            self._cms.append(cm)

    def _chan(self, name):
        if name not in self.chans:
            cm = self.nc.semaphore("ch_" + name)
            h = cm.__enter__()
            self._cms.append(cm)
            self.chans[name] = [h, 0]
        return self.chans[name]

    def _dep(self, eng, deps):
        if self.SERIALIZE or (self.SER_REGION and self.region == self.SER_REGION):
            deps = list(deps) + [(e2, self.counts[e2]) for e2 in ENGS if e2 != eng]
            deps += [("#chan", nm, hc[1]) for nm, hc in self.chans.items()]
        for tok in deps:
            if tok is None:
                continue
            if tok[0] == "#sem":
                key = (eng, id(tok[1]))
                if key not in self._sem_waited:
                    self.q[eng].append(lambda e, h=tok[1]: e.wait_ge(h, 1))
                    self._sem_waited.add(key)
                continue
            if tok[0] == "#chan":
                _, nm, c = tok
                key = (eng, nm)
                if self.chan_waited.get(key, 0) < c:
                    h = self.chans[nm][0]
                    self.q[eng].append(lambda e, h=h, c=c: e.wait_ge(h, c))
                    self.chan_waited[key] = c
                continue
            e2, c = tok
            if self.waited[eng][e2] < c:
                h = self.sems[e2]
                self.q[eng].append(lambda e, h=h, c=c: e.wait_ge(h, c))
                self.waited[eng][e2] = c

    def step(self, eng, fns, deps=()):
        if callable(fns):
            fns = [fns]
        h = self.sems[eng]
        tok = None
        for i, f in enumerate(fns):
            self._dep(eng, (deps if i == 0 else [tok]))
            self.counts[eng] += 1
            c = self.counts[eng]
            self.q[eng].append(lambda e, f=f, h=h: f(e).then_inc(h, 1))
            tok = (eng, c)
        return tok

    def dma(self, eng, fn, deps=(), chan=None):
        self._dep(eng, deps)
        if chan is None:
            chan = f"_dma{self._dmn}"
            self._dmn += 1
        hc = self._chan(chan)
        hc[1] += 16
        c = hc[1]
        self.q[eng].append(lambda e, f=fn, h=hc[0]: f(e).then_inc(h, 16))
        return ("#chan", chan, c)

    def cc(self, fn, deps=()):
        self._dep("gpsimd", deps)
        cm = self.nc.semaphore(f"ccsem{self._ccn}")
        self._ccn += 1
        h = cm.__enter__()
        self._cms.append(cm)
        self.q["gpsimd"].append(lambda e, f=fn, h=h: f(e).then_inc(h))
        return ("#sem", h)

    def wait_all(self, eng, deps):
        self._dep(eng, deps)

    def barrier(self, dummy):
        deps = [(e2, self.counts[e2]) for e2 in ENGS if e2 != "gpsimd"]
        deps += [("#chan", nm, hc[1]) for nm, hc in self.chans.items()]
        self._dep("gpsimd", deps)
        mark = self.step("gpsimd", lambda e, t=dummy: e.memset(t[0:1, :], 0.0))
        for eng in ENGS:
            if eng != "gpsimd":
                self._dep(eng, [mark])
        return mark

    def tok(self, eng):
        return (eng, self.counts[eng])

    def finalize(self):
        nc = self.nc
        with nc.Block() as block:
            @block.sync
            def _(e):
                for f in self.q["sync"]:
                    f(e)

            @block.vector
            def _(e):
                for f in self.q["vector"]:
                    f(e)

            @block.scalar
            def _(e):
                for f in self.q["scalar"]:
                    f(e)

            @block.tensor
            def _(e):
                for f in self.q["tensor"]:
                    f(e)

            @block.gpsimd
            def _(e):
                for f in self.q["gpsimd"]:
                    f(e)
        for cm in reversed(self._cms):
            cm.__exit__(None, None, None)


def host_prep(inputs):
    import ml_dtypes
    P = {}
    Wqkv = np.asarray(inputs["L0_attn_W_qkv"])
    P["WqT"] = np.ascontiguousarray(Wqkv[0::3, :].T)
    P["WkT"] = np.ascontiguousarray(Wqkv[1::3, :].T)
    P["WvT"] = np.ascontiguousarray(Wqkv[2::3, :].T)
    P["WoT"] = np.ascontiguousarray(np.asarray(inputs["L0_attn_W_out"]).T)
    P["Wg0T"] = np.ascontiguousarray(np.asarray(inputs["L0_ffn_Wg"]).T)
    P["W1T0"] = np.ascontiguousarray(np.asarray(inputs["L0_ffn_W1"]).transpose(0, 2, 1))
    P["W3T0"] = np.ascontiguousarray(np.asarray(inputs["L0_ffn_W3"]).transpose(0, 2, 1))
    P["W2T0"] = np.ascontiguousarray(np.asarray(inputs["L0_ffn_W2"]).transpose(0, 2, 1))
    P["qk_w"] = np.ascontiguousarray(np.asarray(inputs["L1_attn_qk_w"]))
    P["qk_b"] = np.ascontiguousarray(np.asarray(inputs["L1_attn_qk_b"]).reshape(1, 2048))
    P["lepe_w"] = np.ascontiguousarray(np.asarray(inputs["L1_attn_lepe_w"]))
    P["lepe_b"] = np.ascontiguousarray(np.asarray(inputs["L1_attn_lepe_b"]).reshape(8, 128).T)
    P["Wg1"] = np.ascontiguousarray(np.asarray(inputs["L1_ffn_Wg"]))
    P["bg1"] = np.ascontiguousarray(np.asarray(inputs["L1_ffn_bg"]).reshape(1, 8))
    P["W1_1"] = np.ascontiguousarray(np.asarray(inputs["L1_ffn_W1"]))
    P["b1_1"] = np.ascontiguousarray(np.asarray(inputs["L1_ffn_b1"]).reshape(8, 32, 128).transpose(0, 2, 1))
    P["W2_1"] = np.ascontiguousarray(np.asarray(inputs["L1_ffn_W2"]))
    P["b2_1"] = np.ascontiguousarray(np.asarray(inputs["L1_ffn_b2"]))
    P["identb"] = np.eye(128, dtype=ml_dtypes.bfloat16)
    P["identf"] = np.eye(128, dtype=np.float32)
    P["ones_row"] = np.ones((1, 128), np.float32)
    P["ones_colb"] = np.ones((128, 1), ml_dtypes.bfloat16)
    ln = np.zeros((128, 2), np.float32)
    ln[:, 0] = -1.0 / 1024.0
    ln[:, 1] = 1.0 / 1024.0
    P["ln_cols"] = ln
    P["actconsts"] = np.full((128, 1), LN_EPS, np.float32)
    P["theta"] = (1.0 / 10000.0 ** (np.arange(512, dtype=np.float64) / 512.0)).astype(np.float32).reshape(1, 512)
    x = np.asarray(inputs["L0_attn_x"])
    P["x0_fm"] = [np.ascontiguousarray(x[b].T) for b in range(2)]
    P["pos_rows"] = [np.arange(qq * TPC, (qq + 1) * TPC, dtype=np.float32).reshape(32, 128) for qq in range(4)]
    return P


def sbuf(nc, stack, name, shape, dt):
    return stack.enter_context(nc.sbuf_tensor("sb_" + name, shape, dt))


def psum(nc, stack, name, shape, dt=F32):
    return stack.enter_context(nc.psum_tensor("ps_" + name, shape, dt))


def alloc_A_shared(nc, stack):
    s = {}
    s["identb"] = sbuf(nc, stack, "identb", [128, 128], BF16)
    s["sc"] = psum(nc, stack, "sc", [128, 1024])
    s["ptp"] = psum(nc, stack, "ptp", [128, 1024], BF16)
    s["pAV"] = psum(nc, stack, "pAV", [128, 1024])
    s["pR1"] = psum(nc, stack, "pR1", [1, 512])
    s["pR2"] = psum(nc, stack, "pR2", [1, 512])
    return s


def alloc_A1(nc, stack):
    s = {}
    for n in ("wq", "wk", "wv"):
        s[n] = sbuf(nc, stack, n, [128, 8, 1024], BF16)
    s["xb"] = [sbuf(nc, stack, f"xb{i}", [128, 8, 512], BF16) for i in range(2)]
    s["ktev"] = [sbuf(nc, stack, f"ktev{i}", [128, 8, 512], BF16) for i in range(2)]
    s["vev"] = [sbuf(nc, stack, f"vev{i}", [128, 4, 1024], BF16) for i in range(2)]
    s["qev"] = [sbuf(nc, stack, f"qev{i}", [128, 8, 512], BF16) for i in range(2)]
    return s


def alloc_A2(nc, stack):
    s = {}
    s["wo"] = sbuf(nc, stack, "wo", [128, 8, 1024], BF16)
    s["qt"] = sbuf(nc, stack, "qt", [128, 8, 1024], BF16)
    s["O"] = sbuf(nc, stack, "O", [128, 8, 1024], F32)
    s["OT"] = sbuf(nc, stack, "OT", [128, 8, 1024], BF16)
    s["ktile"] = [sbuf(nc, stack, f"ktile{i}", [128, 8, 512], BF16) for i in range(2)]
    s["vtile"] = [sbuf(nc, stack, f"vtile{i}", [128, 4, 1024], BF16) for i in range(2)]
    s["P"] = [sbuf(nc, stack, f"P{i}", [128, 512], BF16) for i in range(2)]
    s["PT"] = sbuf(nc, stack, "PT", [128, 512], BF16)
    s["stats"] = sbuf(nc, stack, "stats", [128, 32], F32)
    s["x0c"] = sbuf(nc, stack, "x0c", [128, 8, 512], F32)
    s["r1"] = sbuf(nc, stack, "r1", [128, 8, 1024], F32)
    s["sq"] = sbuf(nc, stack, "sq", [128, 512], F32)
    s["rowA"] = sbuf(nc, stack, "rowA", [1, 512], F32)
    s["rowB"] = sbuf(nc, stack, "rowB", [1, 512], F32)
    s["rowC"] = sbuf(nc, stack, "rowC", [1, 512], F32)
    s["bcm"] = sbuf(nc, stack, "bcm", [128, 512], F32)
    s["bcr"] = sbuf(nc, stack, "bcr", [128, 512], F32)
    return s


def emit_A1(p, d, sh, s):
    wtok = {}
    for n, src in (("wq", d["WqT"]), ("wk", d["WkT"]), ("wv", d["WvT"])):
        wtok[n] = p.dma("gpsimd", lambda e, t=s[n], v=fm(src): e.dma_start(out=t[:, :, :], in_=v), chan=n)
    NT = S // 512
    xload = [None, None]
    xlast = [None, None]
    stored = {"ktev": [None, None], "vev": [None, None], "qev": [None, None]}
    pslot = [(sh["pAV"], slice(0, 512)), (sh["pAV"], slice(512, 1024))]
    ps_evtok = [None, None]
    gidx = 0
    end_toks = []

    for it in range(NT + TPC // 512):
        buf = it % 2
        kv_part = it < NT
        t0 = (it * 512) if kv_part else ((it - NT) * 512)
        srcname = "x0_fm" if kv_part else "x0q_fm"
        xload[buf] = p.dma("gpsimd",
                           lambda e, t=s["xb"][buf], v=fm(d[srcname])[:, :, t0:t0 + 512]:
                           e.dma_start(out=t[:, :, :], in_=v),
                           deps=[xlast[buf]], chan=f"xb{buf}")

        def proj(wname, ev_name):
            nonlocal gidx
            evtoks = []
            mmtok = None
            for oc in range(8):
                sl = gidx % 2
                ps, psl = pslot[sl]
                for c in range(8):
                    mmtok = p.step("tensor",
                                   lambda e, ps=ps, psl=psl, w=s[wname], c=c, oc=oc, x=s["xb"][buf]:
                                   e.matmul(ps[:, psl], w[:, c, oc * 128:(oc + 1) * 128], x[:, c, :],
                                            start=(c == 0), stop=(c == 7)),
                                   deps=([xload[buf], wtok[wname], ps_evtok[sl]] if c == 0 else []))
                evt = p.step("scalar",
                             lambda e, t=s[ev_name][buf], oc=oc, ps=ps, psl=psl: e.copy(t[:, oc, :], ps[:, psl]),
                             deps=[mmtok, stored[ev_name][buf]])
                ps_evtok[sl] = evt
                evtoks.append(evt)
                gidx += 1
            return mmtok, evtoks

        if kv_part:
            mmk, evk = proj("wk", "ktev")
            stored["ktev"][buf] = p.dma("sync",
                                        lambda e, t=s["ktev"][buf], v=fm(d["KT_d"])[:, :, t0:t0 + 512]:
                                        e.dma_start(out=v, in_=t[:, :, :]),
                                        deps=evk, chan=f"ktev{buf}")
            evv = []
            mmv = None
            for tc in range(4):
                for dh in range(2):
                    sl = gidx % 2
                    ps, psl = pslot[sl]
                    for c in range(8):
                        mmv = p.step("tensor",
                                     lambda e, ps=ps, psl=psl, x=s["xb"][buf], c=c, tc=tc, w=s["wv"], dh=dh:
                                     e.matmul(ps[:, psl], x[:, c, tc * 128:(tc + 1) * 128],
                                              w[:, c, dh * 512:(dh + 1) * 512],
                                              start=(c == 0), stop=(c == 7)),
                                     deps=([xload[buf], wtok["wv"], ps_evtok[sl]] if c == 0 else []))
                    evt = p.step("vector",
                                 lambda e, t=s["vev"][buf], tc=tc, dh=dh, ps=ps, psl=psl:
                                 e.tensor_copy(t[:, tc, dh * 512:(dh + 1) * 512], ps[:, psl]),
                                 deps=[mmv, stored["vev"][buf]])
                    ps_evtok[sl] = evt
                    evv.append(evt)
                    gidx += 1
            stored["vev"][buf] = p.dma("sync",
                                       lambda e, t=s["vev"][buf],
                                       v=d["V_d"][t0:t0 + 512, :].rearrange("(a p) dd -> p a dd", p=128):
                                       e.dma_start(out=v, in_=t[:, :, :]),
                                       deps=evv, chan=f"vev{buf}")
            xlast[buf] = mmv
        else:
            mmq, evq = proj("wq", "qev")
            stored["qev"][buf] = p.dma("sync",
                                       lambda e, t=s["qev"][buf],
                                       v=fm(d["QT_d"])[:, :, t0:t0 + 512]:
                                       e.dma_start(out=v, in_=t[:, :, :]),
                                       deps=evq, chan=f"qev{buf}")
            xlast[buf] = mmq
    end_toks = [xlast[0], xlast[1], ps_evtok[0], ps_evtok[1],
                stored["ktev"][0], stored["ktev"][1], stored["vev"][0], stored["vev"][1],
                stored["qev"][0], stored["qev"][1], xload[0], xload[1]]
    return [t for t in end_toks if t is not None]


def emit_A2(p, d, sh, s, a1_end, consts):
    for eng in ENGS:
        p.wait_all(eng, a1_end)
    wo_tok = p.dma("gpsimd", lambda e, t=s["wo"], v=fm(d["WoT"]): e.dma_start(out=t[:, :, :], in_=v))
    consts_tok = None

    QB = 1024
    NQB = TPC // QB
    KTS = 512
    NKT = S // KTS
    qt, O, OT = s["qt"], s["O"], s["OT"]
    Psb, PTsb, stats = s["P"], s["PT"], s["stats"]
    sc, ptp, pAV = sh["sc"], sh["ptp"], sh["pAV"]
    pR1, pR2 = sh["pR1"], sh["pR2"]
    r1, x0c, sq = s["r1"], s["x0c"], s["sq"]

    ktlast = [None, None]
    ktlast_v = [None, None]
    qt_lastuse = None
    x0c_lastuse = None
    r1_stored = None
    x1b_stored = None
    pAB_free = []
    sc_free = [None, None]
    albl = [None, None]
    av_last = None
    ptp_free = None
    pR1_free = None
    pR2_free = None
    Psb_free = [None, None]
    o_upd = None
    mm = None

    for qb in range(NQB):
        qb0 = qb * QB
        ldq = p.dma("sync", lambda e, t=qt, v=fm(d["QT_d"])[:, :, qb0:qb0 + QB]:
                    e.dma_start(out=t[:, :, :], in_=v),
                    deps=[qt_lastuse], chan="qt")
        ini = p.step("vector", [
            lambda e, t=stats: e.memset(t[:, 0:8], -NEG_BIG),
            lambda e, t=stats: e.memset(t[:, 8:16], 0.0),
            lambda e, t=O: e.memset(t[:, :, :], 0.0),
        ], deps=sc_free + albl)

        vtlast = ktlast_v
        p1 = None
        p2 = None
        ldk = ldv = None
        for it in range(NKT * 8):
            kt, qtc = divmod(it, 8)
            kb = kt % 2
            if qtc == 0:
                k0 = kt * KTS
                ldk = p.dma("sync", lambda e, t=s["ktile"][kb], v=fm(d["KT_d"])[:, :, k0:k0 + KTS]:
                            e.dma_start(out=t[:, :, :], in_=v),
                            deps=[ktlast[kb]], chan=f"ktile{kb}")
                ldv = p.dma("sync", lambda e, t=s["vtile"][kb],
                            v=d["V_d"][k0:k0 + KTS, :].rearrange("(a p) dd -> p a dd", p=128):
                            e.dma_start(out=t[:, :, :], in_=v),
                            deps=[vtlast[kb]], chan=f"vtile{kb}")
            half = qtc % 2
            scsl = slice(half * 512, (half + 1) * 512)
            mm = None
            for c in range(8):
                dep0 = ([ldk, ldq, ini, sc_free[half]] + consts) if c == 0 else []
                mm = p.step("tensor", lambda e, w=qt, c=c, q=qtc, k=s["ktile"][kb], ps=sc, scsl=scsl:
                            e.matmul(ps[:, scsl], w[:, c, q * 128:(q + 1) * 128], k[:, c, :],
                                     start=(c == 0), stop=(c == 7)), deps=dep0)
            if qtc == 7:
                ktlast[kb] = mm
            h4 = half * 4
            st1 = p.step("vector", [
                lambda e, t=stats, ps=sc, scsl=scsl, h4=h4: e.reduce_max(t[:, 16 + h4:17 + h4], ps[:, scsl], axis=AX.X),
                lambda e, t=stats, q=qtc, h4=h4: e.tensor_scalar(t[:, 17 + h4:18 + h4], t[:, 16 + h4:17 + h4],
                                                                 -1.0 / 32.0, t[:, q:q + 1], ALU.mult, ALU.subtract),
                lambda e, t=stats, q=qtc, h4=h4: e.tensor_scalar(t[:, 19 + h4:20 + h4], t[:, 17 + h4:18 + h4],
                                                                 0.0, None, ALU.min),
                lambda e, t=stats, q=qtc, h4=h4: e.tensor_scalar(t[:, q:q + 1], t[:, q:q + 1],
                                                                 t[:, 19 + h4:20 + h4], None, ALU.add),
            ], deps=[mm, albl[half]])
            q4 = qtc % 4
            al = p.step("scalar", lambda e, t=stats, q4=q4, h4=h4:
                        e.activation(t[:, 24 + q4:25 + q4], t[:, 19 + h4:20 + h4], AF.Exp),
                        deps=[st1])
            p_tok = p.step("scalar", lambda e, t=Psb[half], ps=sc, st=stats, q=qtc, scsl=scsl, q4=q4:
                           e.activation(t[:, :], ps[:, scsl], AF.Exp, bias=st[:, q:q + 1], scale=1.0 / 32.0,
                                        accum_out=st[:, 28 + q4:29 + q4]),
                           deps=[Psb_free[half], st1])
            sc_free[half] = p_tok
            lup = p.step("vector", lambda e, t=stats, q=qtc, q4=q4:
                         e.scalar_tensor_tensor(t[:, 8 + q:9 + q], t[:, 8 + q:9 + q],
                                                t[:, 24 + q4:25 + q4],
                                                t[:, 28 + q4:29 + q4], ALU.mult, ALU.add),
                         deps=[al, p_tok])
            albl[half] = lup
            cx = {"q": qtc, "q4": q4, "half": half, "mm": mm, "p_tok": p_tok, "lup": lup,
                  "kb": kb, "ldv": ldv, "last_of_kt": (qtc == 7)}
            if p2 is not None:
                av = None
                for kc in range(4):
                    dep0 = ([p2["pte"], p2["ldv"]] + pAB_free) if kc == 0 else []
                    p.step("tensor", lambda e, kc=kc, t=PTsb, v=s["vtile"][p2["kb"]], ps=pAV:
                           e.matmul(ps[:, 0:512], t[:, kc * 128:(kc + 1) * 128], v[:, kc, 0:512],
                                    start=(kc == 0), stop=(kc == 3)), deps=dep0)
                    av = p.step("tensor", lambda e, kc=kc, t=PTsb, v=s["vtile"][p2["kb"]], ps=pAV:
                                e.matmul(ps[:, 512:1024], t[:, kc * 128:(kc + 1) * 128], v[:, kc, 512:1024],
                                         start=(kc == 0), stop=(kc == 3)), deps=[])
                p2["av"] = av
                av_last = av
                if p2["last_of_kt"]:
                    vtlast[p2["kb"]] = av
            if p1 is not None:
                tt_tok = None
                for i in range(4):
                    tt_tok = p.step("tensor", lambda e, i=i, t=Psb[p1["half"]], ps=ptp, idn=sh["identb"]:
                                    e.transpose(ps[:, i * 128:(i + 1) * 128], t[:, i * 128:(i + 1) * 128],
                                                idn[:, :]),
                                    deps=([p1["p_tok"], ptp_free] if i == 0 else []))
                Psb_free[p1["half"]] = tt_tok
                p1["tt"] = tt_tok
                pte = p.step("vector", lambda e, t=PTsb, ps=ptp: e.tensor_copy(t[:, :], ps[:, 0:512]),
                             deps=[tt_tok, (p2["av"] if p2 is not None else None)])
                ptp_free = pte
                p1["pte"] = pte
            if p2 is not None:
                o_upd = p.step("vector", lambda e, t=O, q=p2["q"], st=stats, ps=pAV, q4=p2["q4"]:
                               e.scalar_tensor_tensor(t[:, q, :], t[:, q, :], st[:, 24 + q4:25 + q4],
                                                      ps[:, :], ALU.mult, ALU.add),
                               deps=[p2["av"], p2["lup"]])
                pAB_free = [o_upd]
            p2 = p1
            p1 = cx
        for drain in range(2):
            if p2 is not None:
                av = None
                for kc in range(4):
                    dep0 = ([p2["pte"], p2["ldv"]] + pAB_free) if kc == 0 else []
                    p.step("tensor", lambda e, kc=kc, t=PTsb, v=s["vtile"][p2["kb"]], ps=pAV:
                           e.matmul(ps[:, 0:512], t[:, kc * 128:(kc + 1) * 128], v[:, kc, 0:512],
                                    start=(kc == 0), stop=(kc == 3)), deps=dep0)
                    av = p.step("tensor", lambda e, kc=kc, t=PTsb, v=s["vtile"][p2["kb"]], ps=pAV:
                                e.matmul(ps[:, 512:1024], t[:, kc * 128:(kc + 1) * 128], v[:, kc, 512:1024],
                                         start=(kc == 0), stop=(kc == 3)), deps=[])
                p2["av"] = av
                av_last = av
                if p2["last_of_kt"]:
                    vtlast[p2["kb"]] = av
            if p1 is not None and "tt" not in p1:
                tt_tok = None
                for i in range(4):
                    tt_tok = p.step("tensor", lambda e, i=i, t=Psb[p1["half"]], ps=ptp, idn=sh["identb"]:
                                    e.transpose(ps[:, i * 128:(i + 1) * 128], t[:, i * 128:(i + 1) * 128],
                                                idn[:, :]),
                                    deps=([p1["p_tok"], ptp_free] if i == 0 else []))
                Psb_free[p1["half"]] = tt_tok
                p1["tt"] = tt_tok
                pte = p.step("vector", lambda e, t=PTsb, ps=ptp: e.tensor_copy(t[:, :], ps[:, 0:512]),
                             deps=[tt_tok, (p2["av"] if p2 is not None else None)])
                ptp_free = pte
                p1["pte"] = pte
            if p2 is not None:
                o_upd = p.step("vector", lambda e, t=O, q=p2["q"], st=stats, ps=pAV, q4=p2["q4"]:
                               e.scalar_tensor_tensor(t[:, q, :], t[:, q, :], st[:, 24 + q4:25 + q4],
                                                      ps[:, :], ALU.mult, ALU.add),
                               deps=[p2["av"], p2["lup"]])
                pAB_free = [o_upd]
            p2 = p1
            p1 = None
        qt_lastuse = mm

        ot_evac = None
        for qtc in range(8):
            for half in range(2):
                ob = p.step("vector", ([
                    lambda e, t=stats, q=qtc: e.reciprocal(t[:, 16:17], t[:, 8 + q:9 + q]),
                ] if half == 0 else []) + [
                    lambda e, t=O, st=stats, q=qtc, h=half:
                    e.tensor_scalar(t[:, q, h * 512:(h + 1) * 512], t[:, q, h * 512:(h + 1) * 512],
                                    st[:, 16:17], None, ALU.mult),
                    lambda e, t=Psb[0], o=O, q=qtc, h=half: e.tensor_copy(t[:, :], o[:, q, h * 512:(h + 1) * 512]),
                ], deps=[o_upd] + Psb_free)
                tt_tok = None
                for c in range(4):
                    tt_tok = p.step("tensor", lambda e, c=c, t=Psb[0], ps=ptp, idn=sh["identb"]:
                                    e.transpose(ps[:, c * 128:(c + 1) * 128], t[:, c * 128:(c + 1) * 128],
                                                idn[:, :]),
                                    deps=([ob, ptp_free] if c == 0 else []))
                Psb_free = [tt_tok, tt_tok]
                ot_evac = p.step("vector", [
                    (lambda e, c=c, q=qtc, h=half, t=OT, ps=ptp:
                     e.tensor_copy(t[:, h * 4 + c, q * 128:(q + 1) * 128], ps[:, c * 128:(c + 1) * 128]))
                    for c in range(4)
                ], deps=[tt_tok])
                ptp_free = ot_evac

        if qb == 0 and "dbgO" in d:
            p.dma("sync", lambda e, t=O: e.dma_start(out=d["dbgO"], in_=t[:, :, :].rearrange("p c t -> p (c t)")),
                  deps=[ot_evac, o_upd])
            p.dma("sync", lambda e, t=OT: e.dma_start(out=d["dbgOT"], in_=t[:, :, :].rearrange("p c t -> p (c t)")),
                  deps=[ot_evac])
            p.dma("sync", lambda e, t=stats: e.dma_start(out=d["dbgS"], in_=t[:, :]), deps=[ot_evac])
        for tt in range(2):
            ldx = p.dma("sync", lambda e, t=x0c, v=fm(d["x0q_fm"])[:, :, qb0 + tt * 512:qb0 + tt * 512 + 512]:
                        e.dma_start(out=t[:, :, :], in_=v), deps=[x0c_lastuse], chan="x0c")
            radd_toks = []
            for oc in range(8):
                psl2 = slice(0, 512) if oc % 2 == 0 else slice(512, 1024)
                for c in range(8):
                    dep0 = ([ot_evac, wo_tok] + pAB_free) if c == 0 else []
                    mm = p.step("tensor", lambda e, ps=pAV, psl2=psl2, w=s["wo"], c=c, oc=oc, t=OT, tt=tt:
                                e.matmul(ps[:, psl2], w[:, c, oc * 128:(oc + 1) * 128],
                                         t[:, c, tt * 512:(tt + 1) * 512],
                                         start=(c == 0), stop=(c == 7)), deps=dep0)
                ra = p.step("vector", lambda e, t=r1, oc=oc, tt=tt, x=x0c, ps=pAV, psl2=psl2:
                            e.tensor_add(t[:, oc, tt * 512:(tt + 1) * 512], ps[:, psl2], x[:, oc, :]),
                            deps=[mm, ldx, r1_stored, x1b_stored])
                pAB_free = ([ra] if oc == 0 else pAB_free + [ra])
                radd_toks.append(ra)
            ot_lastread = mm
            mm1 = None
            for oc in range(8):
                mm1 = p.step("tensor", lambda e, oc=oc, t=r1, tt=tt, lc=sh["ln_cols"], ps=pR1:
                             e.matmul(ps[:, :], lc[:, 0:1], t[:, oc, tt * 512:(tt + 1) * 512],
                                      start=(oc == 0), stop=(oc == 7)),
                             deps=((radd_toks + [pR1_free]) if oc == 0 else []))
            mm2 = None
            for oc in range(8):
                sq_tok = p.step("vector", lambda e, t=sq, r=r1, oc=oc, tt=tt:
                                e.tensor_mul(t[:, :], r[:, oc, tt * 512:(tt + 1) * 512],
                                             r[:, oc, tt * 512:(tt + 1) * 512]),
                                deps=[radd_toks[oc], mm2])
                mm2 = p.step("tensor", lambda e, oc=oc, t=sq, ps=pR2, lc=sh["ln_cols"]:
                             e.matmul(ps[:, :], lc[:, 1:2], t[:, :],
                                      start=(oc == 0), stop=(oc == 7)),
                             deps=([sq_tok, pR2_free] if oc == 0 else [sq_tok]))
            rows = p.step("vector", [
                lambda e, t=s["rowA"], ps=pR1: e.tensor_copy(t[0:1, :], ps[0:1, :]),
                lambda e, a=s["rowA"], b=s["rowB"]:
                e.scalar_tensor_tensor(b[0:1, :], a[0:1, :], -1.0, a[0:1, :], ALU.mult, ALU.mult),
                lambda e, b=s["rowB"], ps=pR2: e.tensor_add(b[0:1, :], b[0:1, :], ps[0:1, :]),
            ], deps=[mm1, mm2, pR1_free, pR2_free])
            rstd = p.step("scalar", [
                lambda e, b=s["rowB"], c_=s["rowC"]: e.activation(c_[0:1, :], b[0:1, :], AF.Ln, bias=LN_EPS),
                lambda e, c_=s["rowC"]: e.activation(c_[0:1, :], c_[0:1, :], AF.Exp, scale=-0.5),
            ], deps=[rows])
            bc = p.step("tensor", [
                lambda e, ps=sc, o=sh["ones_row"], a=s["rowA"]:
                e.matmul(ps[:, 0:512], o[0:1, :], a[0:1, :], start=True, stop=True),
                lambda e, ps=sc, o=sh["ones_row"], c_=s["rowC"]:
                e.matmul(ps[:, 512:1024], o[0:1, :], c_[0:1, :], start=True, stop=True),
            ], deps=[rstd, rows] + sc_free)
            pR1_free = bc
            pR2_free = bc
            bce = p.step("vector", [
                lambda e, t=s["bcm"], ps=sc: e.tensor_copy(t[:, :], ps[:, 0:512]),
                lambda e, t=s["bcr"], ps=sc: e.tensor_copy(t[:, :], ps[:, 512:1024]),
            ], deps=[bc])
            sc_free = [bce, bce]
            norm_toks = []
            for oc in range(8):
                nt = p.step("vector", [
                    lambda e, t=r1, oc=oc, tt=tt, m=s["bcm"]:
                    e.tensor_add(t[:, oc, tt * 512:(tt + 1) * 512], t[:, oc, tt * 512:(tt + 1) * 512], m[:, :]),
                    lambda e, t=r1, oc=oc, tt=tt, m=s["bcr"]:
                    e.tensor_mul(t[:, oc, tt * 512:(tt + 1) * 512], t[:, oc, tt * 512:(tt + 1) * 512], m[:, :]),
                ], deps=[bce, mm2])
                norm_toks.append(nt)
            x0c_lastuse = radd_toks[-1]
            if tt == 0:
                norm0 = norm_toks
            else:
                norm1 = norm_toks
        r1_stored = p.dma("sync", lambda e, t=r1, v=fm(d["x1_fm"])[:, :, qb0:qb0 + QB]:
                          e.dma_start(out=v, in_=t[:, :, :]), deps=norm0 + norm1, chan="stf")
        x1b_stored = p.dma("gpsimd", lambda e, t=r1, v=fm(d["x1b_fm"])[:, :, qb0:qb0 + QB]:
                           e.dma_start(out=v, in_=t[:, :, :]), deps=norm0 + norm1, chan="stb")
    p.wait_all("gpsimd", [r1_stored])
    p.wait_all("sync", [x1b_stored])
    return [r1_stored, x1b_stored]


def load_consts(p, d, sh):
    t1 = p.dma("sync", lambda e, t=sh["identb"], v=d["identb"]: e.dma_start(out=t[:, :], in_=v))
    t2 = p.dma("sync", lambda e, t=sh["ones_row"], v=d["ones_row"]: e.dma_start(out=t[:, :], in_=v))
    t3 = p.dma("sync", lambda e, t=sh["ln_cols"], v=d["ln_cols"]: e.dma_start(out=t[:, :], in_=v))
    t4 = p.dma("sync", lambda e, t=sh["actc"], v=d["actconsts"]: e.dma_start(out=t[:, :], in_=v))
    return [t1, t2, t3, t4]


def alloc_B(nc, stack):
    s = {}
    s["x1b"] = sbuf(nc, stack, "b_x1b", [128, 8, 1024], BF16)
    s["x1c"] = sbuf(nc, stack, "b_x1c", [128, 8, 1024], F32)
    s["y"] = sbuf(nc, stack, "b_y", [128, 8, 1024], F32)
    s["G"] = sbuf(nc, stack, "b_G", [128, 22, 1024], BF16)
    s["wg0"] = sbuf(nc, stack, "b_wg0", [128, 8, 8], F32)
    s["identf"] = sbuf(nc, stack, "b_identf", [128, 128], F32)
    s["w1g"] = [sbuf(nc, stack, f"b_w1g{i}", [128, 8, 256], BF16) for i in range(2)]
    s["w3g"] = [sbuf(nc, stack, f"b_w3g{i}", [128, 8, 256], BF16) for i in range(2)]
    s["w2g"] = [sbuf(nc, stack, f"b_w2g{i}", [128, 22, 128], BF16) for i in range(2)]
    s["lg"] = sbuf(nc, stack, "b_lg", [128, 48], F32)
    s["cwT"] = sbuf(nc, stack, "b_cwT", [8, 1024], F32)
    s["cwrow"] = sbuf(nc, stack, "b_cwrow", [1, 1024], F32)
    s["cwb"] = sbuf(nc, stack, "b_cwb", [128, 2, 512], F32)
    s["sil"] = sbuf(nc, stack, "b_sil", [128, 512], F32)
    s["tmp"] = sbuf(nc, stack, "b_tmp", [128, 512], F32)
    s["sq"] = sbuf(nc, stack, "b_sq", [128, 512], F32)
    s["rowA"] = sbuf(nc, stack, "b_rowA", [1, 512], F32)
    s["rowB"] = sbuf(nc, stack, "b_rowB", [1, 512], F32)
    s["rowC"] = sbuf(nc, stack, "b_rowC", [1, 512], F32)
    s["bcm"] = sbuf(nc, stack, "b_bcm", [128, 512], F32)
    s["bcr"] = sbuf(nc, stack, "b_bcr", [128, 512], F32)
    s["pg"] = psum(nc, stack, "b_pg", [128, 512])
    s["pH1"] = psum(nc, stack, "b_pH1", [128, 1024])
    s["pH3"] = psum(nc, stack, "b_pH3", [128, 1024])
    s["pcw"] = psum(nc, stack, "b_pcw", [8, 128])
    s["pR1"] = psum(nc, stack, "b_pR1", [1, 512])
    s["pR2"] = psum(nc, stack, "b_pR2", [1, 512])
    s["pBC"] = s["pH1"]
    return s


B_HGROUPS = [(i, min(i + 2, 22)) for i in range(0, 22, 2)]


def emit_LN(p, s, sh, consts, r, dst_f32, dst_b16, tb0, deps_ready, pR1_free, pR2_free, pBC_free, y_stored):
    pR1, pR2, pBC = s["pR1"], s["pR2"], s["pBC"]
    norm_all = []
    for tt in range(2):
        sl = (slice(None), slice(None), slice(tt * 512, (tt + 1) * 512))
        mm1 = None
        for oc in range(8):
            mm1 = p.step("tensor", lambda e, oc=oc, t=r, tt=tt, lc=sh["ln_cols"], ps=pR1:
                         e.matmul(ps[:, :], lc[:, 0:1], t[:, oc, tt * 512:(tt + 1) * 512],
                                  start=(oc == 0), stop=(oc == 7)),
                         deps=(deps_ready + [pR1_free] + consts if oc == 0 else []))
        mm2 = None
        for oc in range(8):
            sq_tok = p.step("vector", lambda e, t=s["sq"], r=r, oc=oc, tt=tt:
                            e.tensor_mul(t[:, :], r[:, oc, tt * 512:(tt + 1) * 512],
                                         r[:, oc, tt * 512:(tt + 1) * 512]),
                            deps=deps_ready + [mm2])
            mm2 = p.step("tensor", lambda e, oc=oc, t=s["sq"], ps=pR2, lc=sh["ln_cols"]:
                         e.matmul(ps[:, :], lc[:, 1:2], t[:, :],
                                  start=(oc == 0), stop=(oc == 7)),
                         deps=([sq_tok, pR2_free] if oc == 0 else [sq_tok]))
        rows = p.step("vector", [
            lambda e, t=s["rowA"], ps=pR1: e.tensor_copy(t[0:1, :], ps[0:1, :]),
            lambda e, a=s["rowA"], b=s["rowB"]:
            e.scalar_tensor_tensor(b[0:1, :], a[0:1, :], -1.0, a[0:1, :], ALU.mult, ALU.mult),
            lambda e, b=s["rowB"], ps=pR2: e.tensor_add(b[0:1, :], b[0:1, :], ps[0:1, :]),
        ], deps=[mm1, mm2, pR1_free, pR2_free])
        rstd = p.step("scalar", [
            lambda e, b=s["rowB"], c_=s["rowC"]: e.activation(c_[0:1, :], b[0:1, :], AF.Ln, bias=LN_EPS),
            lambda e, c_=s["rowC"]: e.activation(c_[0:1, :], c_[0:1, :], AF.Exp, scale=-0.5),
        ], deps=[rows])
        bc = p.step("tensor", [
            lambda e, ps=pBC, o=sh["ones_row"], a=s["rowA"]:
            e.matmul(ps[:, 0:512], o[0:1, :], a[0:1, :], start=True, stop=True),
            lambda e, ps=pBC, o=sh["ones_row"], c_=s["rowC"]:
            e.matmul(ps[:, 512:1024], o[0:1, :], c_[0:1, :], start=True, stop=True),
        ], deps=[rstd, rows, pBC_free])
        pR1_free = bc
        pR2_free = bc
        bce = p.step("vector", [
            lambda e, t=s["bcm"], ps=pBC: e.tensor_copy(t[:, :], ps[:, 0:512]),
            lambda e, t=s["bcr"], ps=pBC: e.tensor_copy(t[:, :], ps[:, 512:1024]),
        ], deps=[bc])
        pBC_free = bce
        for oc in range(8):
            nt1 = p.step("vector", lambda e, t=r, oc=oc, tt=tt, m=s["bcm"]:
                         e.tensor_add(t[:, oc, tt * 512:(tt + 1) * 512],
                                      t[:, oc, tt * 512:(tt + 1) * 512], m[:, :]),
                         deps=[bce, mm2, y_stored])
            nt2 = p.step("vector", lambda e, t=r, oc=oc, tt=tt, m=s["bcr"]:
                         e.tensor_mul(t[:, oc, tt * 512:(tt + 1) * 512],
                                      t[:, oc, tt * 512:(tt + 1) * 512], m[:, :]),
                         deps=[nt1])
            norm_all.append(nt2)
    st1 = p.dma("sync", lambda e, t=r, v=fm(dst_f32)[:, :, tb0:tb0 + 1024]:
                e.dma_start(out=v, in_=t[:, :, :]), deps=norm_all, chan="stf")
    st2 = None
    if dst_b16 is not None:
        st2 = p.dma("gpsimd", lambda e, t=r, v=fm(dst_b16)[:, :, tb0:tb0 + 1024]:
                    e.dma_start(out=v, in_=t[:, :, :]), deps=norm_all, chan="stb")
    return st1, st2, pR1_free, pR2_free, pBC_free, norm_all


def emit_B(p, d, s, sh, consts):
    wg_tok = p.dma("sync", lambda e, t=s["wg0"], v=d["Wg0T"].rearrange("(c p) ee -> p c ee", p=128):
                   e.dma_start(out=t[:, :, :], in_=v))
    idf_tok = p.dma("sync", lambda e, t=s["identf"], v=d["identf"]: e.dma_start(out=t[:, :], in_=v))
    consts = consts + [idf_tok]

    NTB = 4
    TB = 1024
    x1b, x1c, y, G = s["x1b"], s["x1c"], s["y"], s["G"]
    pg, pH1, pH3, pcw = s["pg"], s["pH1"], s["pH3"], s["pcw"]
    lg, cwT = s["lg"], s["cwT"]

    x1b_lastuse = None
    x1c_lastuse = None
    stf = None
    stb = None
    w1_lastuse = [None, None]
    w3_lastuse = [None, None]
    w2_lastuse = [None, None]
    pg_free = None
    pH1_free = [None, None]
    pH3_free = [None, None]
    pcw_free = None
    pR1_free = None
    pR2_free = None
    pBC_free = None
    cwT_lastuse = None
    cwrow_lastuse = None
    G_lastread = None
    sil_free = None
    tmp_free = None
    y_lastread = None
    norm_all = []

    for tb in range(NTB):
        tb0 = tb * TB
        ldx = p.dma("sync", lambda e, t=x1b, v=fm(d["x1b_fm"])[:, :, tb0:tb0 + TB]:
                    e.dma_start(out=t[:, :, :], in_=v), deps=[x1b_lastuse], chan="xmb")
        ldr = p.dma("sync", lambda e, t=x1c, v=fm(d["x1_fm"])[:, :, tb0:tb0 + TB]:
                    e.dma_start(out=t[:, :, :], in_=v), deps=[x1c_lastuse], chan="xmc")
        cwe_toks = []
        for tc in range(8):
            gmm = None
            for c in range(8):
                gmm = p.step("tensor", lambda e, c=c, tc=tc, x=x1c, w=s["wg0"], ps=pg:
                             e.matmul(ps[:, 0:8], x[:, c, tc * 128:(tc + 1) * 128], w[:, c, :],
                                      start=(c == 0), stop=(c == 7)),
                             deps=(([ldr, ldx, wg_tok, pg_free] + consts) if c == 0 else []))
            t1 = p.step("vector", lambda e, t=lg, ps=pg: e.tensor_copy(t[:, 0:8], ps[:, 0:8]), deps=[gmm])
            pg_free = t1
            t2 = p.step("vector", lambda e, t=lg: e.reduce_max(t[:, 8:9], t[:, 0:8], axis=AX.X), deps=[t1])
            t3 = p.step("vector", lambda e, t=lg: e.tensor_scalar(t[:, 9:10], t[:, 8:9], -1.0, None, ALU.mult), deps=[t2])
            t4 = p.step("vector", lambda e, t=lg: e.tensor_scalar(t[:, 16:24], t[:, 0:8], t[:, 8:9], None, ALU.is_ge), deps=[t2])
            t5 = p.step("vector", lambda e, t=lg: e.scalar_tensor_tensor(t[:, 24:32], t[:, 16:24], NEG_BIG, t[:, 0:8], ALU.mult, ALU.add), deps=[t4])
            t6 = p.step("vector", lambda e, t=lg: e.reduce_max(t[:, 10:11], t[:, 24:32], axis=AX.X), deps=[t5])
            t7 = p.step("scalar", lambda e, t=lg: e.activation(t[:, 32:40], t[:, 0:8], AF.Exp, bias=t[:, 9:10]), deps=[t3])
            t8 = p.step("vector", lambda e, t=lg: e.tensor_sub(t[:, 11:12], t[:, 10:11], t[:, 8:9]), deps=[t6])
            t9 = p.step("scalar", lambda e, t=lg: e.activation(t[:, 12:13], t[:, 11:12], AF.Exp), deps=[t8])
            t10 = p.step("vector", lambda e, t=lg: e.tensor_scalar(t[:, 13:14], t[:, 12:13], 1.0, None, ALU.add), deps=[t9])
            t11 = p.step("vector", lambda e, t=lg: e.reciprocal(t[:, 14:15], t[:, 13:14]), deps=[t10])
            t12 = p.step("vector", lambda e, t=lg: e.tensor_scalar(t[:, 16:24], t[:, 0:8], t[:, 10:11], None, ALU.is_ge), deps=[t6, t7])
            t13 = p.step("vector", lambda e, t=lg: e.tensor_mul(t[:, 40:48], t[:, 32:40], t[:, 16:24]), deps=[t12, t7])
            t14 = p.step("vector", lambda e, t=lg: e.tensor_scalar(t[:, 40:48], t[:, 40:48], t[:, 14:15], None, ALU.mult), deps=[t13, t11])
            t15 = p.step("tensor", lambda e, t=lg, ps=pcw, idn=s["identf"]:
                         e.transpose(ps[:, :], t[:, 40:48], idn[:, :]), deps=[t14, pcw_free])
            t16 = p.step("vector", lambda e, t=cwT, tc=tc, ps=pcw:
                         e.tensor_copy(t[0:8, tc * 128:(tc + 1) * 128], ps[0:8, :]),
                         deps=[t15, cwT_lastuse])
            pcw_free = t16
            cwe_toks.append(t16)
        y_first = [True] * 16
        for ex in range(8):
            cwr = p.dma("sync", lambda e, t=s["cwrow"], src=cwT, ex=ex:
                        e.dma_start(out=t[0:1, :], in_=src[ex:ex + 1, :]),
                        deps=cwe_toks + [cwrow_lastuse], chan="cwrow")
            cbm = p.step("tensor", [
                lambda e, ps=pg, o=sh["ones_row"], t=s["cwrow"]:
                e.matmul(ps[:, :], o[0:1, :], t[0:1, 0:512], start=True, stop=True),
            ], deps=[cwr, pg_free])
            cbe1 = p.step("vector", lambda e, t=s["cwb"], ps=pg: e.tensor_copy(t[:, 0, :], ps[:, :]), deps=[cbm])
            cbm2 = p.step("tensor", lambda e, ps=pg, o=sh["ones_row"], t=s["cwrow"]:
                          e.matmul(ps[:, :], o[0:1, :], t[0:1, 512:1024], start=True, stop=True),
                          deps=[cbe1])
            cbe2 = p.step("vector", lambda e, t=s["cwb"], ps=pg: e.tensor_copy(t[:, 1, :], ps[:, :]), deps=[cbm2])
            pg_free = cbe2
            cwrow_lastuse = cbm2
            g_toks = []
            for gi, (h0, h1) in enumerate(B_HGROUPS):
                gb = gi % 2
                nh = h1 - h0
                ld1 = p.dma("gpsimd", lambda e, t=s["w1g"][gb], ex=ex, h0=h0, nh=nh,
                            v=d["W1T0"][ex].rearrange("(c p) hh -> p c hh", p=128)[:, :, h0 * 128:h1 * 128]:
                            e.dma_start(out=t[:, :, 0:nh * 128], in_=v),
                            deps=[w1_lastuse[gb]], chan=f"w1g{gb}")
                ld3 = p.dma("gpsimd", lambda e, t=s["w3g"][gb], ex=ex, h0=h0, nh=nh,
                            v=d["W3T0"][ex].rearrange("(c p) hh -> p c hh", p=128)[:, :, h0 * 128:h1 * 128]:
                            e.dma_start(out=t[:, :, 0:nh * 128], in_=v),
                            deps=[w3_lastuse[gb]], chan=f"w3g{gb}")
                for hi in range(nh):
                    hc = h0 + hi
                    for tt in range(2):
                        par = (hc * 2 + tt) % 2
                        hsl = slice(par * 512, (par + 1) * 512)
                        m1t = None
                        m3t = None
                        for c in range(8):
                            m1t = p.step("tensor", lambda e, c=c, hi=hi, tt=tt, w=s["w1g"][gb], x=x1b, ps=pH1, hsl=hsl:
                                         e.matmul(ps[:, hsl], w[:, c, hi * 128:(hi + 1) * 128],
                                                  x[:, c, tt * 512:(tt + 1) * 512],
                                                  start=(c == 0), stop=(c == 7)),
                                         deps=([ld1, pH1_free[par]] if c == 0 else []))
                            m3t = p.step("tensor", lambda e, c=c, hi=hi, tt=tt, w=s["w3g"][gb], x=x1b, ps=pH3, hsl=hsl:
                                         e.matmul(ps[:, hsl], w[:, c, hi * 128:(hi + 1) * 128],
                                                  x[:, c, tt * 512:(tt + 1) * 512],
                                                  start=(c == 0), stop=(c == 7)),
                                         deps=([ld3, pH3_free[par]] if c == 0 else []))
                        silt = p.step("scalar", lambda e, t=s["sil"], ps=pH1, hsl=hsl:
                                      e.activation(t[:, :], ps[:, hsl], AF.Silu),
                                      deps=[m1t, sil_free])
                        pH1_free[par] = silt
                        gt = p.step("vector", lambda e, t=G, hc=hc, tt=tt, si=s["sil"], ps=pH3, hsl=hsl:
                                    e.tensor_mul(t[:, hc, tt * 512:(tt + 1) * 512], si[:, :], ps[:, hsl]),
                                    deps=[silt, m3t, G_lastread])
                        pH3_free[par] = gt
                        sil_free = gt
                        g_toks.append(gt)
                w1_lastuse[gb] = m1t
                w3_lastuse[gb] = m3t
            out_toks = []
            for dc in range(8):
                wb = dc % 2
                ld2 = p.dma("gpsimd", lambda e, t=s["w2g"][wb], ex=ex, dc=dc,
                            v=d["W2T0"][ex][:, dc * 128:(dc + 1) * 128].rearrange("(a p) q -> p a q", p=128):
                            e.dma_start(out=t[:, :, :], in_=v),
                            deps=[w2_lastuse[wb]], chan=f"w2g{wb}")
                for tt in range(2):
                    par = tt
                    hsl = slice(par * 512, (par + 1) * 512)
                    pHo = pH3
                    mo = None
                    for hc in range(22):
                        mo = p.step("tensor", lambda e, hc=hc, tt=tt, w=s["w2g"][wb], ps=pHo, hsl=hsl:
                                    e.matmul(ps[:, hsl], w[:, hc, :], G[:, hc, tt * 512:(tt + 1) * 512],
                                             start=(hc == 0), stop=(hc == 21)),
                                    deps=(([ld2, pH3_free[par]] + g_toks) if hc == 0 else []))
                    if ex == 0:
                        yt = p.step("vector", lambda e, t=y, dc=dc, tt=tt, cb=s["cwb"], ps=pHo, hsl=hsl:
                                    e.tensor_tensor(t[:, dc, tt * 512:(tt + 1) * 512], ps[:, hsl], cb[:, tt, :], ALU.mult),
                                    deps=[mo, cbe1 if tt == 0 else cbe2, y_lastread])
                    else:
                        tmpt = p.step("vector", lambda e, t=s["tmp"], cb=s["cwb"], tt=tt, ps=pHo, hsl=hsl:
                                      e.tensor_tensor(t[:, :], ps[:, hsl], cb[:, tt, :], ALU.mult),
                                      deps=[mo, cbe1 if tt == 0 else cbe2, tmp_free])
                        yt = p.step("vector", lambda e, t=y, dc=dc, tt=tt, tp=s["tmp"]:
                                    e.tensor_add(t[:, dc, tt * 512:(tt + 1) * 512],
                                                 t[:, dc, tt * 512:(tt + 1) * 512], tp[:, :]),
                                    deps=[tmpt])
                        tmp_free = yt
                    pH3_free[par] = yt
                    out_toks.append(yt)
                w2_lastuse[wb] = mo
            G_lastread = mo
        radd = []
        for oc in range(8):
            rt = p.step("vector", lambda e, t=y, oc=oc, x=x1c:
                        e.tensor_add(t[:, oc, :], t[:, oc, :], x[:, oc, :]),
                        deps=[ldr] + out_toks)
            radd.append(rt)
        x1c_lastuse = radd[-1]
        x1b_lastuse = m3t
        stf, stb, pR1_free, pR2_free, pBC_free, norm_all = emit_LN(
            p, s, sh, consts, y, d["x2_fm"], d["x2b_fm"], tb0, radd + pH1_free,
            pR1_free, pR2_free, pBC_free, stb)
        pH1_free = [pBC_free, pBC_free]
        y_lastread = stb if stb is not None else stf
        cwT_lastuse = cwr
    p.wait_all("gpsimd", [stf])
    p.wait_all("sync", [stb])
    return [stf, stb] + norm_all


def alloc_C(nc, stack):
    s = {}
    s["xw"] = [sbuf(nc, stack, f"c_xw{i}", [128, 8, 514], BF16) for i in range(2)]
    s["qkw"] = sbuf(nc, stack, "c_qkw", [128, 8, 2048], BF16)
    s["lw"] = [sbuf(nc, stack, f"c_lw{i}", [128, 3, 8, 128], BF16) for i in range(2)]
    s["x2c"] = sbuf(nc, stack, "c_x2c", [128, 8, 512], F32)
    s["r"] = sbuf(nc, stack, "c_r", [128, 8, 1024], F32)
    s["identb"] = sbuf(nc, stack, "c_identb", [128, 128], BF16)
    s["onescb"] = sbuf(nc, stack, "c_onescb", [128, 1], BF16)
    s["theta"] = sbuf(nc, stack, "c_theta", [1, 512], F32)
    s["posc"] = sbuf(nc, stack, "c_posc", [1, 128], F32)
    s["qkb"] = sbuf(nc, stack, "c_qkb", [1, 2048], F32)
    s["lepeb"] = sbuf(nc, stack, "c_lepeb", [128, 8], F32)
    s["wl"] = sbuf(nc, stack, "c_wl", [8, 2], F32)
    s["ag"] = sbuf(nc, stack, "c_ag", [8, 1024], F32)
    s["u"] = sbuf(nc, stack, "c_u", [128, 512], F32)
    s["kk"] = sbuf(nc, stack, "c_kk", [128, 512], F32)
    s["red"] = sbuf(nc, stack, "c_red", [128, 512], F32)
    s["sin"] = sbuf(nc, stack, "c_sin", [128, 512], F32)
    s["cos"] = sbuf(nc, stack, "c_cos", [128, 512], F32)
    s["t1"] = sbuf(nc, stack, "c_t1", [128, 1024], F32)
    s["t2"] = sbuf(nc, stack, "c_t2", [128, 512], F32)
    s["kh"] = sbuf(nc, stack, "c_kh", [128, 1024], F32)
    s["kb"] = sbuf(nc, stack, "c_kb", [128, 1024], BF16)
    s["kr"] = sbuf(nc, stack, "c_kr", [128, 1024], BF16)
    s["vt"] = sbuf(nc, stack, "c_vt", [128, 1024], BF16)
    s["prpi"] = sbuf(nc, stack, "c_prpi", [128, 1024], F32)
    s["zq"] = sbuf(nc, stack, "c_zq", [128, 1024], BF16)
    s["zqT"] = sbuf(nc, stack, "c_zqT", [128, 8, 128], BF16)
    s["z8"] = sbuf(nc, stack, "c_z8", [128, 24], F32)
    s["kvb"] = sbuf(nc, stack, "c_kvb", [128, 8, 128], BF16)
    s["kmb"] = sbuf(nc, stack, "c_kmb", [128, 1024], F32)
    s["ksr"] = sbuf(nc, stack, "c_ksr", [1, 1024], F32)
    s["kmr"] = sbuf(nc, stack, "c_kmr", [1, 1024], F32)
    s["hlr"] = sbuf(nc, stack, "c_hlr", [1, 1024], F32)
    s["lep"] = sbuf(nc, stack, "c_lep", [128, 512], F32)
    s["sq"] = sbuf(nc, stack, "c_sq", [128, 512], F32)
    s["rowA"] = sbuf(nc, stack, "c_rowA", [1, 512], F32)
    s["rowB"] = sbuf(nc, stack, "c_rowB", [1, 512], F32)
    s["rowC"] = sbuf(nc, stack, "c_rowC", [1, 512], F32)
    s["bcm"] = sbuf(nc, stack, "c_bcm", [128, 512], F32)
    s["bcr"] = sbuf(nc, stack, "c_bcr", [128, 512], F32)
    s["pANG"] = psum(nc, stack, "c_pANG", [128, 512])
    s["psK0"] = psum(nc, stack, "c_psK0", [128, 512])
    s["psK1"] = psum(nc, stack, "c_psK1", [128, 512])
    s["pR1"] = psum(nc, stack, "c_pR1", [1, 512])
    s["pR2"] = psum(nc, stack, "c_pR2", [1, 512])
    s["pKV"] = psum(nc, stack, "c_pKV", [128, 1024])
    s["ptv"] = psum(nc, stack, "c_ptv", [128, 1024], BF16)
    s["pBC"] = s["pKV"]
    return s


def _rope_tables(p, d, s, tc, deps):
    pANG = s["pANG"]
    pc = p.dma("sync", lambda e, t=s["posc"], v=d["pos_rows"][tc:tc + 1, :]:
               e.dma_start(out=t[0:1, :], in_=v), deps=deps, chan="posc")
    ang = p.step("tensor", lambda e, ps=pANG, pr=s["posc"], th=s["theta"]:
                 e.matmul(ps[:, :], pr[0:1, :], th[0:1, :],
                          start=True, stop=True), deps=[pc] + deps)
    a1 = p.step("vector", lambda e, t=s["u"], ps=pANG:
                e.tensor_scalar(t[:, :], ps[:, :], INV_2PI, None, ALU.mult), deps=[ang])
    a2 = p.step("vector", lambda e, t=s["kk"], u=s["u"]:
                e.tensor_scalar(t[:, :], u[:, :], MAGIC, MAGIC, ALU.add, ALU.subtract), deps=[a1])
    a3 = p.step("vector", lambda e, t=s["red"], k=s["kk"], ps=pANG:
                e.scalar_tensor_tensor(t[:, :], k[:, :], -TWO_PI_HI, ps[:, :], ALU.mult, ALU.add), deps=[a2])
    a4 = p.step("vector", lambda e, t=s["u"], k=s["kk"], r=s["red"]:
                e.scalar_tensor_tensor(t[:, :], k[:, :], -TWO_PI_LO, r[:, :], ALU.mult, ALU.add), deps=[a3])
    sin_t = p.step("scalar", lambda e, t=s["sin"], r=s["u"]:
                   e.activation(t[:, :], r[:, :], AF.Sin), deps=[a4])
    b1 = p.step("vector", lambda e, t=s["u"], ps=pANG:
                e.tensor_scalar(t[:, :], ps[:, :], INV_2PI, 0.25, ALU.mult, ALU.add), deps=[a4, sin_t])
    b2 = p.step("vector", lambda e, t=s["kk"], u=s["u"]:
                e.tensor_scalar(t[:, :], u[:, :], MAGIC, MAGIC, ALU.add, ALU.subtract), deps=[b1])
    b3 = p.step("vector", lambda e, t=s["red"], k=s["kk"], ps=pANG:
                e.scalar_tensor_tensor(t[:, :], k[:, :], -TWO_PI_HI, ps[:, :], ALU.mult, ALU.add), deps=[b2])
    b4 = p.step("vector", lambda e, t=s["u"], k=s["kk"], r=s["red"]:
                e.scalar_tensor_tensor(t[:, :], k[:, :], -TWO_PI_LO, r[:, :], ALU.mult, ALU.add), deps=[b3])
    b5 = p.step("vector", lambda e, t=s["u"]:
                e.tensor_scalar(t[:, :], t[:, :], HALF_PI, None, ALU.add), deps=[b4])
    cos_t = p.step("scalar", lambda e, t=s["cos"], r=s["u"]:
                   e.activation(t[:, :], r[:, :], AF.Sin), deps=[b5])
    return sin_t, cos_t, b5


def _elu1(p, s, psum0, psum1, out, deps0, deps1):
    toks = []
    for oh, ps in ((0, psum0), (1, psum1)):
        sl = slice(oh * 512, (oh + 1) * 512)
        c1 = p.step("vector", lambda e, t=s["t1"], ps=ps, sl=sl:
                    e.tensor_scalar(t[:, sl], ps[:, :], 0.0, None, ALU.min),
                    deps=(deps0 if oh == 0 else deps1))
        c2 = p.step("vector", lambda e, t=s["t2"], ps=ps, sl=sl:
                    e.tensor_scalar(t[:, :], ps[:, :], 0.0, None, ALU.max),
                    deps=[c1])
        c3 = p.step("scalar", lambda e, t=s["t1"], sl=sl:
                    e.activation(t[:, sl], t[:, sl], AF.Exp), deps=[c1])
        c4 = p.step("vector", lambda e, o=out, t1=s["t1"], t2=s["t2"], sl=sl:
                    e.tensor_add(o[:, sl], t1[:, sl], t2[:, :]), deps=[c2, c3])
        toks.append(c4)
    return toks


def emit_C(p, d, s, sh, consts, rg):
    qkw_tok = p.dma("gpsimd", lambda e, t=s["qkw"], v=d["qk_w"].rearrange("(c p) o -> p c o", p=128):
                    e.dma_start(out=t[:, :, :], in_=v))
    small = [
        p.dma("sync", lambda e, t=s["identb"], v=d["identb"]: e.dma_start(out=t[:, :], in_=v)),
        p.dma("sync", lambda e, t=s["onescb"], v=d["ones_colb"]: e.dma_start(out=t[:, :], in_=v)),
        p.dma("sync", lambda e, t=s["theta"], v=d["theta"]: e.dma_start(out=t[:, :], in_=v)),
        p.dma("sync", lambda e, t=s["qkb"], v=d["qk_b"]: e.dma_start(out=t[:, :], in_=v)),
        p.dma("sync", lambda e, t=s["lepeb"], v=d["lepe_b"]: e.dma_start(out=t[:, :], in_=v)),
        p.dma("sync", lambda e, t=s["wl"], v=d["wlr"]: e.dma_start(out=t[:, :], in_=v)),
    ] + consts
    pANG, psK0, psK1, pKV, ptv = s["pANG"], s["psK0"], s["psK1"], s["pKV"], s["ptv"]

    def load_window(w, buf, with_halo, deps):
        toks = []
        lo = w * 512 - 1
        hi = w * 512 + 513
        clo = max(lo, 0)
        chi = min(hi, TPC)
        off = clo - lo
        toks.append(p.dma("sync", lambda e, t=s["xw"][buf], off=off, clo=clo, chi=chi,
                          v=fm(d["x2b_fm"])[:, :, clo:chi]:
                          e.dma_start(out=t[:, :, off:off + (chi - clo)], in_=v), deps=deps, chan=f"xw{buf}"))
        if with_halo:
            if w == 0:
                toks.append(p.dma("gpsimd", lambda e, t=s["xw"][buf]:
                                  e.dma_start(out=t[:, :, 0:1],
                                              in_=d["halo_l"].rearrange("1 (c p) -> p c 1", p=128),
                                              allow_slow_non_contiguous=True),
                                  deps=deps + [halo_ready[0]]))
            if w == 7:
                toks.append(p.dma("gpsimd", lambda e, t=s["xw"][buf]:
                                  e.dma_start(out=t[:, :, 513:514],
                                              in_=d["halo_r"].rearrange("1 (c p) -> p c 1", p=128),
                                              allow_slow_non_contiguous=True),
                                  deps=deps + [halo_ready[1]]))
        return toks

    p.region = "C1"
    kv_mm = None
    ks_mm = None
    tab_free = None
    elu_free = None
    kb_free = None
    kr_free = None
    vt_free = None
    xw_lastuse = [None, None]
    wload = [None, None]
    halo_ready = [None, None]
    for w in range(8):
        buf = w % 2
        wl_toks = load_window(w, buf, False, [xw_lastuse[buf]])
        wload[buf] = wl_toks[0]
        for tci in range(4):
            tc = w * 4 + tci
            xsl = lambda c, tci=tci: (slice(None), c, slice(1 + tci * 128, 1 + (tci + 1) * 128))
            sin_t, cos_t, tabend = _rope_tables(p, d, s, tc, [tab_free] + (small if tc == 0 else []))
            kms = []
            for oh, ps in ((0, psK0), (1, psK1)):
                mmt = None
                for c in range(8):
                    mmt = p.step("tensor", lambda e, c=c, tci=tci, oh=oh, x=s["xw"][buf], w_=s["qkw"], ps=ps:
                                 e.matmul(ps[:, :], x[:, c, 1 + tci * 128:1 + (tci + 1) * 128],
                                          w_[:, c, 1024 + oh * 512:1024 + (oh + 1) * 512],
                                          start=(c == 0), stop=False),
                                 deps=([wload[buf], qkw_tok, elu_free] if c == 0 else []))
                mmt = p.step("tensor", lambda e, oh=oh, o=sh["ones_row"], b=s["qkb"], ps=ps:
                             e.matmul(ps[:, :], o[0:1, :], b[0:1, 1024 + oh * 512:1024 + (oh + 1) * 512],
                                      start=False, stop=True), deps=[])
                kms.append(mmt)
            el = _elu1(p, s, psK0, psK1, s["kh"], [kms[0]], [kms[1]])
            elu_free = el[-1]
            kbt = p.step("vector", lambda e, t=s["kb"], k=s["kh"]: e.tensor_copy(t[:, :], k[:, :]),
                         deps=el + [kb_free])
            ks1 = p.step("tensor", lambda e, t=s["kb"], o=s["onescb"], ps=s["pR1"], tc=tc:
                         e.matmul(ps[0:1, :], o[:, 0:1], t[:, 0:512],
                                  start=(tc == 0), stop=(tc == 31)), deps=[kbt])
            ks_mm = p.step("tensor", lambda e, t=s["kb"], o=s["onescb"], ps=s["pR2"], tc=tc:
                           e.matmul(ps[0:1, :], o[:, 0:1], t[:, 512:1024],
                                    start=(tc == 0), stop=(tc == 31)), deps=[])
            r1 = p.step("vector", lambda e, t=s["t1"], c_=s["cos"], k=s["kb"]:
                        e.tensor_mul(t[:, 0:512], c_[:, :], k[:, 0:1024:2]), deps=[kbt, cos_t, kr_free])
            r2 = p.step("vector", lambda e, t=s["t2"], sn=s["sin"], k=s["kb"]:
                        e.tensor_mul(t[:, :], sn[:, :], k[:, 1:1024:2]), deps=[sin_t])
            r3 = p.step("vector", lambda e, o=s["kr"], a=s["t1"], b=s["t2"]:
                        e.tensor_sub(o[:, 0:512], a[:, 0:512], b[:, :]), deps=[r1, r2])
            r4 = p.step("vector", lambda e, t=s["t1"], c_=s["cos"], k=s["kb"]:
                        e.tensor_mul(t[:, 0:512], c_[:, :], k[:, 1:1024:2]), deps=[r3])
            r5 = p.step("vector", lambda e, t=s["t2"], sn=s["sin"], k=s["kb"]:
                        e.tensor_mul(t[:, :], sn[:, :], k[:, 0:1024:2]), deps=[r3])
            r6 = p.step("vector", lambda e, o=s["kr"], a=s["t1"], b=s["t2"]:
                        e.tensor_add(o[:, 512:1024], a[:, 0:512], b[:, :]), deps=[r4, r5])
            tab_free = r6
            if tc == 0 and "dbgSIN" in d:
                p.dma("sync", lambda e: e.dma_start(out=d["dbgSIN"], in_=s["sin"][:, :]), deps=[r6])
                p.dma("sync", lambda e: e.dma_start(out=d["dbgCOS"], in_=s["cos"][:, :]), deps=[r6])
                v1 = p.step("vector", lambda e: e.tensor_copy(s["t1"][:, :], s["kb"][:, :]), deps=[r6])
                s1tok = p.dma("sync", lambda e: e.dma_start(out=d["dbgKB"], in_=s["t1"][:, :]), deps=[v1])
                v2 = p.step("vector", lambda e: e.tensor_copy(s["t1"][:, :], s["kr"][:, :]), deps=[s1tok])
                p.dma("sync", lambda e: e.dma_start(out=d["dbgKR"], in_=s["t1"][:, :]), deps=[v2])
            vts = []
            for c in range(8):
                vts.append(p.step("tensor", lambda e, c=c, x=s["xw"][buf], tci=tci, ps=ptv, idn=s["identb"]:
                                  e.transpose(ps[:, c * 128:(c + 1) * 128],
                                              x[:, c, 1 + tci * 128:1 + (tci + 1) * 128], idn[:, :]),
                                  deps=([ks_mm, vt_free, small[0]] if c == 0 else [])))
            vte = p.step("vector", lambda e, t=s["vt"], ps=ptv: e.tensor_copy(t[:, :], ps[:, :]),
                         deps=[vts[-1]])
            kvdeps_extra = [p.tok("sync")] if (tc == 0 and "dbgSIN" in d) else []
            for h in range(8):
                kv_mm = p.step("tensor", lambda e, h=h, kr=s["kr"], v=s["vt"], ps=pKV, tc=tc:
                               e.matmul(ps[:, h * 128:(h + 1) * 128], kr[:, h * 128:(h + 1) * 128],
                                        v[:, h * 128:(h + 1) * 128],
                                        start=(tc == 0 and h % 4 == 0), stop=(tc == 31)),
                               deps=([r6, vte] + kvdeps_extra if h == 0 else []))
            vt_free = kv_mm
            kr_free = kv_mm
            kb_free = ks_mm
        xw_lastuse[buf] = kv_mm
    kvev = p.step("vector", lambda e, t=s["t1"], ps=pKV:
                  e.tensor_scalar(t[:, :], ps[:, :], 1.0 / 128.0, None, ALU.mult), deps=[kv_mm])
    ksev = p.step("vector", [
        lambda e, t=s["ksr"], ps=s["pR1"]: e.tensor_copy(t[0:1, 0:512], ps[0:1, :]),
        lambda e, t=s["ksr"], ps=s["pR2"]: e.tensor_copy(t[0:1, 512:1024], ps[0:1, :]),
    ], deps=[ks_mm])
    d1 = p.dma("gpsimd", lambda e, t=s["t1"]: e.dma_start(out=d["cc_kv_in"], in_=t[:, :]), deps=[kvev])
    d2 = p.dma("gpsimd", lambda e, t=s["ksr"]: e.dma_start(out=d["cc_ks_in"], in_=t[0:1, :]), deps=[ksev])
    d3 = p.dma("gpsimd", lambda e: e.dma_start(out=d["ag_in"][0:1, :], in_=d["x2_fm"][:, 0:1].rearrange("f o -> o f"),
                                               allow_slow_non_contiguous=True), deps=[])
    d4 = p.dma("gpsimd", lambda e: e.dma_start(out=d["ag_in"][1:2, :], in_=d["x2_fm"][:, 4095:4096].rearrange("f o -> o f"),
                                               allow_slow_non_contiguous=True), deps=[])
    cc1 = p.cc(lambda e: e.collective_compute(
        "AllReduce", ALU.add, replica_groups=rg, ins=[d["cc_kv_in"]], outs=[d["cc_kv_out"]]),
        deps=[d1])
    cc2 = p.cc(lambda e: e.collective_compute(
        "AllReduce", ALU.add, replica_groups=rg, ins=[d["cc_ks_in"]], outs=[d["cc_ks_out"]]),
        deps=[d2])
    cc3 = p.cc(lambda e: e.collective_compute(
        "AllGather", ALU.bypass, replica_groups=rg, ins=[d["ag_in"]], outs=[d["ag_out"]]),
        deps=[d3, d4])
    agl = p.dma("sync", lambda e, t=s["ag"]: e.dma_start(out=t[:, :], in_=d["ag_out"]), deps=[cc3])
    hl1 = p.step("tensor", lambda e, ps=pANG, w=s["wl"], a=s["ag"]:
                 e.matmul(ps[0:1, :], w[0:8, 0:1], a[0:8, 0:512], start=True, stop=True),
                 deps=[agl, small[-1], tab_free])
    hle1 = p.step("vector", lambda e, t=s["hlr"], ps=pANG: e.tensor_copy(t[0:1, 0:512], ps[0:1, :]), deps=[hl1])
    hl2 = p.step("tensor", lambda e, ps=pANG, w=s["wl"], a=s["ag"]:
                 e.matmul(ps[0:1, :], w[0:8, 0:1], a[0:8, 512:1024], start=True, stop=True), deps=[hle1])
    hle2 = p.step("vector", lambda e, t=s["hlr"], ps=pANG: e.tensor_copy(t[0:1, 512:1024], ps[0:1, :]), deps=[hl2])
    dhl = p.dma("gpsimd", lambda e, t=s["hlr"]: e.dma_start(out=d["halo_l"], in_=t[0:1, :]), deps=[hle2])
    hr1 = p.step("tensor", lambda e, ps=pANG, w=s["wl"], a=s["ag"]:
                 e.matmul(ps[0:1, :], w[0:8, 1:2], a[0:8, 0:512], start=True, stop=True), deps=[hle2])
    hre1 = p.step("vector", lambda e, t=s["hlr"], ps=pANG: e.tensor_copy(t[0:1, 0:512], ps[0:1, :]), deps=[hr1, dhl])
    hr2 = p.step("tensor", lambda e, ps=pANG, w=s["wl"], a=s["ag"]:
                 e.matmul(ps[0:1, :], w[0:8, 1:2], a[0:8, 512:1024], start=True, stop=True), deps=[hre1])
    hre2 = p.step("vector", lambda e, t=s["hlr"], ps=pANG: e.tensor_copy(t[0:1, 512:1024], ps[0:1, :]), deps=[hr2])
    dhr = p.dma("gpsimd", lambda e, t=s["hlr"]: e.dma_start(out=d["halo_r"], in_=t[0:1, :]), deps=[hre2])
    halo_ready = [dhl, dhr]
    kvg = p.dma("gpsimd", lambda e, t=s["kvb"], v=d["cc_kv_out"].rearrange("p (h ee) -> p h ee", h=8):
                e.dma_start(out=t[:, :, :], in_=v), deps=[cc1])
    ksg = p.dma("sync", lambda e, t=s["ksr"]: e.dma_start(out=t[0:1, :], in_=d["cc_ks_out"]), deps=[cc2, ksev, d2])
    km = p.step("vector", lambda e, t=s["kmr"], k=s["ksr"]:
                e.tensor_scalar(t[0:1, :], k[0:1, :], 1.0 / 16384.0, None, ALU.mult), deps=[ksg])
    kb1 = p.step("tensor", lambda e, ps=pANG, o=sh["ones_row"], k=s["kmr"]:
                 e.matmul(ps[:, :], o[0:1, :], k[0:1, 0:512], start=True, stop=True), deps=[km, hre2, dhr])
    kbe1 = p.step("vector", lambda e, t=s["kmb"], ps=pANG: e.tensor_copy(t[:, 0:512], ps[:, :]), deps=[kb1])
    kb2 = p.step("tensor", lambda e, ps=pANG, o=sh["ones_row"], k=s["kmr"]:
                 e.matmul(ps[:, :], o[0:1, :], k[0:1, 512:1024], start=True, stop=True), deps=[kbe1])
    kbe2 = p.step("vector", lambda e, t=s["kmb"], ps=pANG: e.tensor_copy(t[:, 512:1024], ps[:, :]), deps=[kb2])

    p.region = "C2"
    x2c_lastuse = None
    stf = None
    stb = None
    pR1_free = ksg
    pR2_free = ksg
    pBC_free = kvg
    lw_lastuse = [None, None]
    zq_free = None
    zqT_free = None
    prpi_free = None
    qh_free = None
    lep_free = None
    r_sb = s["r"]
    for th in range(4):
        th0 = th * 1024
        half = [[], []]
        for tth in range(2):
            tt = th * 2 + tth
            buf = tt % 2
            wl_toks = load_window(tt, buf, True, [xw_lastuse[buf]])
            ldx2 = p.dma("sync", lambda e, t=s["x2c"], v=fm(d["x2_fm"])[:, :, tt * 512:(tt + 1) * 512]:
                         e.dma_start(out=t[:, :, :], in_=v), deps=[x2c_lastuse], chan="xmc")
            out_toks = []
            for tci in range(4):
                tc = tt * 4 + tci
                sin_t, cos_t, tabend = _rope_tables(p, d, s, tc, [tab_free, kbe2])
                kms = []
                for oh, ps in ((0, psK0), (1, psK1)):
                    mmt = None
                    for c in range(8):
                        mmt = p.step("tensor", lambda e, c=c, tci=tci, oh=oh, x=s["xw"][buf], w_=s["qkw"], ps=ps:
                                     e.matmul(ps[:, :], x[:, c, 1 + tci * 128:1 + (tci + 1) * 128],
                                              w_[:, c, oh * 512:(oh + 1) * 512],
                                              start=(c == 0), stop=False),
                                     deps=(wl_toks + [elu_free] if c == 0 else []))
                    mmt = p.step("tensor", lambda e, oh=oh, o=sh["ones_row"], b=s["qkb"], ps=ps:
                                 e.matmul(ps[:, :], o[0:1, :], b[0:1, oh * 512:(oh + 1) * 512],
                                          start=False, stop=True), deps=[])
                    kms.append(mmt)
                el = _elu1(p, s, psK0, psK1, s["kh"], [kms[0], qh_free], [kms[1]])
                elu_free = el[-1]
                z1 = p.step("vector", lambda e, t=s["t1"], q=s["kh"], k=s["kmb"]:
                            e.tensor_mul(t[:, :], q[:, :], k[:, :]), deps=el + [kbe2, kbe1])
                z2 = p.step("vector", lambda e, t=s["z8"], a=s["t1"]:
                            e.reduce_sum(t[:, 0:8], a[:, :].rearrange("p (h ee) -> p h ee", h=8), axis=AX.X),
                            deps=[z1])
                z3 = p.step("vector", lambda e, t=s["z8"]:
                            e.tensor_scalar(t[:, 8:16], t[:, 0:8], 1e-6, None, ALU.add), deps=[z2])
                z4 = p.step("vector", lambda e, t=s["z8"]:
                            e.reciprocal(t[:, 16:24], t[:, 8:16]), deps=[z3])
                q1 = p.step("vector", lambda e, t=s["t1"], c_=s["cos"], k=s["kh"]:
                            e.tensor_mul(t[:, 0:512], c_[:, :], k[:, 0:1024:2]), deps=[z1, cos_t, prpi_free])
                q2 = p.step("vector", lambda e, t=s["t2"], sn=s["sin"], k=s["kh"]:
                            e.tensor_mul(t[:, :], sn[:, :], k[:, 1:1024:2]), deps=[sin_t])
                q3 = p.step("vector", lambda e, o=s["prpi"], a=s["t1"], b=s["t2"]:
                            e.tensor_sub(o[:, 0:512], a[:, 0:512], b[:, :]), deps=[q1, q2])
                q4 = p.step("vector", lambda e, t=s["t1"], c_=s["cos"], k=s["kh"]:
                            e.tensor_mul(t[:, 0:512], c_[:, :], k[:, 1:1024:2]), deps=[q3])
                q5 = p.step("vector", lambda e, t=s["t2"], sn=s["sin"], k=s["kh"]:
                            e.tensor_mul(t[:, :], sn[:, :], k[:, 0:1024:2]), deps=[q3])
                q6 = p.step("vector", lambda e, o=s["prpi"], a=s["t1"], b=s["t2"]:
                            e.tensor_add(o[:, 512:1024], a[:, 0:512], b[:, :]), deps=[q4, q5])
                tab_free = q6
                qh_free = q6
                zqs = []
                for h in range(8):
                    zqs.append(p.step("vector", lambda e, t=s["zq"], pp=s["prpi"], z=s["z8"], h=h:
                                      e.tensor_scalar(t[:, h * 128:(h + 1) * 128], pp[:, h * 128:(h + 1) * 128],
                                                      z[:, 16 + h:17 + h], None, ALU.mult),
                                      deps=[q6 if h >= 4 else q3, z4, zq_free]))
                prpi_free = zqs[-1]
                tts = []
                for h in range(8):
                    tts.append(p.step("tensor", lambda e, h=h, t=s["zq"], ps=ptv, idn=s["identb"]:
                                      e.transpose(ps[:, h * 128:(h + 1) * 128], t[:, h * 128:(h + 1) * 128],
                                                  idn[:, :]),
                                      deps=([zqs[-1], zqT_free, vt_free] if h == 0 else [])))
                zte = p.step("vector", lambda e, t=s["zqT"], ps=ptv:
                             e.tensor_copy(t[:, :, :], ps[:, :].rearrange("p (h q) -> p h q", h=8)),
                             deps=[tts[-1]])
                zq_free = zte
                vt_free = tts[-1]
                om = None
                for h in range(8):
                    om = p.step("tensor", lambda e, h=h, kv=s["kvb"], zt=s["zqT"], ps=pKV:
                                e.matmul(ps[:, h * 128:(h + 1) * 128], kv[:, h, :], zt[:, h, :],
                                         start=True, stop=True),
                                deps=([zte, kvg, pBC_free] if h == 0 else []))
                zqT_free = om
                rt = p.step("vector", lambda e, t=r_sb, ttt=tth, tci=tci, x=s["x2c"], ps=pKV:
                            e.tensor_add(t[:, :, ttt * 512 + tci * 128:ttt * 512 + (tci + 1) * 128],
                                         ps[:, :].rearrange("p (h q) -> p h q", h=8),
                                         x[:, :, tci * 128:(tci + 1) * 128]),
                            deps=[om, ldx2] + ([stf, stb] if tci == 0 and tth == 0 else []))
                pBC_free = rt
                out_toks.append(rt)
            p.region = "C2L"
            for oc in range(8):
                lb = oc % 2
                ldl = p.dma("gpsimd", lambda e, t=s["lw"][lb], oc=oc,
                            v=d["lepe_w"][:, :, oc * 128:(oc + 1) * 128].rearrange("w (c p) o -> p w c o", p=128):
                            e.dma_start(out=t[:, :, :, :], in_=v),
                            deps=[lw_lastuse[lb]], chan=f"lw{lb}")
                lm = None
                for wk in range(3):
                    for c in range(8):
                        lm = p.step("tensor", lambda e, wk=wk, c=c, lw=s["lw"][lb], x=s["xw"][buf], ps=psK0:
                                    e.matmul(ps[:, :], lw[:, wk, c, :], x[:, c, wk:wk + 512],
                                             start=(wk == 0 and c == 0), stop=(wk == 2 and c == 7)),
                                    deps=(([ldl, elu_free] + wl_toks) if (wk == 0 and c == 0) else []))
                lw_lastuse[lb] = lm
                lev = p.step("vector", lambda e, t=s["lep"], ps=psK0, oc=oc, lb2=s["lepeb"]:
                             e.tensor_scalar(t[:, :], ps[:, :], lb2[:, oc:oc + 1], None, ALU.add),
                             deps=[lm, lep_free])
                elu_free = lev
                lra = p.step("vector", lambda e, t=r_sb, oc=oc, ttt=tth, le=s["lep"]:
                             e.tensor_add(t[:, oc, ttt * 512:(ttt + 1) * 512],
                                          t[:, oc, ttt * 512:(ttt + 1) * 512], le[:, :]),
                             deps=[lev] + out_toks)
                lep_free = lra
                out_toks.append(lra)
            p.region = "C2"
            x2c_lastuse = out_toks[-1]
            xw_lastuse[buf] = lm
            half[tth] = list(out_toks)
        stf, stb, pR1_free, pR2_free, pBC_free, norm_all = emit_LN(
            p, s, sh, consts, r_sb, d["x3_fm"], d["x3b_fm"], th0, half[0] + half[1],
            pR1_free, pR2_free, pBC_free, stb)
    p.wait_all("gpsimd", [stf])
    p.wait_all("sync", [stb])
    return [stf, stb]


def alloc_D(nc, stack):
    s = {}
    s["x3b"] = sbuf(nc, stack, "d_x3b", [128, 8, 1024], BF16)
    s["x3c"] = sbuf(nc, stack, "d_x3c", [128, 8, 512], F32)
    s["y"] = sbuf(nc, stack, "d_y", [128, 8, 1024], F32)
    s["H"] = sbuf(nc, stack, "d_H", [128, 32, 1024], BF16)
    s["wg1"] = sbuf(nc, stack, "d_wg1", [128, 8, 8], F32)
    s["bgr"] = sbuf(nc, stack, "d_bgr", [1, 8], F32)
    s["b1s"] = sbuf(nc, stack, "d_b1s", [128, 8, 32], F32)
    s["b2s"] = sbuf(nc, stack, "d_b2s", [8, 1024], F32)
    s["identf"] = sbuf(nc, stack, "d_identf", [128, 128], F32)
    s["w1g"] = [sbuf(nc, stack, f"d_w1g{i}", [128, 8, 512], BF16) for i in range(2)]
    s["w2g"] = [sbuf(nc, stack, f"d_w2g{i}", [128, 32, 128], BF16) for i in range(2)]
    s["lg"] = sbuf(nc, stack, "d_lg", [128, 32], F32)
    s["gcT"] = sbuf(nc, stack, "d_gcT", [8, 1024], F32)
    s["gcrow"] = sbuf(nc, stack, "d_gcrow", [1, 1024], F32)
    s["gcb"] = sbuf(nc, stack, "d_gcb", [128, 2, 512], F32)
    s["tmp"] = sbuf(nc, stack, "d_tmp", [128, 512], F32)
    s["ot"] = sbuf(nc, stack, "d_ot", [128, 1024], F32)
    s["sq"] = sbuf(nc, stack, "d_sq", [128, 512], F32)
    s["rowA"] = sbuf(nc, stack, "d_rowA", [1, 512], F32)
    s["rowB"] = sbuf(nc, stack, "d_rowB", [1, 512], F32)
    s["rowC"] = sbuf(nc, stack, "d_rowC", [1, 512], F32)
    s["bcm"] = sbuf(nc, stack, "d_bcm", [128, 512], F32)
    s["bcr"] = sbuf(nc, stack, "d_bcr", [128, 512], F32)
    s["pg"] = psum(nc, stack, "d_pg", [128, 512])
    s["pH1"] = psum(nc, stack, "d_pH1", [128, 512])
    s["pH3"] = psum(nc, stack, "d_pH3", [128, 512])
    s["pcw"] = psum(nc, stack, "d_pcw", [8, 128])
    s["pR1"] = psum(nc, stack, "d_pR1", [1, 512])
    s["pR2"] = psum(nc, stack, "d_pR2", [1, 512])
    s["pBC"] = psum(nc, stack, "d_pBC", [128, 1024])
    return s


def emit_D(p, d, s, sh, consts):
    wg_tok = p.dma("sync", lambda e, t=s["wg1"], v=d["Wg1"].rearrange("(c p) ee -> p c ee", p=128):
                   e.dma_start(out=t[:, :, :], in_=v))
    small = [
        p.dma("sync", lambda e, t=s["bgr"], v=d["bg1"]: e.dma_start(out=t[:, :], in_=v)),
        p.dma("sync", lambda e, t=s["b1s"], v=d["b1_1"].rearrange("ee pp hh -> pp ee hh"):
              e.dma_start(out=t[:, :, :], in_=v)),
        p.dma("sync", lambda e, t=s["b2s"], v=d["b2_1"]: e.dma_start(out=t[:, :], in_=v)),
        p.dma("sync", lambda e, t=s["identf"], v=d["identf"]: e.dma_start(out=t[:, :], in_=v)),
        wg_tok,
    ] + consts

    NTB = 4
    TB = 1024
    x3b, x3c, y, H = s["x3b"], s["x3c"], s["y"], s["H"]
    pg, pH1, pH3, pcw = s["pg"], s["pH1"], s["pH3"], s["pcw"]
    lg, gcT = s["lg"], s["gcT"]

    x3b_lastuse = None
    x3c_lastuse = None
    ystored = None
    w1_lastuse = [None, None]
    w2_lastuse = [None, None]
    pg_free = None
    pH1_free = None
    pH3_free = None
    pcw_free = None
    pR1_free = None
    pR2_free = None
    pBC_free = None
    gcT_lastuse = None
    gcrow_lastuse = None
    H_lastread = None
    tmp_free = None
    y_lastread = None
    ot_free = None
    m3t = None
    mo = None

    for tb in range(NTB):
        tb0 = tb * TB
        ldx = p.dma("sync", lambda e, t=x3b, v=fm(d["x3b_fm"])[:, :, tb0:tb0 + TB]:
                    e.dma_start(out=t[:, :, :], in_=v), deps=[x3b_lastuse], chan="xmb")
        gce_toks = []
        for half in range(2):
            ldr = p.dma("sync", lambda e, t=x3c, v=fm(d["x3_fm"])[:, :, tb0 + half * 512:tb0 + (half + 1) * 512]:
                        e.dma_start(out=t[:, :, :], in_=v), deps=[x3c_lastuse], chan="xmc")
            for tci in range(4):
                tc = half * 4 + tci
                gmm = None
                for c in range(8):
                    gmm = p.step("tensor", lambda e, c=c, tci=tci, x=x3c, w=s["wg1"], ps=pg:
                                 e.matmul(ps[:, 0:8], x[:, c, tci * 128:(tci + 1) * 128], w[:, c, :],
                                          start=(c == 0), stop=False),
                                 deps=(([ldr, pg_free] + small) if c == 0 else []))
                gmm = p.step("tensor", lambda e, o=sh["ones_row"], b=s["bgr"], ps=pg:
                             e.matmul(ps[:, 0:8], o[0:1, :], b[0:1, :], start=False, stop=True), deps=[])
                t1 = p.step("vector", lambda e, t=lg, ps=pg: e.tensor_copy(t[:, 0:8], ps[:, 0:8]), deps=[gmm])
                pg_free = t1
                t2 = p.step("vector", lambda e, t=lg: e.reduce_max(t[:, 8:9], t[:, 0:8], axis=AX.X), deps=[t1])
                t3 = p.step("vector", lambda e, t=lg: e.tensor_scalar(t[:, 9:10], t[:, 8:9], -1.0, None, ALU.mult), deps=[t2])
                t4 = p.step("scalar", lambda e, t=lg: e.activation(t[:, 16:24], t[:, 0:8], AF.Exp, bias=t[:, 9:10],
                                                                  accum_out=t[:, 10:11]), deps=[t3])
                t5 = p.step("vector", lambda e, t=lg: e.reciprocal(t[:, 11:12], t[:, 10:11]), deps=[t4])
                t6 = p.step("vector", lambda e, t=lg: e.tensor_scalar(t[:, 24:32], t[:, 0:8], t[:, 8:9], None, ALU.is_ge), deps=[t4])
                t7 = p.step("vector", lambda e, t=lg: e.tensor_scalar(t[:, 24:32], t[:, 24:32], t[:, 11:12], None, ALU.mult), deps=[t6, t5])
                t8 = p.step("tensor", lambda e, t=lg, ps=pcw, idn=s["identf"]:
                            e.transpose(ps[:, :], t[:, 24:32], idn[:, :]), deps=[t7, pcw_free])
                t9 = p.step("vector", lambda e, t=gcT, tc=tc, ps=pcw:
                            e.tensor_copy(t[0:8, tc * 128:(tc + 1) * 128], ps[0:8, :]),
                            deps=[t8, gcT_lastuse])
                pcw_free = t9
                gce_toks.append(t9)
            if half == 0:
                x3c_lastuse = gmm
        binit = []
        for dc in range(8):
            for tt in range(2):
                bmm = p.step("tensor", lambda e, dc=dc, tt=tt, b2=s["b2s"], g=gcT, ps=pg:
                             e.matmul(ps[:, :], b2[0:8, dc * 128:(dc + 1) * 128], g[0:8, tt * 512:(tt + 1) * 512],
                                      start=True, stop=True),
                             deps=(gce_toks + [pg_free] if (dc == 0 and tt == 0) else [pg_free]))
                yt = p.step("vector", lambda e, t=y, dc=dc, tt=tt, ps=pg:
                            e.tensor_copy(t[:, dc, tt * 512:(tt + 1) * 512], ps[:, :]),
                            deps=[bmm, y_lastread])
                pg_free = yt
                binit.append(yt)
        y_lastread = None
        for ex in range(8):
            gcr = p.dma("sync", lambda e, t=s["gcrow"], src=gcT, ex=ex:
                        e.dma_start(out=t[0:1, :], in_=src[ex:ex + 1, :]),
                        deps=gce_toks + [gcrow_lastuse], chan="cwrow")
            cbm = p.step("tensor", lambda e, ps=pg, o=sh["ones_row"], t=s["gcrow"]:
                         e.matmul(ps[:, :], o[0:1, :], t[0:1, 0:512], start=True, stop=True),
                         deps=[gcr, pg_free])
            cbe1 = p.step("vector", lambda e, t=s["gcb"], ps=pg: e.tensor_copy(t[:, 0, :], ps[:, :]), deps=[cbm])
            cbm2 = p.step("tensor", lambda e, ps=pg, o=sh["ones_row"], t=s["gcrow"]:
                          e.matmul(ps[:, :], o[0:1, :], t[0:1, 512:1024], start=True, stop=True),
                          deps=[cbe1])
            cbe2 = p.step("vector", lambda e, t=s["gcb"], ps=pg: e.tensor_copy(t[:, 1, :], ps[:, :]), deps=[cbm2])
            pg_free = cbe2
            gcrow_lastuse = cbm2
            h_toks = []
            for hg in range(8):
                gb = hg % 2
                ld1 = p.dma("gpsimd", lambda e, t=s["w1g"][gb], ex=ex, hg=hg,
                            v=d["W1_1"][ex].rearrange("(c p) hh -> p c hh", p=128)[:, :, hg * 512:(hg + 1) * 512]:
                            e.dma_start(out=t[:, :, :], in_=v),
                            deps=[w1_lastuse[gb]], chan=f"w1g{gb}")
                for hi in range(4):
                    hc = hg * 4 + hi
                    for tt in range(2):
                        ps = pH1 if tt == 0 else pH3
                        m3t = None
                        for c in range(8):
                            m3t = p.step("tensor", lambda e, c=c, hi=hi, tt=tt, w=s["w1g"][gb], x=x3b, ps=ps:
                                         e.matmul(ps[:, :], w[:, c, hi * 128:(hi + 1) * 128],
                                                  x[:, c, tt * 512:(tt + 1) * 512],
                                                  start=(c == 0), stop=(c == 7)),
                                         deps=([ld1, ldx, (pH1_free if tt == 0 else pH3_free)] if c == 0 else []))
                        silt = p.step("scalar", lambda e, t=H, hc=hc, tt=tt, ps=ps, b1=s["b1s"], ex=ex:
                                      e.activation(t[:, hc, tt * 512:(tt + 1) * 512], ps[:, :], AF.Silu,
                                                   bias=b1[:, ex, hc:hc + 1]),
                                      deps=[m3t, H_lastread])
                        if tt == 0:
                            pH1_free = silt
                        else:
                            pH3_free = silt
                        h_toks.append(silt)
                w1_lastuse[gb] = m3t
            for dc in range(8):
                wb = dc % 2
                ld2 = p.dma("gpsimd", lambda e, t=s["w2g"][wb], ex=ex, dc=dc,
                            v=d["W2_1"][ex][:, dc * 128:(dc + 1) * 128].rearrange("(a p) q -> p a q", p=128):
                            e.dma_start(out=t[:, :, :], in_=v),
                            deps=[w2_lastuse[wb]], chan=f"w2g{wb}")
                for tt in range(2):
                    ps = pH1 if tt == 0 else pH3
                    mo = None
                    for hc in range(32):
                        mo = p.step("tensor", lambda e, hc=hc, tt=tt, w=s["w2g"][wb], ps=ps:
                                    e.matmul(ps[:, :], w[:, hc, :], H[:, hc, tt * 512:(tt + 1) * 512],
                                             start=(hc == 0), stop=(hc == 31)),
                                    deps=(([ld2, (pH1_free if tt == 0 else pH3_free)] + h_toks) if hc == 0 else []))
                    tmpt = p.step("vector", lambda e, t=s["tmp"], cb=s["gcb"], tt=tt, ps=ps:
                                  e.tensor_tensor(t[:, :], ps[:, :], cb[:, tt, :], ALU.mult),
                                  deps=[mo, cbe1 if tt == 0 else cbe2, tmp_free])
                    yt = p.step("vector", lambda e, t=y, dc=dc, tt=tt, tp=s["tmp"]:
                                e.tensor_add(t[:, dc, tt * 512:(tt + 1) * 512],
                                             t[:, dc, tt * 512:(tt + 1) * 512], tp[:, :]),
                                deps=[tmpt, binit[dc * 2 + tt]])
                    tmp_free = yt
                    if tt == 0:
                        pH1_free = yt
                    else:
                        pH3_free = yt
                w2_lastuse[wb] = mo
            H_lastread = mo
        radd = []
        for half in range(2):
            ldr2 = p.dma("sync", lambda e, t=x3c, v=fm(d["x3_fm"])[:, :, tb0 + half * 512:tb0 + (half + 1) * 512]:
                         e.dma_start(out=t[:, :, :], in_=v), deps=[x3c_lastuse], chan="xmc")
            for oc in range(8):
                rt = p.step("vector", lambda e, t=y, oc=oc, half=half, x=x3c:
                            e.tensor_add(t[:, oc, half * 512:(half + 1) * 512],
                                         t[:, oc, half * 512:(half + 1) * 512], x[:, oc, :]),
                            deps=[ldr2, yt] + ([] if oc else [pH1_free, pH3_free]))
                radd.append(rt)
            x3c_lastuse = radd[-1]
        x3b_lastuse = m3t
        stf, stb, pR1_free, pR2_free, pBC_free, norm_all = emit_LN_nostore(
            p, s, sh, consts, y, radd, pR1_free, pR2_free, pBC_free)
        for tc in range(8):
            tts = []
            for c in range(8):
                tts.append(p.step("tensor", lambda e, c=c, tc=tc, t=y, ps=s["pBC"], idn=s["identf"]:
                                  e.transpose(ps[:, c * 128:(c + 1) * 128], t[:, c, tc * 128:(tc + 1) * 128],
                                              idn[:, :]),
                                  deps=([norm_all[-1], pBC_free, ot_free] if c == 0 else [])))
            ote = p.step("vector", lambda e, t=s["ot"], ps=s["pBC"]: e.tensor_copy(t[:, :], ps[:, :]),
                         deps=[tts[-1]])
            pBC_free = ote
            ystored = p.dma("sync", lambda e, t=s["ot"], tb0=tb0, tc=tc,
                            v=d["y_out"][tb0 + tc * 128:tb0 + (tc + 1) * 128, :]:
                            e.dma_start(out=v, in_=t[:, :]), deps=[ote], chan="stf")
            ot_free = ystored
        y_lastread = tts[-1]
    p.wait_all("gpsimd", [ystored])
    return [ystored]


def emit_LN_nostore(p, s, sh, consts, r, deps_ready, pR1_free, pR2_free, pBC_free):
    pR1, pR2, pBC = s["pR1"], s["pR2"], s["pBC"]
    norm_all = []
    for tt in range(2):
        mm1 = None
        for oc in range(8):
            mm1 = p.step("tensor", lambda e, oc=oc, t=r, tt=tt, lc=sh["ln_cols"], ps=pR1:
                         e.matmul(ps[:, :], lc[:, 0:1], t[:, oc, tt * 512:(tt + 1) * 512],
                                  start=(oc == 0), stop=(oc == 7)),
                         deps=(deps_ready + [pR1_free] + consts if oc == 0 else []))
        mm2 = None
        for oc in range(8):
            sq_tok = p.step("vector", lambda e, t=s["sq"], r=r, oc=oc, tt=tt:
                            e.tensor_mul(t[:, :], r[:, oc, tt * 512:(tt + 1) * 512],
                                         r[:, oc, tt * 512:(tt + 1) * 512]),
                            deps=deps_ready + [mm2])
            mm2 = p.step("tensor", lambda e, oc=oc, t=s["sq"], ps=pR2, lc=sh["ln_cols"]:
                         e.matmul(ps[:, :], lc[:, 1:2], t[:, :],
                                  start=(oc == 0), stop=(oc == 7)),
                         deps=([sq_tok, pR2_free] if oc == 0 else [sq_tok]))
        rows = p.step("vector", [
            lambda e, t=s["rowA"], ps=pR1: e.tensor_copy(t[0:1, :], ps[0:1, :]),
            lambda e, a=s["rowA"], b=s["rowB"]:
            e.scalar_tensor_tensor(b[0:1, :], a[0:1, :], -1.0, a[0:1, :], ALU.mult, ALU.mult),
            lambda e, b=s["rowB"], ps=pR2: e.tensor_add(b[0:1, :], b[0:1, :], ps[0:1, :]),
        ], deps=[mm1, mm2, pR1_free, pR2_free])
        rstd = p.step("scalar", [
            lambda e, b=s["rowB"], c_=s["rowC"]: e.activation(c_[0:1, :], b[0:1, :], AF.Ln, bias=LN_EPS),
            lambda e, c_=s["rowC"]: e.activation(c_[0:1, :], c_[0:1, :], AF.Exp, scale=-0.5),
        ], deps=[rows])
        bc = p.step("tensor", [
            lambda e, ps=pBC, o=sh["ones_row"], a=s["rowA"]:
            e.matmul(ps[:, 0:512], o[0:1, :], a[0:1, :], start=True, stop=True),
            lambda e, ps=pBC, o=sh["ones_row"], c_=s["rowC"]:
            e.matmul(ps[:, 512:1024], o[0:1, :], c_[0:1, :], start=True, stop=True),
        ], deps=[rstd, rows, pBC_free])
        pR1_free = bc
        pR2_free = bc
        bce = p.step("vector", [
            lambda e, t=s["bcm"], ps=pBC: e.tensor_copy(t[:, :], ps[:, 0:512]),
            lambda e, t=s["bcr"], ps=pBC: e.tensor_copy(t[:, :], ps[:, 512:1024]),
        ], deps=[bc])
        pBC_free = bce
        for oc in range(8):
            nt1 = p.step("vector", lambda e, t=r, oc=oc, tt=tt, m=s["bcm"]:
                         e.tensor_add(t[:, oc, tt * 512:(tt + 1) * 512],
                                      t[:, oc, tt * 512:(tt + 1) * 512], m[:, :]),
                         deps=[bce, mm2])
            nt2 = p.step("vector", lambda e, t=r, oc=oc, tt=tt, m=s["bcr"]:
                         e.tensor_mul(t[:, oc, tt * 512:(tt + 1) * 512],
                                      t[:, oc, tt * 512:(tt + 1) * 512], m[:, :]),
                         deps=[nt1])
            norm_all.append(nt2)
    return None, None, pR1_free, pR2_free, pBC_free, norm_all


from contextlib import ExitStack
from concourse.bass_utils import run_bass_kernel_spmd

RG = [[0, 1, 2, 3], [4, 5, 6, 7]]


def build_fused():
    nc = bass.Bass("TRN2", target_bir_lowering=False, debug=False)
    d = {}
    EIN = [
        ("x0_fm", [1024, S], F32), ("x0q_fm", [1024, TPC], F32),
        ("WqT", [1024, 1024], F32), ("WkT", [1024, 1024], F32),
        ("WvT", [1024, 1024], F32), ("WoT", [1024, 1024], F32),
        ("Wg0T", [1024, 8], F32), ("W1T0", [8, 1024, 2816], F32),
        ("W3T0", [8, 1024, 2816], F32), ("W2T0", [8, 2816, 1024], F32),
        ("qk_w", [1024, 2048], F32), ("qk_b", [1, 2048], F32),
        ("lepe_w", [3, 1024, 1024], F32), ("lepe_b", [128, 8], F32),
        ("theta", [1, 512], F32), ("pos_rows", [32, 128], F32), ("wlr", [8, 2], F32),
        ("Wg1", [1024, 8], F32), ("bg1", [1, 8], F32),
        ("W1_1", [8, 1024, 4096], F32), ("b1_1", [8, 128, 32], F32),
        ("W2_1", [8, 4096, 1024], F32), ("b2_1", [8, 1024], F32),
        ("identb", [128, 128], BF16), ("identf", [128, 128], F32),
        ("ones_row", [1, 128], F32), ("ones_colb", [128, 1], BF16),
        ("ln_cols", [128, 2], F32), ("actconsts", [128, 1], F32),
    ]
    INT = [
        ("KT_d", [1024, S], BF16), ("V_d", [S, 1024], BF16), ("QT_d", [1024, TPC], BF16),
        ("x1_fm", [1024, TPC], F32), ("x1b_fm", [1024, TPC], BF16),
        ("x2_fm", [1024, TPC], F32), ("x2b_fm", [1024, TPC], BF16),
        ("x3_fm", [1024, TPC], F32), ("x3b_fm", [1024, TPC], BF16),
        ("cc_kv_in", [128, 1024], F32), ("cc_kv_out", [128, 1024], F32),
        ("cc_ks_in", [1, 1024], F32), ("cc_ks_out", [1, 1024], F32),
        ("ag_in", [2, 1024], F32), ("ag_out", [8, 1024], F32),
        ("halo_l", [1, 1024], F32), ("halo_r", [1, 1024], F32),
    ]
    for name, shape, dt in EIN:
        d[name] = nc.dram_tensor(name, shape, dt, kind="ExternalInput").ap()
    for name, shape, dt in INT:
        d[name] = nc.dram_tensor(name, shape, dt, kind="Internal").ap()
    d["y_out"] = nc.dram_tensor("y_out", [TPC, 1024], F32, kind="ExternalOutput").ap()

    with ExitStack() as stack:
        p = Ctx(nc)
        sh = {}
        sh["ones_row"] = sbuf(nc, stack, "sh_ones_row", [1, 128], F32)
        sh["ln_cols"] = sbuf(nc, stack, "sh_ln_cols", [128, 2], F32)
        sh["actc"] = sbuf(nc, stack, "sh_actc", [128, 1], F32)
        sh["dummy"] = sbuf(nc, stack, "sh_dummy", [1, 4], F32)
        nc.const_aps.register(F32, LN_EPS, sh["actc"][:, 0:1])
        consts = [
            p.dma("sync", lambda e: e.dma_start(out=sh["ones_row"][:, :], in_=d["ones_row"])),
            p.dma("sync", lambda e: e.dma_start(out=sh["ln_cols"][:, :], in_=d["ln_cols"])),
            p.dma("sync", lambda e: e.dma_start(out=sh["actc"][:, :], in_=d["actconsts"])),
        ]
        with ExitStack() as stA:
            shA = alloc_A_shared(nc, stA)
            shA.update(sh)
            consA = consts + [p.dma("sync", lambda e: e.dma_start(out=shA["identb"][:, :], in_=d["identb"]))]
            with ExitStack() as st1:
                s1 = alloc_A1(nc, st1)
                a1_end = emit_A1(p, d, shA, s1)
            s2 = alloc_A2(nc, stA)
            emit_A2(p, d, shA, s2, a1_end, consA)
        p.barrier(sh["dummy"])
        with ExitStack() as stB:
            sB = alloc_B(nc, stB)
            emit_B(p, d, sB, sh, consts)
        p.barrier(sh["dummy"])
        with ExitStack() as stC:
            sC = alloc_C(nc, stC)
            emit_C(p, d, sC, sh, consts, RG)
        p.barrier(sh["dummy"])
        with ExitStack() as stD:
            sD = alloc_D(nc, stD)
            emit_D(p, d, sD, sh, consts)
        p.finalize()
    return nc


def kernel(**inputs):
    prep = host_prep(inputs)
    nc = build_fused()
    maps = []
    for core in range(8):
        b, qq = core // 4, core % 4
        wlr = np.zeros((8, 2), np.float32)
        if qq > 0:
            wlr[2 * (qq - 1) + 1, 0] = 1.0
        if qq < 3:
            wlr[2 * (qq + 1), 1] = 1.0
        maps.append({
            "x0_fm": prep["x0_fm"][b],
            "x0q_fm": np.ascontiguousarray(prep["x0_fm"][b][:, qq * TPC:(qq + 1) * TPC]),
            "WqT": prep["WqT"], "WkT": prep["WkT"], "WvT": prep["WvT"], "WoT": prep["WoT"],
            "Wg0T": prep["Wg0T"], "W1T0": prep["W1T0"], "W3T0": prep["W3T0"], "W2T0": prep["W2T0"],
            "qk_w": prep["qk_w"], "qk_b": prep["qk_b"],
            "lepe_w": prep["lepe_w"], "lepe_b": prep["lepe_b"],
            "theta": prep["theta"], "pos_rows": prep["pos_rows"][qq], "wlr": wlr,
            "Wg1": prep["Wg1"], "bg1": prep["bg1"],
            "W1_1": prep["W1_1"], "b1_1": prep["b1_1"], "W2_1": prep["W2_1"], "b2_1": prep["b2_1"],
            "identb": prep["identb"], "identf": prep["identf"],
            "ones_row": prep["ones_row"], "ones_colb": prep["ones_colb"],
            "ln_cols": prep["ln_cols"], "actconsts": prep["actconsts"],
        })
    res = run_bass_kernel_spmd(nc, maps, core_ids=list(range(8)))
    out = np.empty((2, S, 1024), np.float32)
    for core in range(8):
        b, qq = core // 4, core % 4
        out[b, qq * TPC:(qq + 1) * TPC, :] = res.results[core]["y_out"]
    return out
